# version 04
# baseline (speedup 1.0000x reference)
import numpy as np
import concourse.bass as bass
import concourse.mybir as mybir
from concourse.bass_utils import run_bass_kernel_spmd
import ml_dtypes

F32 = mybir.dt.float32
BF16 = mybir.dt.bfloat16
I32 = mybir.dt.int32
AF = mybir.ActivationFunctionType
ALU = mybir.AluOpType
AX = mybir.AxisListType
BF = ml_dtypes.bfloat16

T = 2048
C = 2048
H = 16
D = 128
HID = 5632
NT = T // 128
NCk = C // 128
NH = HID // 128
EPS = 1e-5
LN127 = float(np.log(127.0))
INV127 = 1.0 / 127.0
NELW = float(C * C)
GROUPS = [list(range(0, 15)), list(range(15, 30)), list(range(30, 44))]


class Prog:

    ENGINES = ("sync", "gpsimd", "tensor", "vector", "scalar")

    def __init__(self, nc):
        self.nc = nc
        self.ops = []
        self.last_write = {}
        self.readers = {}

    def op(self, eng, fn, reads=(), writes=(), dma=False, tag=""):
        deps = set()
        for k in reads:
            w = self.last_write.get(k)
            if w is not None:
                deps.add(w)
        for k in writes:
            w = self.last_write.get(k)
            if w is not None:
                deps.add(w)
            for r in self.readers.get(k, ()):
                deps.add(r)
        i = len(self.ops)
        if dma:
            assert writes, f"dma op {tag} needs a write resource for its semaphore"
        self.ops.append(dict(eng=eng, fn=fn, deps=sorted(deps), dma=dma, tag=tag,
                             w0=(writes[0] if writes else None)))
        for k in reads:
            self.readers.setdefault(k, []).append(i)
        for k in writes:
            self.last_write[k] = i
            self.readers[k] = []
        return i

    def transfer(self, old, new):
        if old in self.last_write:
            lw = self.last_write[old]
            if new not in self.last_write or self.last_write[new] < lw:
                self.last_write[new] = lw
        self.readers.setdefault(new, [])
        self.readers[new] += list(self.readers.get(old, ()))

    def seq(self, instr, handle):
        e = self._cur
        instr.then_inc(self._aux[e], 1)
        self._auxcnt[e] += 1
        handle.wait_ge(self._aux[e], self._auxcnt[e])

    def emit(self):
        nc = self.nc
        self._aux = {e: nc.semaphore(f"aux_{e}").__enter__() for e in self.ENGINES}
        self._auxcnt = {e: 0 for e in self.ENGINES}
        self._cur = None
        need_inc = set()
        for i, o in enumerate(self.ops):
            if o["dma"]:
                need_inc.add(i)
            for d in o["deps"]:
                need_inc.add(d)
        semid = {}
        res2sem = {}
        for i, o in enumerate(self.ops):
            if o["dma"]:
                r = o["w0"]
                if r not in res2sem:
                    res2sem[r] = f"dma_{len(res2sem)}_{r}"
                semid[i] = res2sem[r]
            else:
                semid[i] = f"eng_{o['eng']}"
        counts = {}
        running = {}
        for i, o in enumerate(self.ops):
            if i in need_inc:
                s = semid[i]
                running[s] = running.get(s, 0) + (16 if o["dma"] else 1)
                counts[i] = running[s]
        sems = {s: nc.semaphore(s).__enter__() for s in
                set(semid.values())}
        ops = self.ops

        def engine_body(ename, handle):
            waited = {}
            self._cur = ename
            for i, o in enumerate(ops):
                if o["eng"] != ename:
                    continue
                self._cur = ename
                for d in o["deps"]:
                    s = semid[d]
                    if d in counts and counts[d] > waited.get(s, 0):
                        handle.wait_ge(sems[s], counts[d])
                        waited[s] = counts[d]
                instr = o["fn"](handle)
                if i in need_inc:
                    assert instr is not None, f"op {i} {o['tag']} returned None but needs inc"
                    instr.then_inc(sems[semid[i]], 16 if o["dma"] else 1)

        with nc.Block() as block:
            @block.sync
            def _(e):
                engine_body("sync", e)

            @block.gpsimd
            def _(e):
                engine_body("gpsimd", e)

            @block.tensor
            def _(e):
                engine_body("tensor", e)

            @block.vector
            def _(e):
                engine_body("vector", e)

            @block.scalar
            def _(e):
                engine_body("scalar", e)
        return len(self.ops), len(need_inc)


def build(upto=99, bheads=16, bcut=9, bq=9):
    nc = bass.Bass("TRN2", target_bir_lowering=False, debug=False)

    x_d = nc.dram_tensor("x", [T, C], F32, kind="ExternalInput").ap()
    w_attn = {k: nc.dram_tensor(f"w{k}", [C, C], F32, kind="ExternalInput").ap()
              for k in ("q", "k", "v", "p")}
    wg_d = nc.dram_tensor("wg", [HID, C], I32, kind="ExternalInput").ap()
    wu_d = nc.dram_tensor("wu", [HID, C], I32, kind="ExternalInput").ap()
    wd_d = nc.dram_tensor("wd", [C, HID], I32, kind="ExternalInput").ap()
    sg_d = nc.dram_tensor("sg", [128, NH], F32, kind="ExternalInput").ap()
    su_d = nc.dram_tensor("su", [128, NH], F32, kind="ExternalInput").ap()
    sd_d = nc.dram_tensor("sd", [128, NCk], F32, kind="ExternalInput").ap()
    out_d = nc.dram_tensor("out", [T, C], F32, kind="ExternalOutput").ap()

    tern_d = {k: nc.dram_tensor(f"tern{k}", [C, C], BF16, kind="Internal").ap()
              for k in ("q", "k", "v", "p")}
    wgb_d = nc.dram_tensor("wgb", [HID, C], BF16, kind="Internal").ap()
    wub_d = nc.dram_tensor("wub", [HID, C], BF16, kind="Internal").ap()
    wdb_d = nc.dram_tensor("wdb", [C, HID], BF16, kind="Internal").ap()
    wdT_d = nc.dram_tensor("wdT", [HID, C], BF16, kind="Internal").ap()
    s2_d = nc.dram_tensor("s2", [T, C], F32, kind="Internal").ap()
    part_d = nc.dram_tensor("part", [C, T], F32, kind="Internal").ap()

    ar1 = nc.sbuf_tensor("ar1", [128, 16384], F32).__enter__()[:]
    ar2 = nc.sbuf_tensor("ar2", [128, 16384], F32).__enter__()[:]
    ar3 = nc.sbuf_tensor("ar3", [128, 14848], F32).__enter__()[:]
    sm = nc.sbuf_tensor("smalls", [128, 512], F32).__enter__()[:]
    ones_r = nc.sbuf_tensor("ones_r", [1, 128], F32).__enter__()[:]
    ones_c = nc.sbuf_tensor("ones_c", [128, 1], F32).__enter__()[:]
    rowscr = nc.sbuf_tensor("rowscr", [1, 16], F32).__enter__()[:]

    def carve(ar, cur, n):
        return ar[:, cur:cur + n], cur + n

    xq_fm = ar1.bitcast(BF16)
    WpT = ar1.bitcast(BF16)
    gu_ar = ar1.bitcast(BF16)

    cur = 0
    xs0, cur = carve(ar2, cur, 2048)
    xs1, cur = carve(ar2, cur, 2048)
    sqv, cur = carve(ar2, cur, 2048)
    xnv, cur = carve(ar2, cur, 2048)
    rxq = ar2[:, 8192:9216].bitcast(BF16)
    yqT = ar2.bitcast(BF16)
    xn2_fm = ar2.bitcast(BF16)
    hiT = ar2.bitcast(BF16)
    LOT = 16384

    cur = 0
    a_ternT, cur = carve(ar3, cur, 3072)
    a_qkvraw, cur = carve(ar3, cur, 1536)
    a_i32, cur = carve(ar3, cur, 1024)
    a_p127a, cur = carve(ar3, cur, 1024)
    a_p127b, cur = carve(ar3, cur, 1024)
    a_qk_tm, cur = carve(ar3, cur, 1024)
    a_v_tm, cur = carve(ar3, cur, 1024)
    a_q_fm, cur = carve(ar3, cur, 1024)
    a_k_fm, cur = carve(ar3, cur, 1024)
    a_r, cur = carve(ar3, cur, 1024)
    a_rT0, cur = carve(ar3, cur, 1024)
    a_rT1, cur = carve(ar3, cur, 1024)
    assert cur <= 14848, cur

    off = {}
    o = 0
    for nm, n in (("ternT", 3072), ("qkvraw", 1536), ("i32", 1024), ("p127a", 1024),
                  ("p127b", 1024), ("qk_tm", 1024), ("v_tm", 1024), ("q_fm", 1024),
                  ("k_fm", 1024), ("r", 1024), ("rT0", 1024), ("rT1", 1024)):
        off[nm] = o
        o += n
    ternT = a_ternT.bitcast(BF16)
    qkvraw = a_qkvraw.bitcast(BF16)
    y_q = a_qkvraw.bitcast(BF16)
    i32b = a_i32.bitcast(I32)
    i32c = a_qkvraw[:, 0:1024].bitcast(I32)
    p127 = [a_p127a, a_p127b]
    q_tm = a_qk_tm.bitcast(BF16)
    k_tm = a_qk_tm.bitcast(BF16)
    y_raw = a_qk_tm.bitcast(BF16)
    v_tm = a_v_tm.bitcast(BF16)
    q_fm = a_q_fm.bitcast(BF16)
    k_fm = a_k_fm.bitcast(BF16)
    r_bf = a_r.bitcast(BF16)
    rT = [a_rT0.bitcast(BF16), a_rT1.bitcast(BF16)]
    wrow = [a_p127a, a_p127b]
    w2i = [a_i32.bitcast(I32)[:, 0:1024], a_v_tm.bitcast(I32)[:, 0:1024]]
    w2o = [a_q_fm.bitcast(BF16)[:, 0:2048], a_k_fm.bitcast(BF16)[:, 0:2048]]
    xsC = [a_ternT[:, 0:2048], ar3[:, off["qk_tm"]:off["qk_tm"] + 2048]]
    wgT = [a_ternT.bitcast(BF16)[:, 0:2048], a_ternT.bitcast(BF16)[:, 2048:4096]]
    wuT = [a_qkvraw.bitcast(BF16)[:, 0:2048],
           a_ternT.bitcast(BF16)[:, 4096:6144]]
    wdT_col = ar3[:, off["p127a"]:off["p127a"] + 2048].bitcast(BF16)
    sqscr = ar3[:, off["p127a"]:off["p127a"] + 2048]
    sil = [ar3[:, off["qk_tm"]:off["qk_tm"] + 1024],
           ar3[:, off["v_tm"]:off["v_tm"] + 1024]]
    s2t = [ar3[:, off["q_fm"]:off["q_fm"] + 2048],
           ar3[:, off["r"]:off["r"] + 2048]]
    outasm = ar3[:, off["qk_tm"]:off["qk_tm"] + 2048]
    hilo = a_i32.bitcast(BF16)

    scur = 0

    def scarve(n):
        nonlocal scur
        ap = sm[:, scur:scur + n]
        scur += n
        return ap

    consts = scarve(6)
    c_eps, c_ln127, c_one, c_tiny, c_half, cm128 = (consts[:, i:i + 1] for i in range(6))
    Ceff = scarve(1)
    colM = scarve(3 * NT)
    colU = scarve(3 * NT)
    colV = scarve(3 * NT)
    colR = scarve(NT)
    colG = scarve(NT)
    colB = scarve(NT)
    colS = scarve(NT)
    colSh = scarve(NT)
    colSi = scarve(NT)
    colMy = scarve(NT)
    colUy = scarve(NT)
    colVy = scarve(NT)
    invq_xq = scarve(NT)
    aW = scarve(4)
    sWi = scarve(4)
    asum = scarve(NCk)
    stat = scarve(8)
    tmp = scarve(16)
    s_gate = scarve(NH)
    s_up = scarve(NH)
    s_down = scarve(NCk)
    pM = scarve(1)
    pU = scarve(1)
    pV = scarve(1)
    assert scur <= 512, scur

    psA = nc.psum_tensor("psA", [128, 2048], F32).__enter__()[:]
    psB = nc.psum_tensor("psB", [128, 2048], F32).__enter__()[:]

    P = Prog(nc)

    def f_init(v):
        P.seq(v.memset(c_eps, EPS), v)
        P.seq(v.memset(c_ln127, LN127), v)
        P.seq(v.memset(c_one, 1.0), v)
        P.seq(v.memset(c_tiny, float(2.0 ** -30)), v)
        P.seq(v.memset(c_half, 0.75), v)
        P.seq(v.memset(cm128, -128.0), v)
        P.seq(v.memset(ones_c, 1.0), v)
        P.seq(v.memset(ones_r, 1.0), v)
        P.seq(v.tensor_copy(out=i32b[:, 0:1], in_=c_half), v)
        return v.tensor_scalar(out=Ceff, in0=i32b[:, 0:1], scalar1=-0.5, scalar2=128.5,
                               op0=ALU.mult, op1=ALU.add)
    P.op("vector", f_init, writes=["consts", "Ceff", "i32b", "ones"], tag="init")
    P.op("gpsimd", lambda g: g.dma_start(out=s_gate, in_=sg_d), writes=["s_gate"], dma=True)
    P.op("gpsimd", lambda g: g.dma_start(out=s_up, in_=su_d), writes=["s_up"], dma=True)
    P.op("gpsimd", lambda g: g.dma_start(out=s_down, in_=sd_d), writes=["s_down"], dma=True)

    for wi, wkey in (enumerate(("q", "k", "v", "p")) if upto >= 1 else []):
        wd = w_attn[wkey]
        for ch in range(NCk):
            for hf in range(2):
                bi = (ch * 2 + hf) % 2
                buf, bkey = wrow[bi], f"wrow{bi}"
                src = wd[ch * 128:(ch + 1) * 128, hf * 1024:(hf + 1) * 1024]
                P.op("sync", (lambda b, s: lambda e: e.dma_start(out=b, in_=s))(buf, src),
                     writes=[bkey], dma=True, tag=f"w0l{wkey}{ch}{hf}")
                dst = asum[:, ch:ch + 1] if hf == 0 else tmp[:, 4:5]
                dkey = "asum" if hf == 0 else "tmp45"
                P.op("scalar", (lambda b, d: lambda e: e.activation(
                    out=sqv[:, 0:1024], in_=b, func=AF.Abs, accum_out=d))(buf, dst),
                     reads=[bkey], writes=["sqv", dkey], tag=f"w0a{wkey}{ch}{hf}")
                if hf == 1:
                    P.op("vector", (lambda ci: lambda v: v.tensor_tensor(
                        out=asum[:, ci:ci + 1], in0=asum[:, ci:ci + 1], in1=tmp[:, 4:5], op=ALU.add))(ch),
                         reads=["tmp45"], writes=["asum"], tag=f"w0+{wkey}{ch}")
        P.op("vector", lambda v: v.tensor_reduce(out=tmp[:, 5:6], in_=asum[:, 0:NCk],
                                                 axis=AX.X, op=ALU.add),
             reads=["asum"], writes=["tmp5"], tag=f"w0r{wkey}")
        P.op("tensor", lambda t: t.matmul(psA[0:1, 0:1], ones_c, tmp[:, 5:6], start=True, stop=True),
             reads=["tmp5", "ones"], writes=["psA"], tag=f"w0m{wkey}")
        def f_sml(v):
            P.seq(v.tensor_scalar(out=rowscr[0:1, 0:1], in0=psA[0:1, 0:1], scalar1=1.0 / NELW,
                                  scalar2=EPS, op0=ALU.mult, op1=ALU.max), v)
            return v.reciprocal(out=rowscr[0:1, 1:2], in_=rowscr[0:1, 0:1])
        P.op("vector", f_sml, reads=["psA"], writes=["rowscr"], tag=f"w0s{wkey}")
        P.op("tensor", lambda t: t.matmul(psB[:, 0:2], ones_r, rowscr[0:1, 0:2], start=True, stop=True),
             reads=["rowscr", "ones"], writes=["psB"], tag=f"w0b{wkey}")
        def f_cp(v, wi=wi):
            v.tensor_copy(out=aW[:, wi:wi + 1], in_=psB[:, 0:1])
            return v.tensor_copy(out=sWi[:, wi:wi + 1], in_=psB[:, 1:2])
        P.op("vector", f_cp, reads=["psB"], writes=["aW", "sWi"], tag=f"w0c{wkey}")

    ternbuf = rxq
    for wi, wkey in (enumerate(("q", "k", "v", "p")) if upto >= 2 else []):
        wd = w_attn[wkey]
        for ch in range(NCk):
            for hf in range(2):
                bi = hf
                buf, bkey = wrow[bi], f"wrow{bi}"
                src = wd[ch * 128:(ch + 1) * 128, hf * 1024:(hf + 1) * 1024]
                P.op("sync", (lambda b, s: lambda e: e.dma_start(out=b, in_=s))(buf, src),
                     writes=[bkey], dma=True, tag=f"w1l{wkey}{ch}{hf}")
                def f_tern(v, b=buf, wi=wi, hf=hf):
                    P.seq(v.tensor_scalar(out=i32b[:, 0:1024], in0=b, scalar1=sWi[:, wi:wi + 1],
                                          scalar2=Ceff, op0=ALU.mult, op1=ALU.add), v)
                    P.seq(v.tensor_scalar(out=i32b[:, 0:1024], in0=i32b[:, 0:1024], scalar1=127.0,
                                          scalar2=129.0, op0=ALU.max, op1=ALU.min), v)
                    return v.tensor_scalar(out=ternbuf[:, hf * 1024:(hf + 1) * 1024],
                                           in0=i32b[:, 0:1024], scalar1=-128.0, scalar2=None, op0=ALU.add)
                P.op("vector", f_tern, reads=[bkey, "sWi", "Ceff"], writes=["i32b", "ternbuf"],
                     tag=f"w1t{wkey}{ch}{hf}")
            P.op("gpsimd", (lambda wkey, ch: lambda g: g.dma_start(
                out=tern_d[wkey][ch * 128:(ch + 1) * 128, :], in_=ternbuf))(wkey, ch),
                 reads=["ternbuf"], writes=[f"tern_d{wkey}"], dma=True, tag=f"w1s{wkey}{ch}")

    P.transfer("i32b", "w2i0")
    for nm, src_d, dst_d in ((("g", wg_d, wgb_d), ("u", wu_d, wub_d)) if upto >= 3 else []):
        for ch in range(NH):
            bi = ch % 2
            ibuf, ikey = w2i[bi], ("w2i0" if bi == 0 else "w2i1")
            obuf, okey = w2o[bi], ("w2o0" if bi == 0 else "w2o1")
            for hf in range(2):
                P.op("sync", (lambda ib, s=src_d, ch=ch, hf=hf: lambda e: e.dma_start(
                    out=ib, in_=s[ch * 128:(ch + 1) * 128, hf * 1024:(hf + 1) * 1024]))(ibuf),
                     writes=[ikey], dma=True, tag=f"w2l{nm}{ch}{hf}")
                P.op("vector", (lambda ib, ob, hf=hf: lambda v: v.tensor_scalar(
                    out=ob[:, hf * 1024:(hf + 1) * 1024], in0=ib, scalar1=-128.0,
                    scalar2=None, op0=ALU.add))(ibuf, obuf),
                     reads=[ikey], writes=[okey], tag=f"w2c{nm}{ch}{hf}")
            P.op("gpsimd", (lambda ob, d=dst_d, ch=ch: lambda g: g.dma_start(
                out=d[ch * 128:(ch + 1) * 128, :], in_=ob))(obuf),
                 reads=[okey], writes=[f"wb{nm}"], dma=True, tag=f"w2s{nm}{ch}")
    P.transfer("w2i0", "i32b")
    dn_o = xq_fm[:, 0:HID]
    dn_t = xq_fm[:, 8192:8192 + NH * 128]
    for ch in (range(NCk) if upto >= 3 else []):
        for sub in range(6):
            w = 1024 if sub < 5 else 512
            P.op("sync", (lambda s=wd_d, ch=ch, sub=sub, w=w: lambda e: e.dma_start(
                out=i32b[:, 0:w], in_=s[ch * 128:(ch + 1) * 128, sub * 1024:sub * 1024 + w]))(),
                 writes=["i32b"], dma=True, tag=f"w2ld{ch}{sub}")
            P.op("vector", (lambda sub=sub, w=w: lambda v: v.tensor_scalar(
                out=dn_o[:, sub * 1024:sub * 1024 + w], in0=i32b[:, 0:w],
                scalar1=-128.0, scalar2=None, op0=ALU.add))(),
                 reads=["i32b"], writes=["dn_o"], tag=f"w2cd{ch}{sub}")
        P.op("gpsimd", (lambda ch=ch: lambda g: g.dma_start(
            out=wdb_d[ch * 128:(ch + 1) * 128, :], in_=dn_o))(),
             reads=["dn_o"], writes=["wdb"], dma=True, tag=f"w2sd{ch}")
        P.op("sync", (lambda ch=ch: lambda e: e.dma_start(
            out=dn_t.rearrange("p (m n) -> p m n", m=NH),
            in_=dn_o.rearrange("p (m n) -> p m n", m=NH), transpose=True))(),
             reads=["dn_o"], writes=["dn_t"], dma=True, tag=f"w2tp{ch}")
        P.op("gpsimd", (lambda ch=ch: lambda g: g.dma_start(
            out=wdT_d[:, ch * 128:(ch + 1) * 128].rearrange("(m p) n -> p m n", p=128),
            in_=dn_t.rearrange("p (m n) -> p m n", m=NH)))(),
             reads=["dn_t"], writes=["wdT"], dma=True, tag=f"w2st{ch}")

    P.transfer("dn_o", "xq_fm")
    P.transfer("dn_t", "xq_fm")
    P.transfer("ternbuf", "rxq")
    for ti in (range(NT) if upto >= 4 else []):
        xb, xkey = (xs0, "xs0") if ti % 2 == 0 else (xs1, "xs1")
        P.op("gpsimd", (lambda xb=xb, ti=ti: lambda g: g.dma_start(
            out=xb, in_=x_d[ti * 128:(ti + 1) * 128, :]))(),
             writes=[xkey], dma=True, tag=f"Ax{ti}")
        P.op("vector", (lambda xb=xb: lambda v: v.tensor_reduce(
            out=stat[:, 0:1], in_=xb, axis=AX.X, op=ALU.add))(),
             reads=[xkey], writes=["stat0"], tag=f"As{ti}")
        P.op("scalar", (lambda xb=xb: lambda s: s.activation(
            out=sqv, in_=xb, func=AF.Square, accum_out=stat[:, 1:2]))(),
             reads=[xkey], writes=["sqv", "stat1"], tag=f"Aq{ti}")
        def f_stats(v):
            P.seq(v.tensor_scalar(out=stat[:, 2:3], in0=stat[:, 0:1], scalar1=1.0 / C, scalar2=None, op0=ALU.mult), v)
            P.seq(v.tensor_tensor(out=stat[:, 3:4], in0=stat[:, 2:3], in1=stat[:, 2:3], op=ALU.mult), v)
            P.seq(v.tensor_scalar(out=stat[:, 4:5], in0=stat[:, 1:2], scalar1=1.0 / C, scalar2=None, op0=ALU.mult), v)
            return v.tensor_tensor(out=stat[:, 4:5], in0=stat[:, 4:5], in1=stat[:, 3:4], op=ALU.subtract)
        P.op("vector", f_stats, reads=["stat0", "stat1"], writes=["statm"], tag=f"Av{ti}")
        def f_rstd(s):
            P.seq(s.activation(out=stat[:, 5:6], in_=stat[:, 4:5], func=AF.Ln, bias=c_eps), s)
            return s.activation(out=stat[:, 6:7], in_=stat[:, 5:6], func=AF.Exp, scale=-0.5)
        P.op("scalar", f_rstd, reads=["statm", "consts"], writes=["rstd"], tag=f"Ar{ti}")
        def f_nmr(v):
            P.seq(v.tensor_scalar(out=stat[:, 7:8], in0=stat[:, 2:3], scalar1=-1.0, scalar2=None, op0=ALU.mult), v)
            return v.tensor_tensor(out=stat[:, 7:8], in0=stat[:, 7:8], in1=stat[:, 6:7], op=ALU.mult)
        P.op("vector", f_nmr, reads=["statm", "rstd"], writes=["nmr"], tag=f"An{ti}")
        P.op("scalar", (lambda xb=xb: lambda s: s.activation(
            out=xnv, in_=xb, func=AF.Identity, scale=stat[:, 6:7], bias=stat[:, 7:8]))(),
             reads=[xkey, "rstd", "nmr"], writes=["xnv"], tag=f"Aa{ti}")
        P.op("vector", lambda v: v.tensor_reduce(out=tmp[:, 6:7], in_=xnv, axis=AX.X,
                                                 op=ALU.max, apply_absolute_value=True),
             reads=["xnv"], writes=["qm"], tag=f"Am{ti}")
        def f_qs(v, ti=ti):
            P.seq(v.tensor_scalar(out=tmp[:, 7:8], in0=tmp[:, 6:7], scalar1=EPS, scalar2=None, op0=ALU.max), v)
            P.seq(v.tensor_scalar(out=invq_xq[:, ti:ti + 1], in0=tmp[:, 7:8], scalar1=INV127, scalar2=None, op0=ALU.mult), v)
            P.seq(v.reciprocal(out=tmp[:, 8:9], in_=tmp[:, 7:8]), v)
            return v.tensor_scalar(out=tmp[:, 9:10], in0=tmp[:, 8:9], scalar1=127.0, scalar2=None, op0=ALU.mult)
        P.op("vector", f_qs, reads=["qm"], writes=["invq_xq", "qs"], tag=f"Aw{ti}")
        for hf in range(2):
            P.op("vector", (lambda hf=hf: lambda v: v.tensor_scalar(
                out=i32b[:, 0:1024], in0=xnv[:, hf * 1024:(hf + 1) * 1024],
                scalar1=tmp[:, 9:10], scalar2=Ceff, op0=ALU.mult, op1=ALU.add))(),
                 reads=["xnv", "qs", "Ceff"], writes=["i32b"], tag=f"A1{ti}{hf}")
            P.op("vector", (lambda hf=hf: lambda v: v.tensor_scalar(
                out=rxq[:, hf * 1024:(hf + 1) * 1024], in0=i32b[:, 0:1024],
                scalar1=-128.0, scalar2=None, op0=ALU.add))(),
                 reads=["i32b"], writes=["rxq"], tag=f"A2{ti}{hf}")
        P.op("sync", (lambda ti=ti: lambda e: e.dma_start(
            out=xq_fm.rearrange("p (j t) -> p j t", j=NCk)[:, :, ti * 128:(ti + 1) * 128],
            in_=rxq.rearrange("p (j n) -> p j n", j=NCk), transpose=True))(),
             reads=["rxq"], writes=["xq_fm"], dma=True, tag=f"At{ti}")

    for nm in ("sqv", "xnv", "xs0", "xs1", "rxq"):
        P.transfer(nm, "yqT")
    P.transfer("wrow0", "p1270")
    P.transfer("wrow1", "p1271")
    P.transfer("w2i1", "v_tm")
    P.transfer("w2o0", "q_fm")
    P.transfer("w2o1", "k_fm")

    for h in (range(min(H, bheads)) if upto >= 5 else []):
        P.transfer("y_q", "qkvraw")
        P.transfer("i32c", "qkvraw")
        P.transfer("y_raw", "qk_tm")
        P.transfer("qk_tm", "y_raw")
        for wi, wkey in enumerate(("q", "k", "v")):
            P.op("sync", (lambda wkey=wkey, wi=wi, h=h: lambda e: e.dma_start(
                out=ternT.rearrange("p (j w) -> p j w", j=NCk)[:, :, wi * 128:(wi + 1) * 128],
                in_=tern_d[wkey][h * 128:(h + 1) * 128, :].rearrange("p (j n) -> p j n", j=NCk),
                transpose=True))(),
                 reads=[f"tern_d{wkey}"], writes=["ternT"], dma=True, tag=f"B{h}T{wkey}")
        colM3 = colM.rearrange("p (w t) -> p w t", w=3)
        for half in range(2):
            for tq in (range(8) if bq >= 1 else []):
                ti = half * 8 + tq
                pst, pkey = (psA, "psA") if ti % 2 == 0 else (psB, "psB")
                def f_mm(t, pst=pst, ti=ti):
                    for j in range(NCk):
                        ins = t.matmul(pst[:, 0:384],
                                       xq_fm[:, j * T + ti * 128:j * T + (ti + 1) * 128],
                                       ternT[:, j * 384:(j + 1) * 384],
                                       start=(j == 0), stop=(j == NCk - 1))
                    return ins
                P.op("tensor", f_mm, reads=["xq_fm", "ternT"], writes=[pkey], tag=f"B{h}q{ti}")
                if bq < 2:
                    continue
                P.op("scalar", (lambda pst=pst, tq=tq: lambda s: s.copy(
                    out=qkvraw[:, tq * 384:(tq + 1) * 384], in_=pst[:, 0:384]))(),
                     reads=[pkey], writes=["qkvraw"], tag=f"B{h}e{ti}")
                if bq < 3:
                    continue
                def f_red3(v, tq=tq, ti=ti):
                    for wi in range(3):
                        ins = v.tensor_reduce(
                            out=colM[:, wi * NT + ti:wi * NT + ti + 1],
                            in_=qkvraw[:, tq * 384 + wi * 128:tq * 384 + (wi + 1) * 128],
                            axis=AX.X, op=ALU.max, apply_absolute_value=True)
                        if wi < 2:
                            P.seq(ins, v)
                    return ins
                P.op("vector", f_red3, reads=["qkvraw"], writes=["colM"], tag=f"B{h}m{ti}")
            if bcut < 2:
                continue
            def f_usml(v, half=half):
                iq = invq_xq[:, half * 8:half * 8 + 8]
                for wi in range(3):
                    sl = slice(wi * NT + half * 8, wi * NT + half * 8 + 8)
                    P.seq(v.tensor_tensor(out=colU[:, sl], in0=colM[:, sl], in1=iq, op=ALU.mult), v)
                    P.seq(v.tensor_scalar(out=colU[:, sl], in0=colU[:, sl], scalar1=aW[:, wi:wi + 1],
                                          scalar2=EPS, op0=ALU.mult, op1=ALU.max), v)
                    P.seq(v.tensor_scalar(out=colV[:, sl], in0=colU[:, sl], scalar1=INV127, scalar2=None, op0=ALU.mult), v)
                    P.seq(v.reciprocal(out=colU[:, sl], in_=colU[:, sl]), v)
                    P.seq(v.tensor_tensor(out=colU[:, sl], in0=colU[:, sl], in1=iq, op=ALU.mult), v)
                    P.seq(v.tensor_scalar(out=colU[:, sl], in0=colU[:, sl], scalar1=aW[:, wi:wi + 1],
                                          scalar2=None, op0=ALU.mult), v)
                    P.seq(v.tensor_scalar(out=colU[:, sl], in0=colU[:, sl], scalar1=127.0,
                                          scalar2=None, op0=ALU.mult), v)
                return v.tensor_scalar(out=colG[:, half * 8:half * 8 + 8],
                                       in0=colV[:, half * 8:half * 8 + 8],
                                       scalar1=float(1.0 / np.sqrt(128.0)), scalar2=None, op0=ALU.mult)
            P.op("vector", f_usml, reads=["colM", "invq_xq", "aW"],
                 writes=["colU", "colV", "colG"], tag=f"B{h}u{half}")
            for wi, (dst, dkey) in enumerate(((q_tm, "qk_tm"), (k_tm, "qk_tm"), (v_tm, "v_tm"))):
                for tq in range(8):
                    ti = half * 8 + tq
                    P.op("vector", (lambda wi=wi, ti=ti, tq=tq: lambda v: v.tensor_scalar(
                        out=i32b[:, 0:128], in0=qkvraw[:, tq * 384 + wi * 128:tq * 384 + (wi + 1) * 128],
                        scalar1=colU[:, wi * NT + ti:wi * NT + ti + 1], scalar2=Ceff,
                        op0=ALU.mult, op1=ALU.add))(),
                         reads=["qkvraw", "colU", "Ceff"], writes=["i32b"], tag=f"B{h}c{wi}{ti}")
                    if wi == 0:
                        fn = (lambda dst=dst, ti=ti: lambda v: v.tensor_scalar(
                            out=dst[:, ti * 128:(ti + 1) * 128], in0=i32b[:, 0:128],
                            scalar1=-128.0, scalar2=None, op0=ALU.add))()
                    else:
                        fn = (lambda dst=dst, ti=ti, wi=wi: lambda v: v.tensor_scalar(
                            out=dst[:, ti * 128:(ti + 1) * 128], in0=i32b[:, 0:128],
                            scalar1=-128.0, scalar2=colV[:, wi * NT + ti:wi * NT + ti + 1],
                            op0=ALU.add, op1=ALU.mult))()
                    P.op("vector", fn, reads=["i32b", "colV"], writes=[dkey], tag=f"B{h}d{wi}{ti}")
                if wi == 0:
                    P.op("sync", (lambda half=half: lambda e: e.dma_start(
                        out=q_fm.rearrange("p (j n) -> p j n", j=NT)[:, half * 8:(half + 1) * 8, :],
                        in_=q_tm[:, half * 1024:(half + 1) * 1024].rearrange("p (j n) -> p j n", j=8),
                        transpose=True))(),
                         reads=["qk_tm"], writes=["q_fm"], dma=True, tag=f"B{h}tq{half}")
                elif wi == 1:
                    P.op("sync", (lambda half=half: lambda e: e.dma_start(
                        out=k_fm.rearrange("p (j n) -> p j n", j=NT)[:, half * 8:(half + 1) * 8, :],
                        in_=k_tm[:, half * 1024:(half + 1) * 1024].rearrange("p (j n) -> p j n", j=8),
                        transpose=True))(),
                         reads=["qk_tm"], writes=["k_fm"], dma=True, tag=f"B{h}tk{half}")
        def emit_av(ti):
            pst, pkey = (psA, "psA") if ti % 2 == 0 else (psB, "psB")
            rb, rkey = rT[ti % 2], f"rT{ti % 2}"
            def f_av(t, pst=pst, rb=rb):
                for j in range(NT):
                    ins = t.matmul(pst[:, 0:128], rb[:, j * 128:(j + 1) * 128],
                                   v_tm[:, j * 128:(j + 1) * 128],
                                   start=(j == 0), stop=(j == NT - 1))
                return ins
            P.op("tensor", f_av, reads=[rkey, "v_tm"], writes=[pkey], tag=f"B{h}A{ti}")
            P.op("scalar", (lambda pst=pst, ti=ti: lambda s: s.activation(
                out=y_raw[:, ti * 128:(ti + 1) * 128], in_=pst[:, 0:128],
                func=AF.Identity, scale=colSi[:, ti:ti + 1]))(),
                 reads=[pkey, "colSi"], writes=["y_raw"], tag=f"B{h}Y{ti}")
            P.op("vector", (lambda ti=ti: lambda v: v.tensor_reduce(
                out=colMy[:, ti:ti + 1], in_=y_raw[:, ti * 128:(ti + 1) * 128],
                axis=AX.X, op=ALU.max, apply_absolute_value=True))(),
                 reads=["y_raw"], writes=["colMy"], tag=f"B{h}M{ti}")

        for ti in (range(NT) if bcut >= 3 else []):
            pst, pkey = (psA, "psA") if ti % 2 == 0 else (psB, "psB")
            def f_smm(t, pst=pst, ti=ti):
                for half in range(4):
                    ins = t.matmul(pst[:, half * 512:(half + 1) * 512],
                                   q_fm[:, ti * 128:(ti + 1) * 128],
                                   k_fm[:, half * 512:(half + 1) * 512],
                                   start=True, stop=True)
                return ins
            P.op("tensor", f_smm, reads=["q_fm", "k_fm"], writes=[pkey], tag=f"B{h}s{ti}")
            P.op("vector", (lambda pst=pst, ti=ti: lambda v: v.tensor_reduce(
                out=colR[:, ti:ti + 1], in_=pst, axis=AX.X, op=ALU.max))(),
                 reads=[pkey], writes=["colR"], tag=f"B{h}x{ti}")
            def f_bias(v, ti=ti):
                P.seq(v.tensor_tensor(out=colB[:, ti:ti + 1], in0=colR[:, ti:ti + 1],
                                      in1=colG[:, ti:ti + 1], op=ALU.mult), v)
                return v.tensor_scalar(out=colB[:, ti:ti + 1], in0=colB[:, ti:ti + 1],
                                       scalar1=-1.0, scalar2=LN127, op0=ALU.mult, op1=ALU.add)
            P.op("vector", f_bias, reads=["colR", "colG"], writes=["colB"], tag=f"B{h}b{ti}")
            for hf in range(2):
                P.op("scalar", (lambda pst=pst, ti=ti, hf=hf: lambda s: s.activation(
                    out=p127[hf], in_=pst[:, hf * 1024:(hf + 1) * 1024], func=AF.Exp,
                    scale=colG[:, ti:ti + 1], bias=colB[:, ti:ti + 1],
                    accum_out=(colSh if hf == 0 else colS)[:, ti:ti + 1]))(),
                     reads=[pkey, "colG", "colB"],
                     writes=[f"p127{hf}", "colSh" if hf == 0 else "colSx"], tag=f"B{h}E{ti}{hf}")
                if hf == 0:
                    P.op("scalar", (lambda hf=hf: lambda s: s.activation(
                        out=i32b[:, 0:1024], in_=p127[hf], func=AF.Identity, bias=Ceff))(),
                         reads=[f"p127{hf}", "Ceff"], writes=["i32b"], tag=f"B{h}1{ti}{hf}")
                    P.op("vector", (lambda hf=hf: lambda v: v.tensor_scalar(
                        out=r_bf[:, hf * 1024:(hf + 1) * 1024], in0=i32b[:, 0:1024],
                        scalar1=-128.0, scalar2=None, op0=ALU.add))(),
                         reads=["i32b"], writes=["r_bf"], tag=f"B{h}2{ti}{hf}")
                else:
                    P.op("vector", (lambda hf=hf: lambda v: v.tensor_scalar(
                        out=i32c[:, 0:1024], in0=p127[hf], scalar1=Ceff, scalar2=None, op0=ALU.add))(),
                         reads=[f"p127{hf}", "Ceff"], writes=["i32c"], tag=f"B{h}1{ti}{hf}")
                    P.op("scalar", (lambda hf=hf: lambda s: s.activation(
                        out=r_bf[:, hf * 1024:(hf + 1) * 1024],
                        in_=i32c[:, 0:1024], func=AF.Identity, bias=cm128))(),
                         reads=["i32c", "consts"], writes=["r_bf"], tag=f"B{h}2{ti}{hf}")
            def f_sums(v, ti=ti):
                P.seq(v.tensor_tensor(out=colS[:, ti:ti + 1], in0=colS[:, ti:ti + 1],
                                      in1=colSh[:, ti:ti + 1], op=ALU.add), v)
                return v.reciprocal(out=colSi[:, ti:ti + 1], in_=colS[:, ti:ti + 1])
            P.op("vector", f_sums, reads=["colSx", "colSh"], writes=["colS", "colSi"], tag=f"B{h}S{ti}")
            rb, rkey = rT[ti % 2], f"rT{ti % 2}"
            P.op("sync", (lambda rb=rb: lambda e: e.dma_start(
                out=rb.rearrange("p (j n) -> p j n", j=NT),
                in_=r_bf.rearrange("p (j n) -> p j n", j=NT), transpose=True))(),
                 reads=["r_bf"], writes=[rkey], dma=True, tag=f"B{h}R{ti}")
            if ti > 0 and bcut >= 3:
                emit_av(ti - 1)
        if bcut >= 3:
            emit_av(NT - 1)
        if bcut < 4:
            continue
        def f_ysml(v):
            P.seq(v.tensor_scalar(out=colMy, in0=colMy, scalar1=EPS, scalar2=None, op0=ALU.max), v)
            P.seq(v.tensor_scalar(out=colVy, in0=colMy, scalar1=INV127, scalar2=None, op0=ALU.mult), v)
            P.seq(v.reciprocal(out=colUy, in_=colMy), v)
            return v.tensor_scalar(out=colUy, in0=colUy, scalar1=127.0, scalar2=None, op0=ALU.mult)
        P.op("vector", f_ysml, reads=["colMy"], writes=["colUy", "colVy"], tag=f"B{h}Z")
        P.transfer("i32c", "y_q")
        for ti in range(NT):
            P.op("vector", (lambda ti=ti: lambda v: v.tensor_scalar(
                out=i32b[:, 0:128], in0=y_raw[:, ti * 128:(ti + 1) * 128],
                scalar1=colUy[:, ti:ti + 1], scalar2=Ceff, op0=ALU.mult, op1=ALU.add))(),
                 reads=["y_raw", "colUy", "Ceff"], writes=["i32b"], tag=f"B{h}g{ti}")
            P.op("vector", (lambda ti=ti: lambda v: v.tensor_scalar(
                out=y_q[:, ti * 128:(ti + 1) * 128], in0=i32b[:, 0:128],
                scalar1=-128.0, scalar2=colVy[:, ti:ti + 1], op0=ALU.add, op1=ALU.mult))(),
                 reads=["i32b", "colVy"], writes=["y_q"], tag=f"B{h}h{ti}")
        P.op("sync", (lambda h=h: lambda e: e.dma_start(
            out=yqT.rearrange("p (a n) -> p a n", a=H * NT)[:, h * NT:(h + 1) * NT, :],
            in_=y_q[:, 0:T].rearrange("p (j n) -> p j n", j=NT), transpose=True))(),
             reads=["y_q"], writes=["yqT"], dma=True, tag=f"B{h}yT")

    P.transfer("xq_fm", "WpT")
    P.transfer("ternT", "xsC0")
    P.transfer("y_raw", "xsC1")
    P.transfer("qk_tm", "xsC1")
    for j in (range(NCk) if upto >= 6 else []):
        P.op("sync", (lambda j=j: lambda e: e.dma_start(
            out=WpT[:, j * C:(j + 1) * C],
            in_=tern_d["p"][:, j * 128:(j + 1) * 128], transpose=True))(),
             reads=["tern_dp"], writes=["WpT"], dma=True, tag=f"Cw{j}")
    for ti in (range(NT) if upto >= 6 else []):
        pst, pkey = (psA, "psA") if ti % 2 == 0 else (psB, "psB")
        xb, xkey = (xsC[0], "xsC0") if ti % 2 == 0 else (xsC[1], "xsC1")
        P.op("gpsimd", (lambda xb=xb, ti=ti: lambda g: g.dma_start(
            out=xb, in_=x_d[ti * 128:(ti + 1) * 128, :]))(),
             writes=[xkey], dma=True, tag=f"Cx{ti}")
        def f_pmm(t, pst=pst, ti=ti):
            for j in range(NCk):
                for half in range(4):
                    ins = t.matmul(pst[:, half * 512:(half + 1) * 512],
                                   yqT[:, (j * NT + ti) * 128:(j * NT + ti + 1) * 128],
                                   WpT[:, j * C + half * 512:j * C + (half + 1) * 512],
                                   start=(j == 0), stop=(j == NCk - 1))
            return ins
        P.op("tensor", f_pmm, reads=["yqT", "WpT"], writes=[pkey], tag=f"Cp{ti}")
        P.op("vector", (lambda pst=pst: lambda v: v.tensor_reduce(
            out=pM, in_=pst, axis=AX.X, op=ALU.max, apply_absolute_value=True))(),
             reads=[pkey], writes=["pM"], tag=f"Cm{ti}")
        def f_psml(v):
            P.seq(v.tensor_scalar(out=pM, in0=pM, scalar1=aW[:, 3:4], scalar2=EPS, op0=ALU.mult, op1=ALU.max), v)
            P.seq(v.tensor_scalar(out=pV, in0=pM, scalar1=INV127, scalar2=None, op0=ALU.mult), v)
            P.seq(v.reciprocal(out=pU, in_=pM), v)
            P.seq(v.tensor_scalar(out=pU, in0=pU, scalar1=aW[:, 3:4], scalar2=None, op0=ALU.mult), v)
            return v.tensor_scalar(out=pU, in0=pU, scalar1=127.0, scalar2=None, op0=ALU.mult)
        P.op("vector", f_psml, reads=["pM", "aW"], writes=["pU", "pV"], tag=f"Cs{ti}")
        for hf in range(2):
            P.op("vector", (lambda pst=pst, hf=hf: lambda v: v.tensor_scalar(
                out=i32b[:, 0:1024], in0=pst[:, hf * 1024:(hf + 1) * 1024],
                scalar1=pU, scalar2=Ceff, op0=ALU.mult, op1=ALU.add))(),
                 reads=[pkey, "pU", "Ceff"], writes=["i32b"], tag=f"C1{ti}{hf}")
            P.op("vector", (lambda hf=hf: lambda v: v.tensor_scalar(
                out=p127[hf], in0=i32b[:, 0:1024], scalar1=-128.0, scalar2=pV,
                op0=ALU.add, op1=ALU.mult))(),
                 reads=["i32b", "pV"], writes=[f"p127{hf}"], tag=f"C2{ti}{hf}")
            P.op("vector", (lambda xb=xb, hf=hf: lambda v: v.tensor_tensor(
                out=xb[:, hf * 1024:(hf + 1) * 1024], in0=xb[:, hf * 1024:(hf + 1) * 1024],
                in1=p127[hf], op=ALU.add))(),
                 reads=[f"p127{hf}", xkey], writes=[xkey], tag=f"C3{ti}{hf}")
        P.op("gpsimd", (lambda xb=xb, ti=ti: lambda g: g.dma_start(
            out=s2_d[ti * 128:(ti + 1) * 128, :], in_=xb))(),
             reads=[xkey], writes=["s2_d"], dma=True, tag=f"Cq{ti}")

    P.transfer("yqT", "xn2_fm")
    P.transfer("WpT", "gu")
    P.transfer("xsC0", "wgT0")
    P.transfer("xsC0", "wgT1")
    P.transfer("xsC0", "wuT1")
    P.transfer("qkvraw", "wuT0")
    P.transfer("y_q", "wuT0")
    P.transfer("q_fm", "s2t0")
    P.transfer("k_fm", "s2t0")
    P.transfer("r_bf", "s2t1")
    P.transfer("rT0", "s2t1")
    P.transfer("p1270", "wdTcol")
    P.transfer("p1271", "wdTcol")
    P.transfer("wrow0", "wdTcol")
    P.transfer("wrow1", "wdTcol")
    P.transfer("i32b", "hilo")
    P.transfer("xsC1", "sil0")
    P.transfer("v_tm", "sil1")
    P.transfer("xsC1", "outasm")
    P.transfer("v_tm", "outasm")
    xn2b = hilo
    for ti in (range(NT) if upto >= 7 else []):
        sb_, skey = (s2t[0], "s2t0") if ti % 2 == 0 else (s2t[1], "s2t1")
        P.op("gpsimd", (lambda sb_=sb_, ti=ti: lambda g: g.dma_start(
            out=sb_, in_=s2_d[ti * 128:(ti + 1) * 128, :]))(),
             reads=["s2_d"], writes=[skey], dma=True, tag=f"Dl{ti}")
        P.op("vector", (lambda sb_=sb_: lambda v: v.tensor_reduce(
            out=stat[:, 0:1], in_=sb_, axis=AX.X, op=ALU.add))(),
             reads=[skey], writes=["stat0"], tag=f"Ds{ti}")
        P.op("scalar", (lambda sb_=sb_: lambda s: s.activation(
            out=sqscr, in_=sb_, func=AF.Square, accum_out=stat[:, 1:2]))(),
             reads=[skey], writes=["wdTcol", "stat1"], tag=f"Dq{ti}")
        def f_stats(v):
            P.seq(v.tensor_scalar(out=stat[:, 2:3], in0=stat[:, 0:1], scalar1=1.0 / C, scalar2=None, op0=ALU.mult), v)
            P.seq(v.tensor_tensor(out=stat[:, 3:4], in0=stat[:, 2:3], in1=stat[:, 2:3], op=ALU.mult), v)
            P.seq(v.tensor_scalar(out=stat[:, 4:5], in0=stat[:, 1:2], scalar1=1.0 / C, scalar2=None, op0=ALU.mult), v)
            return v.tensor_tensor(out=stat[:, 4:5], in0=stat[:, 4:5], in1=stat[:, 3:4], op=ALU.subtract)
        P.op("vector", f_stats, reads=["stat0", "stat1"], writes=["statm"], tag=f"Dv{ti}")
        def f_rstd(s):
            P.seq(s.activation(out=stat[:, 5:6], in_=stat[:, 4:5], func=AF.Ln, bias=c_eps), s)
            return s.activation(out=stat[:, 6:7], in_=stat[:, 5:6], func=AF.Exp, scale=-0.5)
        P.op("scalar", f_rstd, reads=["statm", "consts"], writes=["rstd"], tag=f"Dr{ti}")
        def f_nmr(v):
            P.seq(v.tensor_scalar(out=stat[:, 7:8], in0=stat[:, 2:3], scalar1=-1.0, scalar2=None, op0=ALU.mult), v)
            return v.tensor_tensor(out=stat[:, 7:8], in0=stat[:, 7:8], in1=stat[:, 6:7], op=ALU.mult)
        P.op("vector", f_nmr, reads=["statm", "rstd"], writes=["nmr"], tag=f"Dn{ti}")
        P.op("scalar", (lambda sb_=sb_: lambda s: s.activation(
            out=xn2b, in_=sb_, func=AF.Identity, scale=stat[:, 6:7], bias=stat[:, 7:8]))(),
             reads=[skey, "rstd", "nmr"], writes=["hilo"], tag=f"Da{ti}")
        P.op("sync", (lambda ti=ti: lambda e: e.dma_start(
            out=xn2_fm.rearrange("p (j t) -> p j t", j=NCk)[:, :, ti * 128:(ti + 1) * 128],
            in_=xn2b.rearrange("p (j n) -> p j n", j=NCk), transpose=True))(),
             reads=["hilo"], writes=["xn2_fm"], dma=True, tag=f"Dt{ti}")

    for gi, jlist in (enumerate(GROUPS) if upto >= 8 else []):
        for jj, j in enumerate(jlist):
            gslot = jj % 2
            gview, gk = wgT[gslot], f"wgT{gslot}"
            uview, uk = wuT[gslot], f"wuT{gslot}"
            P.op("sync", (lambda j=j, gview=gview: lambda e: e.dma_start(
                out=gview.rearrange("p (a n) -> p a n", a=NCk),
                in_=wgb_d[j * 128:(j + 1) * 128, :].rearrange("p (a n) -> p a n", a=NCk),
                transpose=True))(),
                 reads=["wbg"], writes=[gk], dma=True, tag=f"DG{j}")
            P.op("sync", (lambda j=j, uview=uview: lambda e: e.dma_start(
                out=uview.rearrange("p (a n) -> p a n", a=NCk),
                in_=wub_d[j * 128:(j + 1) * 128, :].rearrange("p (a n) -> p a n", a=NCk),
                transpose=True))(),
                 reads=["wbu"], writes=[uk], dma=True, tag=f"DU{j}")
            def f_gmm(t, gview=gview):
                for a in range(NCk):
                    for qq in range(4):
                        ins = t.matmul(psA[:, qq * 512:(qq + 1) * 512],
                                       gview[:, a * 128:(a + 1) * 128],
                                       xn2_fm[:, a * T + qq * 512:a * T + (qq + 1) * 512],
                                       start=(a == 0), stop=(a == NCk - 1))
                return ins
            P.op("tensor", f_gmm, reads=[gk, "xn2_fm"], writes=["psA"], tag=f"Dg{j}")
            def f_umm(t, uview=uview):
                for a in range(NCk):
                    for qq in range(4):
                        ins = t.matmul(psB[:, qq * 512:(qq + 1) * 512],
                                       uview[:, a * 128:(a + 1) * 128],
                                       xn2_fm[:, a * T + qq * 512:a * T + (qq + 1) * 512],
                                       start=(a == 0), stop=(a == NCk - 1))
                return ins
            P.op("tensor", f_umm, reads=[uk, "xn2_fm"], writes=["psB"], tag=f"Dh{j}")
            guv = gu_ar[:, jj * T:(jj + 1) * T]
            for hf in range(2):
                P.op("scalar", (lambda hf=hf, j=j: lambda s: s.activation(
                    out=sil[hf], in_=psA[:, hf * 1024:(hf + 1) * 1024],
                    func=AF.Silu, scale=s_gate[:, j:j + 1]))(),
                     reads=["psA", "s_gate"], writes=[f"sil{hf}", "outasm"], tag=f"DA{j}{hf}")
                P.op("vector", (lambda hf=hf, j=j, guv=guv: lambda v: v.scalar_tensor_tensor(
                    out=guv[:, hf * 1024:(hf + 1) * 1024], in0=psB[:, hf * 1024:(hf + 1) * 1024],
                    scalar=s_up[:, j:j + 1], in1=sil[hf], op0=ALU.mult, op1=ALU.mult))(),
                     reads=["psB", "s_up", f"sil{hf}"], writes=["gu"], tag=f"DC{j}{hf}")
        last = gi == len(GROUPS) - 1
        jbase = jlist[0]
        nj = len(jlist)
        if not last:
            for i in range(NCk):
                P.op("sync", (lambda i=i, jbase=jbase, nj=nj: lambda e: e.dma_start(
                    out=wdT_col[:, 0:nj * 128].rearrange("p (m n) -> p m n", m=nj),
                    in_=wdT_d[jbase * 128:(jbase + nj) * 128, i * 128:(i + 1) * 128]
                        .rearrange("(m p) n -> p m n", p=128)))(),
                     reads=["wdT"], writes=["wdTcol"], dma=True, tag=f"Dc{gi}{i}")
                pst, pkey = (psA, "psA") if i % 2 == 0 else (psB, "psB")
                def f_dmm(t, pst=pst, nj=nj):
                    for jj in range(nj):
                        for qq in range(4):
                            ins = t.matmul(pst[:, qq * 512:(qq + 1) * 512],
                                           wdT_col[:, jj * 128:(jj + 1) * 128],
                                           gu_ar[:, jj * T + qq * 512:jj * T + (qq + 1) * 512],
                                           start=(jj == 0), stop=(jj == nj - 1))
                    return ins
                P.op("tensor", f_dmm, reads=["wdTcol", "gu"], writes=[pkey], tag=f"Dd{gi}{i}")
                if gi == 0:
                    P.op("scalar", (lambda pst=pst: lambda s: s.copy(out=outasm, in_=pst))(),
                         reads=[pkey], writes=["outasm", "sil0", "sil1"], tag=f"De{gi}{i}")
                else:
                    P.op("gpsimd", (lambda i=i: lambda g: g.dma_start(
                        out=s2t[0], in_=part_d[i * 128:(i + 1) * 128, :]))(),
                         reads=["part_d", "s2t0"], writes=["s2t0"], dma=True, tag=f"Dpl{gi}{i}")
                    def f_acc(v, pst=pst):
                        return v.tensor_tensor(out=outasm, in0=pst, in1=s2t[0], op=ALU.add)
                    P.op("vector", f_acc, reads=[pkey, "s2t0"], writes=["outasm", "sil0", "sil1"], tag=f"De{gi}{i}")
                P.op("gpsimd", (lambda i=i: lambda g: g.dma_start(
                    out=part_d[i * 128:(i + 1) * 128, :], in_=outasm))(),
                     reads=["outasm"], writes=["part_d"], dma=True, tag=f"Dw{gi}{i}")
        else:
            P.transfer("xn2_fm", "hiT")
            P.transfer("xn2_fm", "loT")
            for th in range(2):
                for i in range(NCk):
                    P.op("sync", (lambda i=i, jbase=jbase, nj=nj: lambda e: e.dma_start(
                        out=wdT_col[:, 0:nj * 128].rearrange("p (m n) -> p m n", m=nj),
                        in_=wdT_d[jbase * 128:(jbase + nj) * 128, i * 128:(i + 1) * 128]
                            .rearrange("(m p) n -> p m n", p=128)))(),
                         reads=["wdT"], writes=["wdTcol"], dma=True, tag=f"Dc2{th}{i}")
                    pst, pkey = (psA, "psA") if i % 2 == 0 else (psB, "psB")
                    def f_dmm(t, pst=pst, nj=nj, th=th):
                        for jj in range(nj):
                            for qq in range(2):
                                ins = t.matmul(pst[:, qq * 512:(qq + 1) * 512],
                                               wdT_col[:, jj * 128:(jj + 1) * 128],
                                               gu_ar[:, jj * T + th * 1024 + qq * 512:
                                                     jj * T + th * 1024 + (qq + 1) * 512],
                                               start=(jj == 0), stop=(jj == nj - 1))
                        return ins
                    P.op("tensor", f_dmm, reads=["wdTcol", "gu"], writes=[pkey], tag=f"DD{th}{i}")
                    P.op("gpsimd", (lambda i=i, th=th: lambda g: g.dma_start(
                        out=s2t[0][:, 0:1024],
                        in_=part_d[i * 128:(i + 1) * 128, th * 1024:(th + 1) * 1024]))(),
                         reads=["part_d", "s2t0"], writes=["s2t0"], dma=True, tag=f"DP{th}{i}")
                    def f_fin(v, pst=pst, i=i):
                        P.seq(v.tensor_tensor(out=outasm[:, 0:1024], in0=pst[:, 0:1024],
                                              in1=s2t[0][:, 0:1024], op=ALU.add), v)
                        P.seq(v.tensor_scalar(out=outasm[:, 0:1024], in0=outasm[:, 0:1024],
                                              scalar1=s_down[:, i:i + 1], scalar2=None, op0=ALU.mult), v)
                        P.seq(v.tensor_copy(out=hilo[:, 0:1024], in_=outasm[:, 0:1024]), v)
                        return v.tensor_tensor(out=hilo[:, 1024:2048], in0=outasm[:, 0:1024],
                                               in1=hilo[:, 0:1024], op=ALU.subtract)
                    P.op("vector", f_fin, reads=[pkey, "s2t0", "s_down"],
                         writes=["outasm", "hilo", "sil0", "sil1"], tag=f"DF{th}{i}")
                    P.op("sync", (lambda i=i: lambda e: e.dma_start(
                        out=hiT[:, 0:16384].rearrange("p (tt ii n) -> p tt ii n", tt=8, ii=NCk)[:, :, i, :],
                        in_=hilo[:, 0:1024].rearrange("p (tt n) -> p tt n", tt=8), transpose=True))(),
                         reads=["hilo"], writes=["hiT"], dma=True, tag=f"DH{th}{i}")
                    P.op("sync", (lambda i=i: lambda e: e.dma_start(
                        out=hiT[:, LOT:LOT + 16384].rearrange("p (tt ii n) -> p tt ii n", tt=8, ii=NCk)[:, :, i, :],
                        in_=hilo[:, 1024:2048].rearrange("p (tt n) -> p tt n", tt=8), transpose=True))(),
                         reads=["hilo"], writes=["loT"], dma=True, tag=f"DL{th}{i}")
                for tt in range(8):
                    ti = th * 8 + tt
                    P.op("gpsimd", (lambda ti=ti: lambda g: g.dma_start(
                        out=s2t[1], in_=s2_d[ti * 128:(ti + 1) * 128, :]))(),
                         reads=["s2_d", "s2t1"], writes=["s2t1"], dma=True, tag=f"DS{ti}")
                    def f_asm(v, tt=tt):
                        P.seq(v.tensor_tensor(out=outasm, in0=hiT[:, tt * 2048:(tt + 1) * 2048],
                                              in1=hiT[:, LOT + tt * 2048:LOT + (tt + 1) * 2048], op=ALU.add), v)
                        return v.tensor_tensor(out=outasm, in0=outasm, in1=s2t[1], op=ALU.add)
                    P.op("vector", f_asm, reads=["hiT", "loT", "s2t1"],
                         writes=["outasm", "sil0", "sil1"], tag=f"DO{ti}")
                    P.op("gpsimd", (lambda ti=ti: lambda g: g.dma_start(
                        out=out_d[ti * 128:(ti + 1) * 128, :], in_=outasm))(),
                         reads=["outasm"], writes=[f"out{ti}", "outasm"], dma=True, tag=f"DW{ti}")

    if upto >= 8:
        P.op("gpsimd", lambda g: g.engine_nop(),
             reads=[f"out{ti}" for ti in range(NT)], tag="fin")
    else:
        P.op("gpsimd", lambda g: g.dma_start(out=out_d[0:128, :], in_=x_d[0:128, :]),
             writes=["outdbg"], dma=True, tag="dbgout")
        P.op("gpsimd", lambda g: g.engine_nop(), reads=["outdbg"], tag="fin")

    n_ops, n_inc = P.emit()
    return nc, n_ops, n_inc


_NC_CACHE = {}


def kernel(L0_attn_x, L0_attn_Wq, L0_attn_Wk, L0_attn_Wv, L0_attn_Wp,
           L0_ffn_wq_gate, L0_ffn_s_gate, L0_ffn_wq_up, L0_ffn_s_up,
           L0_ffn_wq_down, L0_ffn_s_down):
    x = np.ascontiguousarray(np.asarray(L0_attn_x, dtype=np.float32))
    wq = np.ascontiguousarray(np.asarray(L0_attn_Wq, dtype=np.float32))
    wk = np.ascontiguousarray(np.asarray(L0_attn_Wk, dtype=np.float32))
    wv = np.ascontiguousarray(np.asarray(L0_attn_Wv, dtype=np.float32))
    wp = np.ascontiguousarray(np.asarray(L0_attn_Wp, dtype=np.float32))
    wg = np.ascontiguousarray(np.asarray(L0_ffn_wq_gate, dtype=np.int32))
    wu = np.ascontiguousarray(np.asarray(L0_ffn_wq_up, dtype=np.int32))
    wd = np.ascontiguousarray(np.asarray(L0_ffn_wq_down, dtype=np.int32))
    sg = np.ascontiguousarray(np.asarray(L0_ffn_s_gate, dtype=np.float32).reshape(NH, 128).T)
    su = np.ascontiguousarray(np.asarray(L0_ffn_s_up, dtype=np.float32).reshape(NH, 128).T)
    sd = np.ascontiguousarray(np.asarray(L0_ffn_s_down, dtype=np.float32).reshape(NCk, 128).T)

    if "nc" not in _NC_CACHE:
        _NC_CACHE["nc"] = build()[0]
    nc = _NC_CACHE["nc"]
    in_maps = [{"x": x[b], "wq": wq, "wk": wk, "wv": wv, "wp": wp,
                "wg": wg, "wu": wu, "wd": wd, "sg": sg, "su": su, "sd": sd}
               for b in range(8)]
    res = run_bass_kernel_spmd(nc, in_maps, core_ids=list(range(8)))
    return np.stack([r["out"] for r in res.results], axis=0)


# version 05
# speedup vs baseline: 1.0045x; 1.0045x over previous
import numpy as np
import concourse.bass as bass
import concourse.mybir as mybir
from concourse.bass_utils import run_bass_kernel_spmd
import ml_dtypes

F32 = mybir.dt.float32
BF16 = mybir.dt.bfloat16
I32 = mybir.dt.int32
AF = mybir.ActivationFunctionType
ALU = mybir.AluOpType
AX = mybir.AxisListType
BF = ml_dtypes.bfloat16

T = 2048
C = 2048
H = 16
D = 128
HID = 5632
NT = T // 128
NCk = C // 128
NH = HID // 128
EPS = 1e-5
LN127 = float(np.log(127.0))
INV127 = 1.0 / 127.0
NELW = float(C * C)
GROUPS = [list(range(0, 15)), list(range(15, 30)), list(range(30, 44))]


class Prog:

    ENGINES = ("sync", "gpsimd", "tensor", "vector", "scalar")

    def __init__(self, nc):
        self.nc = nc
        self.ops = []
        self.last_write = {}
        self.readers = {}

    def op(self, eng, fn, reads=(), writes=(), dma=False, tag=""):
        deps = set()
        for k in reads:
            w = self.last_write.get(k)
            if w is not None:
                deps.add(w)
        for k in writes:
            w = self.last_write.get(k)
            if w is not None:
                deps.add(w)
            for r in self.readers.get(k, ()):
                deps.add(r)
        i = len(self.ops)
        if dma:
            assert writes, f"dma op {tag} needs a write resource for its semaphore"
        self.ops.append(dict(eng=eng, fn=fn, deps=sorted(deps), dma=dma, tag=tag,
                             w0=(writes[0] if writes else None)))
        for k in reads:
            self.readers.setdefault(k, []).append(i)
        for k in writes:
            self.last_write[k] = i
            self.readers[k] = []
        return i

    def transfer(self, old, new):
        if old in self.last_write:
            lw = self.last_write[old]
            if new not in self.last_write or self.last_write[new] < lw:
                self.last_write[new] = lw
        self.readers.setdefault(new, [])
        self.readers[new] += list(self.readers.get(old, ()))

    def seq(self, instr, handle):
        e = self._cur
        instr.then_inc(self._aux[e], 1)
        self._auxcnt[e] += 1
        handle.wait_ge(self._aux[e], self._auxcnt[e])

    def emit(self):
        nc = self.nc
        self._aux = {e: nc.semaphore(f"aux_{e}").__enter__() for e in self.ENGINES}
        self._auxcnt = {e: 0 for e in self.ENGINES}
        self._cur = None
        need_inc = set()
        for i, o in enumerate(self.ops):
            if o["dma"]:
                need_inc.add(i)
            for d in o["deps"]:
                need_inc.add(d)
        semid = {}
        res2sem = {}
        for i, o in enumerate(self.ops):
            if o["dma"]:
                r = o["w0"]
                if r not in res2sem:
                    res2sem[r] = f"dma_{len(res2sem)}_{r}"
                semid[i] = res2sem[r]
            else:
                semid[i] = f"eng_{o['eng']}"
        counts = {}
        running = {}
        for i, o in enumerate(self.ops):
            if i in need_inc:
                s = semid[i]
                running[s] = running.get(s, 0) + (16 if o["dma"] else 1)
                counts[i] = running[s]
        sems = {s: nc.semaphore(s).__enter__() for s in
                set(semid.values())}
        ops = self.ops

        def engine_body(ename, handle):
            waited = {}
            self._cur = ename
            for i, o in enumerate(ops):
                if o["eng"] != ename:
                    continue
                self._cur = ename
                for d in o["deps"]:
                    s = semid[d]
                    if d in counts and counts[d] > waited.get(s, 0):
                        handle.wait_ge(sems[s], counts[d])
                        waited[s] = counts[d]
                instr = o["fn"](handle)
                if i in need_inc:
                    assert instr is not None, f"op {i} {o['tag']} returned None but needs inc"
                    instr.then_inc(sems[semid[i]], 16 if o["dma"] else 1)

        with nc.Block() as block:
            @block.sync
            def _(e):
                engine_body("sync", e)

            @block.gpsimd
            def _(e):
                engine_body("gpsimd", e)

            @block.tensor
            def _(e):
                engine_body("tensor", e)

            @block.vector
            def _(e):
                engine_body("vector", e)

            @block.scalar
            def _(e):
                engine_body("scalar", e)
        return len(self.ops), len(need_inc)


def build(upto=99, bheads=16, bcut=9, bq=9):
    nc = bass.Bass("TRN2", target_bir_lowering=False, debug=False)

    x_d = nc.dram_tensor("x", [T, C], F32, kind="ExternalInput").ap()
    w_attn = {k: nc.dram_tensor(f"w{k}", [C, C], F32, kind="ExternalInput").ap()
              for k in ("q", "k", "v", "p")}
    wg_d = nc.dram_tensor("wg", [HID, C], I32, kind="ExternalInput").ap()
    wu_d = nc.dram_tensor("wu", [HID, C], I32, kind="ExternalInput").ap()
    wd_d = nc.dram_tensor("wd", [C, HID], I32, kind="ExternalInput").ap()
    sg_d = nc.dram_tensor("sg", [128, NH], F32, kind="ExternalInput").ap()
    su_d = nc.dram_tensor("su", [128, NH], F32, kind="ExternalInput").ap()
    sd_d = nc.dram_tensor("sd", [128, NCk], F32, kind="ExternalInput").ap()
    out_d = nc.dram_tensor("out", [T, C], F32, kind="ExternalOutput").ap()

    tern_d = {k: nc.dram_tensor(f"tern{k}", [C, C], BF16, kind="Internal").ap()
              for k in ("q", "k", "v", "p")}
    wgb_d = nc.dram_tensor("wgb", [HID, C], BF16, kind="Internal").ap()
    wub_d = nc.dram_tensor("wub", [HID, C], BF16, kind="Internal").ap()
    wdb_d = nc.dram_tensor("wdb", [C, HID], BF16, kind="Internal").ap()
    wdT_d = nc.dram_tensor("wdT", [HID, C], BF16, kind="Internal").ap()
    s2_d = nc.dram_tensor("s2", [T, C], F32, kind="Internal").ap()
    part_d = nc.dram_tensor("part", [C, T], F32, kind="Internal").ap()

    ar1 = nc.sbuf_tensor("ar1", [128, 16384], F32).__enter__()[:]
    ar2 = nc.sbuf_tensor("ar2", [128, 16384], F32).__enter__()[:]
    ar3 = nc.sbuf_tensor("ar3", [128, 14848], F32).__enter__()[:]
    sm = nc.sbuf_tensor("smalls", [128, 512], F32).__enter__()[:]
    ones_r = nc.sbuf_tensor("ones_r", [1, 128], F32).__enter__()[:]
    ones_c = nc.sbuf_tensor("ones_c", [128, 1], F32).__enter__()[:]
    rowscr = nc.sbuf_tensor("rowscr", [1, 16], F32).__enter__()[:]

    def carve(ar, cur, n):
        return ar[:, cur:cur + n], cur + n

    xq_fm = ar1.bitcast(BF16)
    WpT = ar1.bitcast(BF16)
    gu_ar = ar1.bitcast(BF16)

    cur = 0
    xs0, cur = carve(ar2, cur, 2048)
    xs1, cur = carve(ar2, cur, 2048)
    sqv, cur = carve(ar2, cur, 2048)
    xnv, cur = carve(ar2, cur, 2048)
    rxq = ar2[:, 8192:9216].bitcast(BF16)
    yqT = ar2.bitcast(BF16)
    xn2_fm = ar2.bitcast(BF16)
    hiT = ar2.bitcast(BF16)
    LOT = 16384

    cur = 0
    a_ternT, cur = carve(ar3, cur, 3072)
    a_qkvraw, cur = carve(ar3, cur, 1536)
    a_i32, cur = carve(ar3, cur, 1024)
    a_p127a, cur = carve(ar3, cur, 1024)
    a_p127b, cur = carve(ar3, cur, 1024)
    a_qk_tm, cur = carve(ar3, cur, 1024)
    a_v_tm, cur = carve(ar3, cur, 1024)
    a_q_fm, cur = carve(ar3, cur, 1024)
    a_k_fm, cur = carve(ar3, cur, 1024)
    a_r, cur = carve(ar3, cur, 1024)
    a_rT0, cur = carve(ar3, cur, 1024)
    a_rT1, cur = carve(ar3, cur, 1024)
    assert cur <= 14848, cur

    off = {}
    o = 0
    for nm, n in (("ternT", 3072), ("qkvraw", 1536), ("i32", 1024), ("p127a", 1024),
                  ("p127b", 1024), ("qk_tm", 1024), ("v_tm", 1024), ("q_fm", 1024),
                  ("k_fm", 1024), ("r", 1024), ("rT0", 1024), ("rT1", 1024)):
        off[nm] = o
        o += n
    ternT = a_ternT.bitcast(BF16)
    qkvraw = a_qkvraw.bitcast(BF16)
    y_q = a_qkvraw.bitcast(BF16)
    i32b = a_i32.bitcast(I32)
    i32c = a_qkvraw[:, 0:1024].bitcast(I32)
    p127 = [a_p127a, a_p127b]
    q_tm = a_qk_tm.bitcast(BF16)
    k_tm = a_qk_tm.bitcast(BF16)
    y_raw = a_qk_tm.bitcast(BF16)
    v_tm = a_v_tm.bitcast(BF16)
    q_fm = a_q_fm.bitcast(BF16)
    k_fm = a_k_fm.bitcast(BF16)
    r_bf = a_r.bitcast(BF16)
    rT = [a_rT0.bitcast(BF16), a_rT1.bitcast(BF16)]
    wrow = [a_p127a, a_p127b]
    w2i = [a_i32.bitcast(I32)[:, 0:1024], a_v_tm.bitcast(I32)[:, 0:1024]]
    w2o = [a_q_fm.bitcast(BF16)[:, 0:2048], a_k_fm.bitcast(BF16)[:, 0:2048]]
    xsC = [a_ternT[:, 0:2048], ar3[:, off["qk_tm"]:off["qk_tm"] + 2048]]
    wgT = [a_ternT.bitcast(BF16)[:, 0:2048], a_ternT.bitcast(BF16)[:, 2048:4096]]
    wuT = [a_qkvraw.bitcast(BF16)[:, 0:2048],
           a_ternT.bitcast(BF16)[:, 4096:6144]]
    wdT_col = ar3[:, off["p127a"]:off["p127a"] + 2048].bitcast(BF16)
    sqscr = ar3[:, off["p127a"]:off["p127a"] + 2048]
    sil = [ar3[:, off["qk_tm"]:off["qk_tm"] + 1024],
           ar3[:, off["v_tm"]:off["v_tm"] + 1024]]
    s2t = [ar3[:, off["q_fm"]:off["q_fm"] + 2048],
           ar3[:, off["r"]:off["r"] + 2048]]
    outasm = ar3[:, off["qk_tm"]:off["qk_tm"] + 2048]
    hilo = a_i32.bitcast(BF16)

    scur = 0

    def scarve(n):
        nonlocal scur
        ap = sm[:, scur:scur + n]
        scur += n
        return ap

    consts = scarve(6)
    c_eps, c_ln127, c_one, c_tiny, c_half, cm128 = (consts[:, i:i + 1] for i in range(6))
    Ceff = scarve(1)
    colM = scarve(3 * NT)
    colU = scarve(3 * NT)
    colV = scarve(3 * NT)
    colR = scarve(NT)
    colG = scarve(NT)
    colB = scarve(NT)
    colS = scarve(NT)
    colSh = scarve(NT)
    colSi = scarve(NT)
    colMy = scarve(NT)
    colUy = scarve(NT)
    colVy = scarve(NT)
    invq_xq = scarve(NT)
    aW = scarve(4)
    sWi = scarve(4)
    asum = scarve(NCk)
    stat = scarve(8)
    tmp = scarve(16)
    s_gate = scarve(NH)
    s_up = scarve(NH)
    s_down = scarve(NCk)
    pM = scarve(1)
    pU = scarve(1)
    pV = scarve(1)
    assert scur <= 512, scur

    psA = nc.psum_tensor("psA", [128, 2048], F32).__enter__()[:]
    psB = nc.psum_tensor("psB", [128, 2048], F32).__enter__()[:]

    P = Prog(nc)

    def f_init(v):
        P.seq(v.memset(c_eps, EPS), v)
        P.seq(v.memset(c_ln127, LN127), v)
        P.seq(v.memset(c_one, 1.0), v)
        P.seq(v.memset(c_tiny, float(2.0 ** -30)), v)
        P.seq(v.memset(c_half, 0.75), v)
        P.seq(v.memset(cm128, -128.0), v)
        P.seq(v.memset(ones_c, 1.0), v)
        P.seq(v.memset(ones_r, 1.0), v)
        P.seq(v.tensor_copy(out=i32b[:, 0:1], in_=c_half), v)
        return v.tensor_scalar(out=Ceff, in0=i32b[:, 0:1], scalar1=-0.5, scalar2=128.5,
                               op0=ALU.mult, op1=ALU.add)
    P.op("vector", f_init, writes=["consts", "Ceff", "i32b", "ones"], tag="init")
    P.op("gpsimd", lambda g: g.dma_start(out=s_gate, in_=sg_d), writes=["s_gate"], dma=True)
    P.op("gpsimd", lambda g: g.dma_start(out=s_up, in_=su_d), writes=["s_up"], dma=True)
    P.op("gpsimd", lambda g: g.dma_start(out=s_down, in_=sd_d), writes=["s_down"], dma=True)

    for wi, wkey in (enumerate(("q", "k", "v", "p")) if upto >= 1 else []):
        wd = w_attn[wkey]
        for ch in range(NCk):
            for hf in range(2):
                bi = (ch * 2 + hf) % 2
                buf, bkey = wrow[bi], f"wrow{bi}"
                src = wd[ch * 128:(ch + 1) * 128, hf * 1024:(hf + 1) * 1024]
                P.op("sync", (lambda b, s: lambda e: e.dma_start(out=b, in_=s))(buf, src),
                     writes=[bkey], dma=True, tag=f"w0l{wkey}{ch}{hf}")
                dst = asum[:, ch:ch + 1] if hf == 0 else tmp[:, 4:5]
                dkey = "asum" if hf == 0 else "tmp45"
                P.op("scalar", (lambda b, d: lambda e: e.activation(
                    out=sqv[:, 0:1024], in_=b, func=AF.Abs, accum_out=d))(buf, dst),
                     reads=[bkey], writes=["sqv", dkey], tag=f"w0a{wkey}{ch}{hf}")
                if hf == 1:
                    P.op("vector", (lambda ci: lambda v: v.tensor_tensor(
                        out=asum[:, ci:ci + 1], in0=asum[:, ci:ci + 1], in1=tmp[:, 4:5], op=ALU.add))(ch),
                         reads=["tmp45"], writes=["asum"], tag=f"w0+{wkey}{ch}")
        P.op("vector", lambda v: v.tensor_reduce(out=tmp[:, 5:6], in_=asum[:, 0:NCk],
                                                 axis=AX.X, op=ALU.add),
             reads=["asum"], writes=["tmp5"], tag=f"w0r{wkey}")
        P.op("tensor", lambda t: t.matmul(psA[0:1, 0:1], ones_c, tmp[:, 5:6], start=True, stop=True),
             reads=["tmp5", "ones"], writes=["psA"], tag=f"w0m{wkey}")
        def f_sml(v):
            P.seq(v.tensor_scalar(out=rowscr[0:1, 0:1], in0=psA[0:1, 0:1], scalar1=1.0 / NELW,
                                  scalar2=EPS, op0=ALU.mult, op1=ALU.max), v)
            return v.reciprocal(out=rowscr[0:1, 1:2], in_=rowscr[0:1, 0:1])
        P.op("vector", f_sml, reads=["psA"], writes=["rowscr"], tag=f"w0s{wkey}")
        P.op("tensor", lambda t: t.matmul(psB[:, 0:2], ones_r, rowscr[0:1, 0:2], start=True, stop=True),
             reads=["rowscr", "ones"], writes=["psB"], tag=f"w0b{wkey}")
        def f_cp(v, wi=wi):
            v.tensor_copy(out=aW[:, wi:wi + 1], in_=psB[:, 0:1])
            return v.tensor_copy(out=sWi[:, wi:wi + 1], in_=psB[:, 1:2])
        P.op("vector", f_cp, reads=["psB"], writes=["aW", "sWi"], tag=f"w0c{wkey}")

    ternbuf = rxq
    for wi, wkey in (enumerate(("q", "k", "v", "p")) if upto >= 2 else []):
        wd = w_attn[wkey]
        for ch in range(NCk):
            for hf in range(2):
                bi = hf
                buf, bkey = wrow[bi], f"wrow{bi}"
                src = wd[ch * 128:(ch + 1) * 128, hf * 1024:(hf + 1) * 1024]
                P.op("sync", (lambda b, s: lambda e: e.dma_start(out=b, in_=s))(buf, src),
                     writes=[bkey], dma=True, tag=f"w1l{wkey}{ch}{hf}")
                def f_tern(v, b=buf, wi=wi, hf=hf):
                    P.seq(v.tensor_scalar(out=i32b[:, 0:1024], in0=b, scalar1=sWi[:, wi:wi + 1],
                                          scalar2=Ceff, op0=ALU.mult, op1=ALU.add), v)
                    P.seq(v.tensor_scalar(out=i32b[:, 0:1024], in0=i32b[:, 0:1024], scalar1=127.0,
                                          scalar2=129.0, op0=ALU.max, op1=ALU.min), v)
                    return v.tensor_scalar(out=ternbuf[:, hf * 1024:(hf + 1) * 1024],
                                           in0=i32b[:, 0:1024], scalar1=-128.0, scalar2=None, op0=ALU.add)
                P.op("vector", f_tern, reads=[bkey, "sWi", "Ceff"], writes=["i32b", "ternbuf"],
                     tag=f"w1t{wkey}{ch}{hf}")
            P.op("gpsimd", (lambda wkey, ch: lambda g: g.dma_start(
                out=tern_d[wkey][ch * 128:(ch + 1) * 128, :], in_=ternbuf))(wkey, ch),
                 reads=["ternbuf"], writes=[f"tern_d{wkey}"], dma=True, tag=f"w1s{wkey}{ch}")

    P.transfer("i32b", "w2i0")
    for nm, src_d, dst_d in ((("g", wg_d, wgb_d), ("u", wu_d, wub_d)) if upto >= 3 else []):
        for ch in range(NH):
            bi = ch % 2
            ibuf, ikey = w2i[bi], ("w2i0" if bi == 0 else "w2i1")
            obuf, okey = w2o[bi], ("w2o0" if bi == 0 else "w2o1")
            for hf in range(2):
                P.op("sync", (lambda ib, s=src_d, ch=ch, hf=hf: lambda e: e.dma_start(
                    out=ib, in_=s[ch * 128:(ch + 1) * 128, hf * 1024:(hf + 1) * 1024]))(ibuf),
                     writes=[ikey], dma=True, tag=f"w2l{nm}{ch}{hf}")
                P.op("vector", (lambda ib, ob, hf=hf: lambda v: v.tensor_scalar(
                    out=ob[:, hf * 1024:(hf + 1) * 1024], in0=ib, scalar1=-128.0,
                    scalar2=None, op0=ALU.add))(ibuf, obuf),
                     reads=[ikey], writes=[okey], tag=f"w2c{nm}{ch}{hf}")
            P.op("gpsimd", (lambda ob, d=dst_d, ch=ch: lambda g: g.dma_start(
                out=d[ch * 128:(ch + 1) * 128, :], in_=ob))(obuf),
                 reads=[okey], writes=[f"wb{nm}"], dma=True, tag=f"w2s{nm}{ch}")
    P.transfer("w2i0", "i32b")
    dn_o = xq_fm[:, 0:HID]
    dn_t = xq_fm[:, 8192:8192 + NH * 128]
    for ch in (range(NCk) if upto >= 3 else []):
        for sub in range(6):
            w = 1024 if sub < 5 else 512
            P.op("sync", (lambda s=wd_d, ch=ch, sub=sub, w=w: lambda e: e.dma_start(
                out=i32b[:, 0:w], in_=s[ch * 128:(ch + 1) * 128, sub * 1024:sub * 1024 + w]))(),
                 writes=["i32b"], dma=True, tag=f"w2ld{ch}{sub}")
            P.op("vector", (lambda sub=sub, w=w: lambda v: v.tensor_scalar(
                out=dn_o[:, sub * 1024:sub * 1024 + w], in0=i32b[:, 0:w],
                scalar1=-128.0, scalar2=None, op0=ALU.add))(),
                 reads=["i32b"], writes=["dn_o"], tag=f"w2cd{ch}{sub}")
        P.op("gpsimd", (lambda ch=ch: lambda g: g.dma_start(
            out=wdb_d[ch * 128:(ch + 1) * 128, :], in_=dn_o))(),
             reads=["dn_o"], writes=["wdb"], dma=True, tag=f"w2sd{ch}")
        P.op("sync", (lambda ch=ch: lambda e: e.dma_start(
            out=dn_t.rearrange("p (m n) -> p m n", m=NH),
            in_=dn_o.rearrange("p (m n) -> p m n", m=NH), transpose=True))(),
             reads=["dn_o"], writes=["dn_t"], dma=True, tag=f"w2tp{ch}")
        P.op("gpsimd", (lambda ch=ch: lambda g: g.dma_start(
            out=wdT_d[:, ch * 128:(ch + 1) * 128].rearrange("(m p) n -> p m n", p=128),
            in_=dn_t.rearrange("p (m n) -> p m n", m=NH)))(),
             reads=["dn_t"], writes=["wdT"], dma=True, tag=f"w2st{ch}")

    P.transfer("dn_o", "xq_fm")
    P.transfer("dn_t", "xq_fm")
    P.transfer("ternbuf", "rxq")
    for ti in (range(NT) if upto >= 4 else []):
        xb, xkey = (xs0, "xs0") if ti % 2 == 0 else (xs1, "xs1")
        P.op("gpsimd", (lambda xb=xb, ti=ti: lambda g: g.dma_start(
            out=xb, in_=x_d[ti * 128:(ti + 1) * 128, :]))(),
             writes=[xkey], dma=True, tag=f"Ax{ti}")
        P.op("vector", (lambda xb=xb: lambda v: v.tensor_reduce(
            out=stat[:, 0:1], in_=xb, axis=AX.X, op=ALU.add))(),
             reads=[xkey], writes=["stat0"], tag=f"As{ti}")
        P.op("scalar", (lambda xb=xb: lambda s: s.activation(
            out=sqv, in_=xb, func=AF.Square, accum_out=stat[:, 1:2]))(),
             reads=[xkey], writes=["sqv", "stat1"], tag=f"Aq{ti}")
        def f_stats(v):
            P.seq(v.tensor_scalar(out=stat[:, 2:3], in0=stat[:, 0:1], scalar1=1.0 / C, scalar2=None, op0=ALU.mult), v)
            P.seq(v.tensor_tensor(out=stat[:, 3:4], in0=stat[:, 2:3], in1=stat[:, 2:3], op=ALU.mult), v)
            P.seq(v.tensor_scalar(out=stat[:, 4:5], in0=stat[:, 1:2], scalar1=1.0 / C, scalar2=None, op0=ALU.mult), v)
            return v.tensor_tensor(out=stat[:, 4:5], in0=stat[:, 4:5], in1=stat[:, 3:4], op=ALU.subtract)
        P.op("vector", f_stats, reads=["stat0", "stat1"], writes=["statm"], tag=f"Av{ti}")
        def f_rstd(s):
            P.seq(s.activation(out=stat[:, 5:6], in_=stat[:, 4:5], func=AF.Ln, bias=c_eps), s)
            return s.activation(out=stat[:, 6:7], in_=stat[:, 5:6], func=AF.Exp, scale=-0.5)
        P.op("scalar", f_rstd, reads=["statm", "consts"], writes=["rstd"], tag=f"Ar{ti}")
        def f_nmr(v):
            P.seq(v.tensor_scalar(out=stat[:, 7:8], in0=stat[:, 2:3], scalar1=-1.0, scalar2=None, op0=ALU.mult), v)
            return v.tensor_tensor(out=stat[:, 7:8], in0=stat[:, 7:8], in1=stat[:, 6:7], op=ALU.mult)
        P.op("vector", f_nmr, reads=["statm", "rstd"], writes=["nmr"], tag=f"An{ti}")
        P.op("scalar", (lambda xb=xb: lambda s: s.activation(
            out=xnv, in_=xb, func=AF.Identity, scale=stat[:, 6:7], bias=stat[:, 7:8]))(),
             reads=[xkey, "rstd", "nmr"], writes=["xnv"], tag=f"Aa{ti}")
        P.op("vector", lambda v: v.tensor_reduce(out=tmp[:, 6:7], in_=xnv, axis=AX.X,
                                                 op=ALU.max, apply_absolute_value=True),
             reads=["xnv"], writes=["qm"], tag=f"Am{ti}")
        def f_qs(v, ti=ti):
            P.seq(v.tensor_scalar(out=tmp[:, 7:8], in0=tmp[:, 6:7], scalar1=EPS, scalar2=None, op0=ALU.max), v)
            P.seq(v.tensor_scalar(out=invq_xq[:, ti:ti + 1], in0=tmp[:, 7:8], scalar1=INV127, scalar2=None, op0=ALU.mult), v)
            P.seq(v.reciprocal(out=tmp[:, 8:9], in_=tmp[:, 7:8]), v)
            return v.tensor_scalar(out=tmp[:, 9:10], in0=tmp[:, 8:9], scalar1=127.0, scalar2=None, op0=ALU.mult)
        P.op("vector", f_qs, reads=["qm"], writes=["invq_xq", "qs"], tag=f"Aw{ti}")
        for hf in range(2):
            P.op("vector", (lambda hf=hf: lambda v: v.tensor_scalar(
                out=i32b[:, 0:1024], in0=xnv[:, hf * 1024:(hf + 1) * 1024],
                scalar1=tmp[:, 9:10], scalar2=Ceff, op0=ALU.mult, op1=ALU.add))(),
                 reads=["xnv", "qs", "Ceff"], writes=["i32b"], tag=f"A1{ti}{hf}")
            P.op("vector", (lambda hf=hf: lambda v: v.tensor_scalar(
                out=rxq[:, hf * 1024:(hf + 1) * 1024], in0=i32b[:, 0:1024],
                scalar1=-128.0, scalar2=None, op0=ALU.add))(),
                 reads=["i32b"], writes=["rxq"], tag=f"A2{ti}{hf}")
        P.op("sync", (lambda ti=ti: lambda e: e.dma_start(
            out=xq_fm.rearrange("p (j t) -> p j t", j=NCk)[:, :, ti * 128:(ti + 1) * 128],
            in_=rxq.rearrange("p (j n) -> p j n", j=NCk), transpose=True))(),
             reads=["rxq"], writes=["xq_fm"], dma=True, tag=f"At{ti}")

    for nm in ("sqv", "xnv", "xs0", "xs1", "rxq"):
        P.transfer(nm, "yqT")
    P.transfer("wrow0", "p1270")
    P.transfer("wrow1", "p1271")
    P.transfer("w2i1", "v_tm")
    P.transfer("w2o0", "q_fm")
    P.transfer("w2o1", "k_fm")

    for h in (range(min(H, bheads)) if upto >= 5 else []):
        P.transfer("y_q", "qkvraw")
        P.transfer("i32c", "qkvraw")
        P.transfer("y_raw", "qk_tm")
        P.transfer("qk_tm", "y_raw")
        for wi, wkey in enumerate(("q", "k", "v")):
            P.op("sync", (lambda wkey=wkey, wi=wi, h=h: lambda e: e.dma_start(
                out=ternT.rearrange("p (j w) -> p j w", j=NCk)[:, :, wi * 128:(wi + 1) * 128],
                in_=tern_d[wkey][h * 128:(h + 1) * 128, :].rearrange("p (j n) -> p j n", j=NCk),
                transpose=True))(),
                 reads=[f"tern_d{wkey}"], writes=["ternT"], dma=True, tag=f"B{h}T{wkey}")
        colM3 = colM.rearrange("p (w t) -> p w t", w=3)
        for half in range(2):
            for tq in (range(8) if bq >= 1 else []):
                ti = half * 8 + tq
                pst, pkey = (psA, "psA") if ti % 2 == 0 else (psB, "psB")
                def f_mm(t, pst=pst, ti=ti):
                    for j in range(NCk):
                        ins = t.matmul(pst[:, 0:384],
                                       xq_fm[:, j * T + ti * 128:j * T + (ti + 1) * 128],
                                       ternT[:, j * 384:(j + 1) * 384],
                                       start=(j == 0), stop=(j == NCk - 1))
                    return ins
                P.op("tensor", f_mm, reads=["xq_fm", "ternT"], writes=[pkey], tag=f"B{h}q{ti}")
                if bq < 2:
                    continue
                P.op("scalar", (lambda pst=pst, tq=tq: lambda s: s.copy(
                    out=qkvraw[:, tq * 384:(tq + 1) * 384], in_=pst[:, 0:384]))(),
                     reads=[pkey], writes=["qkvraw"], tag=f"B{h}e{ti}")
                if bq < 3:
                    continue
                def f_red3(v, tq=tq, ti=ti):
                    for wi in range(3):
                        ins = v.tensor_reduce(
                            out=colM[:, wi * NT + ti:wi * NT + ti + 1],
                            in_=qkvraw[:, tq * 384 + wi * 128:tq * 384 + (wi + 1) * 128],
                            axis=AX.X, op=ALU.max, apply_absolute_value=True)
                        if wi < 2:
                            P.seq(ins, v)
                    return ins
                P.op("vector", f_red3, reads=["qkvraw"], writes=["colM"], tag=f"B{h}m{ti}")
            if bcut < 2:
                continue
            def f_usml(v, half=half):
                iq = invq_xq[:, half * 8:half * 8 + 8]
                for wi in range(3):
                    sl = slice(wi * NT + half * 8, wi * NT + half * 8 + 8)
                    P.seq(v.tensor_tensor(out=colU[:, sl], in0=colM[:, sl], in1=iq, op=ALU.mult), v)
                    P.seq(v.tensor_scalar(out=colU[:, sl], in0=colU[:, sl], scalar1=aW[:, wi:wi + 1],
                                          scalar2=EPS, op0=ALU.mult, op1=ALU.max), v)
                    P.seq(v.tensor_scalar(out=colV[:, sl], in0=colU[:, sl], scalar1=INV127, scalar2=None, op0=ALU.mult), v)
                    P.seq(v.reciprocal(out=colU[:, sl], in_=colU[:, sl]), v)
                    P.seq(v.tensor_tensor(out=colU[:, sl], in0=colU[:, sl], in1=iq, op=ALU.mult), v)
                    P.seq(v.tensor_scalar(out=colU[:, sl], in0=colU[:, sl], scalar1=aW[:, wi:wi + 1],
                                          scalar2=None, op0=ALU.mult), v)
                    P.seq(v.tensor_scalar(out=colU[:, sl], in0=colU[:, sl], scalar1=127.0,
                                          scalar2=None, op0=ALU.mult), v)
                return v.tensor_scalar(out=colG[:, half * 8:half * 8 + 8],
                                       in0=colV[:, half * 8:half * 8 + 8],
                                       scalar1=float(1.0 / np.sqrt(128.0)), scalar2=None, op0=ALU.mult)
            P.op("vector", f_usml, reads=["colM", "invq_xq", "aW"],
                 writes=["colU", "colV", "colG"], tag=f"B{h}u{half}")
            for wi, (dst, dkey) in enumerate(((q_tm, "qk_tm"), (k_tm, "qk_tm"), (v_tm, "v_tm"))):
                for tq in range(8):
                    ti = half * 8 + tq
                    P.op("vector", (lambda wi=wi, ti=ti, tq=tq: lambda v: v.tensor_scalar(
                        out=i32b[:, 0:128], in0=qkvraw[:, tq * 384 + wi * 128:tq * 384 + (wi + 1) * 128],
                        scalar1=colU[:, wi * NT + ti:wi * NT + ti + 1], scalar2=Ceff,
                        op0=ALU.mult, op1=ALU.add))(),
                         reads=["qkvraw", "colU", "Ceff"], writes=["i32b"], tag=f"B{h}c{wi}{ti}")
                    if wi == 0:
                        fn = (lambda dst=dst, ti=ti: lambda v: v.tensor_scalar(
                            out=dst[:, ti * 128:(ti + 1) * 128], in0=i32b[:, 0:128],
                            scalar1=-128.0, scalar2=None, op0=ALU.add))()
                    else:
                        fn = (lambda dst=dst, ti=ti, wi=wi: lambda v: v.tensor_scalar(
                            out=dst[:, ti * 128:(ti + 1) * 128], in0=i32b[:, 0:128],
                            scalar1=-128.0, scalar2=colV[:, wi * NT + ti:wi * NT + ti + 1],
                            op0=ALU.add, op1=ALU.mult))()
                    P.op("vector", fn, reads=["i32b", "colV"], writes=[dkey], tag=f"B{h}d{wi}{ti}")
                if wi == 0:
                    P.op("sync", (lambda half=half: lambda e: e.dma_start(
                        out=q_fm.rearrange("p (j n) -> p j n", j=NT)[:, half * 8:(half + 1) * 8, :],
                        in_=q_tm[:, half * 1024:(half + 1) * 1024].rearrange("p (j n) -> p j n", j=8),
                        transpose=True))(),
                         reads=["qk_tm"], writes=["q_fm"], dma=True, tag=f"B{h}tq{half}")
                elif wi == 1:
                    P.op("sync", (lambda half=half: lambda e: e.dma_start(
                        out=k_fm.rearrange("p (j n) -> p j n", j=NT)[:, half * 8:(half + 1) * 8, :],
                        in_=k_tm[:, half * 1024:(half + 1) * 1024].rearrange("p (j n) -> p j n", j=8),
                        transpose=True))(),
                         reads=["qk_tm"], writes=["k_fm"], dma=True, tag=f"B{h}tk{half}")
        def emit_av(ti):
            pst, pkey = (psA, "psA") if ti % 2 == 0 else (psB, "psB")
            rb, rkey = rT[ti % 2], f"rT{ti % 2}"
            def f_av(t, pst=pst, rb=rb):
                for j in range(NT):
                    ins = t.matmul(pst[:, 0:128], rb[:, j * 128:(j + 1) * 128],
                                   v_tm[:, j * 128:(j + 1) * 128],
                                   start=(j == 0), stop=(j == NT - 1))
                return ins
            P.op("tensor", f_av, reads=[rkey, "v_tm"], writes=[pkey], tag=f"B{h}A{ti}")
            P.op("scalar", (lambda pst=pst, ti=ti: lambda s: s.activation(
                out=y_raw[:, ti * 128:(ti + 1) * 128], in_=pst[:, 0:128],
                func=AF.Identity, scale=colSi[:, ti:ti + 1]))(),
                 reads=[pkey, "colSi"], writes=["y_raw"], tag=f"B{h}Y{ti}")
            P.op("vector", (lambda ti=ti: lambda v: v.tensor_reduce(
                out=colMy[:, ti:ti + 1], in_=y_raw[:, ti * 128:(ti + 1) * 128],
                axis=AX.X, op=ALU.max, apply_absolute_value=True))(),
                 reads=["y_raw"], writes=["colMy"], tag=f"B{h}M{ti}")

        for ti in (range(NT) if bcut >= 3 else []):
            pst, pkey = (psA, "psA") if ti % 2 == 0 else (psB, "psB")
            def f_smm(t, pst=pst, ti=ti):
                for half in range(4):
                    ins = t.matmul(pst[:, half * 512:(half + 1) * 512],
                                   q_fm[:, ti * 128:(ti + 1) * 128],
                                   k_fm[:, half * 512:(half + 1) * 512],
                                   start=True, stop=True)
                return ins
            P.op("tensor", f_smm, reads=["q_fm", "k_fm"], writes=[pkey], tag=f"B{h}s{ti}")
            P.op("vector", (lambda pst=pst, ti=ti: lambda v: v.tensor_reduce(
                out=colR[:, ti:ti + 1], in_=pst, axis=AX.X, op=ALU.max))(),
                 reads=[pkey], writes=["colR"], tag=f"B{h}x{ti}")
            def f_bias(v, ti=ti):
                P.seq(v.tensor_tensor(out=colB[:, ti:ti + 1], in0=colR[:, ti:ti + 1],
                                      in1=colG[:, ti:ti + 1], op=ALU.mult), v)
                return v.tensor_scalar(out=colB[:, ti:ti + 1], in0=colB[:, ti:ti + 1],
                                       scalar1=-1.0, scalar2=LN127, op0=ALU.mult, op1=ALU.add)
            P.op("vector", f_bias, reads=["colR", "colG"], writes=["colB"], tag=f"B{h}b{ti}")
            for hf in range(2):
                P.op("scalar", (lambda pst=pst, ti=ti, hf=hf: lambda s: s.activation(
                    out=p127[hf], in_=pst[:, hf * 1024:(hf + 1) * 1024], func=AF.Exp,
                    scale=colG[:, ti:ti + 1], bias=colB[:, ti:ti + 1],
                    accum_out=(colSh if hf == 0 else colS)[:, ti:ti + 1]))(),
                     reads=[pkey, "colG", "colB"],
                     writes=[f"p127{hf}", "colSh" if hf == 0 else "colSx"], tag=f"B{h}E{ti}{hf}")
                if hf == 0:
                    P.op("scalar", (lambda hf=hf: lambda s: s.activation(
                        out=i32b[:, 0:1024], in_=p127[hf], func=AF.Identity, bias=Ceff))(),
                         reads=[f"p127{hf}", "Ceff"], writes=["i32b"], tag=f"B{h}1{ti}{hf}")
                    P.op("vector", (lambda hf=hf: lambda v: v.tensor_scalar(
                        out=r_bf[:, hf * 1024:(hf + 1) * 1024], in0=i32b[:, 0:1024],
                        scalar1=-128.0, scalar2=None, op0=ALU.add))(),
                         reads=["i32b"], writes=[f"r_bf{hf}"], tag=f"B{h}2{ti}{hf}")
                else:
                    P.op("vector", (lambda hf=hf: lambda v: v.tensor_scalar(
                        out=i32c[:, 0:1024], in0=p127[hf], scalar1=Ceff, scalar2=None, op0=ALU.add))(),
                         reads=[f"p127{hf}", "Ceff"], writes=["i32c"], tag=f"B{h}1{ti}{hf}")
                    P.op("scalar", (lambda hf=hf: lambda s: s.activation(
                        out=r_bf[:, hf * 1024:(hf + 1) * 1024],
                        in_=i32c[:, 0:1024], func=AF.Identity, bias=cm128))(),
                         reads=["i32c", "consts"], writes=[f"r_bf{hf}"], tag=f"B{h}2{ti}{hf}")
            def f_sums(v, ti=ti):
                P.seq(v.tensor_tensor(out=colS[:, ti:ti + 1], in0=colS[:, ti:ti + 1],
                                      in1=colSh[:, ti:ti + 1], op=ALU.add), v)
                return v.reciprocal(out=colSi[:, ti:ti + 1], in_=colS[:, ti:ti + 1])
            P.op("vector", f_sums, reads=["colSx", "colSh"], writes=["colS", "colSi"], tag=f"B{h}S{ti}")
            rb, rkey = rT[ti % 2], f"rT{ti % 2}"
            for hf in range(2):
                P.op("sync", (lambda rb=rb, hf=hf: lambda e: e.dma_start(
                    out=rb[:, hf * 1024:(hf + 1) * 1024].rearrange("p (j n) -> p j n", j=8),
                    in_=r_bf[:, hf * 1024:(hf + 1) * 1024].rearrange("p (j n) -> p j n", j=8),
                    transpose=True))(),
                     reads=[f"r_bf{hf}"], writes=[rkey], dma=True, tag=f"B{h}R{ti}{hf}")
            if ti > 0 and bcut >= 3:
                emit_av(ti - 1)
        if bcut >= 3:
            emit_av(NT - 1)
        if bcut < 4:
            continue
        def f_ysml(v):
            P.seq(v.tensor_scalar(out=colMy, in0=colMy, scalar1=EPS, scalar2=None, op0=ALU.max), v)
            P.seq(v.tensor_scalar(out=colVy, in0=colMy, scalar1=INV127, scalar2=None, op0=ALU.mult), v)
            P.seq(v.reciprocal(out=colUy, in_=colMy), v)
            return v.tensor_scalar(out=colUy, in0=colUy, scalar1=127.0, scalar2=None, op0=ALU.mult)
        P.op("vector", f_ysml, reads=["colMy"], writes=["colUy", "colVy"], tag=f"B{h}Z")
        P.transfer("i32c", "y_q")
        for ti in range(NT):
            P.op("vector", (lambda ti=ti: lambda v: v.tensor_scalar(
                out=i32b[:, 0:128], in0=y_raw[:, ti * 128:(ti + 1) * 128],
                scalar1=colUy[:, ti:ti + 1], scalar2=Ceff, op0=ALU.mult, op1=ALU.add))(),
                 reads=["y_raw", "colUy", "Ceff"], writes=["i32b"], tag=f"B{h}g{ti}")
            P.op("vector", (lambda ti=ti: lambda v: v.tensor_scalar(
                out=y_q[:, ti * 128:(ti + 1) * 128], in0=i32b[:, 0:128],
                scalar1=-128.0, scalar2=colVy[:, ti:ti + 1], op0=ALU.add, op1=ALU.mult))(),
                 reads=["i32b", "colVy"], writes=["y_q"], tag=f"B{h}h{ti}")
        P.op("sync", (lambda h=h: lambda e: e.dma_start(
            out=yqT.rearrange("p (a n) -> p a n", a=H * NT)[:, h * NT:(h + 1) * NT, :],
            in_=y_q[:, 0:T].rearrange("p (j n) -> p j n", j=NT), transpose=True))(),
             reads=["y_q"], writes=["yqT"], dma=True, tag=f"B{h}yT")

    P.transfer("xq_fm", "WpT")
    P.transfer("ternT", "xsC0")
    P.transfer("y_raw", "xsC1")
    P.transfer("qk_tm", "xsC1")
    for j in (range(NCk) if upto >= 6 else []):
        P.op("sync", (lambda j=j: lambda e: e.dma_start(
            out=WpT[:, j * C:(j + 1) * C],
            in_=tern_d["p"][:, j * 128:(j + 1) * 128], transpose=True))(),
             reads=["tern_dp"], writes=["WpT"], dma=True, tag=f"Cw{j}")
    for ti in (range(NT) if upto >= 6 else []):
        pst, pkey = (psA, "psA") if ti % 2 == 0 else (psB, "psB")
        xb, xkey = (xsC[0], "xsC0") if ti % 2 == 0 else (xsC[1], "xsC1")
        P.op("gpsimd", (lambda xb=xb, ti=ti: lambda g: g.dma_start(
            out=xb, in_=x_d[ti * 128:(ti + 1) * 128, :]))(),
             writes=[xkey], dma=True, tag=f"Cx{ti}")
        def f_pmm(t, pst=pst, ti=ti):
            for j in range(NCk):
                for half in range(4):
                    ins = t.matmul(pst[:, half * 512:(half + 1) * 512],
                                   yqT[:, (j * NT + ti) * 128:(j * NT + ti + 1) * 128],
                                   WpT[:, j * C + half * 512:j * C + (half + 1) * 512],
                                   start=(j == 0), stop=(j == NCk - 1))
            return ins
        P.op("tensor", f_pmm, reads=["yqT", "WpT"], writes=[pkey], tag=f"Cp{ti}")
        P.op("vector", (lambda pst=pst: lambda v: v.tensor_reduce(
            out=pM, in_=pst, axis=AX.X, op=ALU.max, apply_absolute_value=True))(),
             reads=[pkey], writes=["pM"], tag=f"Cm{ti}")
        def f_psml(v):
            P.seq(v.tensor_scalar(out=pM, in0=pM, scalar1=aW[:, 3:4], scalar2=EPS, op0=ALU.mult, op1=ALU.max), v)
            P.seq(v.tensor_scalar(out=pV, in0=pM, scalar1=INV127, scalar2=None, op0=ALU.mult), v)
            P.seq(v.reciprocal(out=pU, in_=pM), v)
            P.seq(v.tensor_scalar(out=pU, in0=pU, scalar1=aW[:, 3:4], scalar2=None, op0=ALU.mult), v)
            return v.tensor_scalar(out=pU, in0=pU, scalar1=127.0, scalar2=None, op0=ALU.mult)
        P.op("vector", f_psml, reads=["pM", "aW"], writes=["pU", "pV"], tag=f"Cs{ti}")
        for hf in range(2):
            P.op("vector", (lambda pst=pst, hf=hf: lambda v: v.tensor_scalar(
                out=i32b[:, 0:1024], in0=pst[:, hf * 1024:(hf + 1) * 1024],
                scalar1=pU, scalar2=Ceff, op0=ALU.mult, op1=ALU.add))(),
                 reads=[pkey, "pU", "Ceff"], writes=["i32b"], tag=f"C1{ti}{hf}")
            P.op("vector", (lambda hf=hf: lambda v: v.tensor_scalar(
                out=p127[hf], in0=i32b[:, 0:1024], scalar1=-128.0, scalar2=pV,
                op0=ALU.add, op1=ALU.mult))(),
                 reads=["i32b", "pV"], writes=[f"p127{hf}"], tag=f"C2{ti}{hf}")
            P.op("vector", (lambda xb=xb, hf=hf: lambda v: v.tensor_tensor(
                out=xb[:, hf * 1024:(hf + 1) * 1024], in0=xb[:, hf * 1024:(hf + 1) * 1024],
                in1=p127[hf], op=ALU.add))(),
                 reads=[f"p127{hf}", xkey], writes=[xkey], tag=f"C3{ti}{hf}")
        P.op("gpsimd", (lambda xb=xb, ti=ti: lambda g: g.dma_start(
            out=s2_d[ti * 128:(ti + 1) * 128, :], in_=xb))(),
             reads=[xkey], writes=["s2_d"], dma=True, tag=f"Cq{ti}")

    P.transfer("yqT", "xn2_fm")
    P.transfer("WpT", "gu")
    P.transfer("xsC0", "wgT0")
    P.transfer("xsC0", "wgT1")
    P.transfer("xsC0", "wuT1")
    P.transfer("qkvraw", "wuT0")
    P.transfer("y_q", "wuT0")
    P.transfer("q_fm", "s2t0")
    P.transfer("k_fm", "s2t0")
    P.transfer("r_bf0", "s2t1")
    P.transfer("r_bf1", "s2t1")
    P.transfer("rT0", "s2t1")
    P.transfer("p1270", "wdTcol")
    P.transfer("p1271", "wdTcol")
    P.transfer("wrow0", "wdTcol")
    P.transfer("wrow1", "wdTcol")
    P.transfer("i32b", "hilo")
    P.transfer("xsC1", "sil0")
    P.transfer("v_tm", "sil1")
    P.transfer("xsC1", "outasm")
    P.transfer("v_tm", "outasm")
    xn2b = hilo
    for ti in (range(NT) if upto >= 7 else []):
        sb_, skey = (s2t[0], "s2t0") if ti % 2 == 0 else (s2t[1], "s2t1")
        P.op("gpsimd", (lambda sb_=sb_, ti=ti: lambda g: g.dma_start(
            out=sb_, in_=s2_d[ti * 128:(ti + 1) * 128, :]))(),
             reads=["s2_d"], writes=[skey], dma=True, tag=f"Dl{ti}")
        P.op("vector", (lambda sb_=sb_: lambda v: v.tensor_reduce(
            out=stat[:, 0:1], in_=sb_, axis=AX.X, op=ALU.add))(),
             reads=[skey], writes=["stat0"], tag=f"Ds{ti}")
        P.op("scalar", (lambda sb_=sb_: lambda s: s.activation(
            out=sqscr, in_=sb_, func=AF.Square, accum_out=stat[:, 1:2]))(),
             reads=[skey], writes=["wdTcol", "stat1"], tag=f"Dq{ti}")
        def f_stats(v):
            P.seq(v.tensor_scalar(out=stat[:, 2:3], in0=stat[:, 0:1], scalar1=1.0 / C, scalar2=None, op0=ALU.mult), v)
            P.seq(v.tensor_tensor(out=stat[:, 3:4], in0=stat[:, 2:3], in1=stat[:, 2:3], op=ALU.mult), v)
            P.seq(v.tensor_scalar(out=stat[:, 4:5], in0=stat[:, 1:2], scalar1=1.0 / C, scalar2=None, op0=ALU.mult), v)
            return v.tensor_tensor(out=stat[:, 4:5], in0=stat[:, 4:5], in1=stat[:, 3:4], op=ALU.subtract)
        P.op("vector", f_stats, reads=["stat0", "stat1"], writes=["statm"], tag=f"Dv{ti}")
        def f_rstd(s):
            P.seq(s.activation(out=stat[:, 5:6], in_=stat[:, 4:5], func=AF.Ln, bias=c_eps), s)
            return s.activation(out=stat[:, 6:7], in_=stat[:, 5:6], func=AF.Exp, scale=-0.5)
        P.op("scalar", f_rstd, reads=["statm", "consts"], writes=["rstd"], tag=f"Dr{ti}")
        def f_nmr(v):
            P.seq(v.tensor_scalar(out=stat[:, 7:8], in0=stat[:, 2:3], scalar1=-1.0, scalar2=None, op0=ALU.mult), v)
            return v.tensor_tensor(out=stat[:, 7:8], in0=stat[:, 7:8], in1=stat[:, 6:7], op=ALU.mult)
        P.op("vector", f_nmr, reads=["statm", "rstd"], writes=["nmr"], tag=f"Dn{ti}")
        P.op("scalar", (lambda sb_=sb_: lambda s: s.activation(
            out=xn2b, in_=sb_, func=AF.Identity, scale=stat[:, 6:7], bias=stat[:, 7:8]))(),
             reads=[skey, "rstd", "nmr"], writes=["hilo"], tag=f"Da{ti}")
        P.op("sync", (lambda ti=ti: lambda e: e.dma_start(
            out=xn2_fm.rearrange("p (j t) -> p j t", j=NCk)[:, :, ti * 128:(ti + 1) * 128],
            in_=xn2b.rearrange("p (j n) -> p j n", j=NCk), transpose=True))(),
             reads=["hilo"], writes=["xn2_fm"], dma=True, tag=f"Dt{ti}")

    for gi, jlist in (enumerate(GROUPS) if upto >= 8 else []):
        for jj, j in enumerate(jlist):
            gslot = jj % 2
            gview, gk = wgT[gslot], f"wgT{gslot}"
            uview, uk = wuT[gslot], f"wuT{gslot}"
            P.op("sync", (lambda j=j, gview=gview: lambda e: e.dma_start(
                out=gview.rearrange("p (a n) -> p a n", a=NCk),
                in_=wgb_d[j * 128:(j + 1) * 128, :].rearrange("p (a n) -> p a n", a=NCk),
                transpose=True))(),
                 reads=["wbg"], writes=[gk], dma=True, tag=f"DG{j}")
            P.op("sync", (lambda j=j, uview=uview: lambda e: e.dma_start(
                out=uview.rearrange("p (a n) -> p a n", a=NCk),
                in_=wub_d[j * 128:(j + 1) * 128, :].rearrange("p (a n) -> p a n", a=NCk),
                transpose=True))(),
                 reads=["wbu"], writes=[uk], dma=True, tag=f"DU{j}")
            def f_gmm(t, gview=gview):
                for a in range(NCk):
                    for qq in range(4):
                        ins = t.matmul(psA[:, qq * 512:(qq + 1) * 512],
                                       gview[:, a * 128:(a + 1) * 128],
                                       xn2_fm[:, a * T + qq * 512:a * T + (qq + 1) * 512],
                                       start=(a == 0), stop=(a == NCk - 1))
                return ins
            P.op("tensor", f_gmm, reads=[gk, "xn2_fm"], writes=["psA"], tag=f"Dg{j}")
            def f_umm(t, uview=uview):
                for a in range(NCk):
                    for qq in range(4):
                        ins = t.matmul(psB[:, qq * 512:(qq + 1) * 512],
                                       uview[:, a * 128:(a + 1) * 128],
                                       xn2_fm[:, a * T + qq * 512:a * T + (qq + 1) * 512],
                                       start=(a == 0), stop=(a == NCk - 1))
                return ins
            P.op("tensor", f_umm, reads=[uk, "xn2_fm"], writes=["psB"], tag=f"Dh{j}")
            guv = gu_ar[:, jj * T:(jj + 1) * T]
            for hf in range(2):
                P.op("scalar", (lambda hf=hf, j=j: lambda s: s.activation(
                    out=sil[hf], in_=psA[:, hf * 1024:(hf + 1) * 1024],
                    func=AF.Silu, scale=s_gate[:, j:j + 1]))(),
                     reads=["psA", "s_gate"], writes=[f"sil{hf}", "outasm"], tag=f"DA{j}{hf}")
                P.op("vector", (lambda hf=hf, j=j, guv=guv: lambda v: v.scalar_tensor_tensor(
                    out=guv[:, hf * 1024:(hf + 1) * 1024], in0=psB[:, hf * 1024:(hf + 1) * 1024],
                    scalar=s_up[:, j:j + 1], in1=sil[hf], op0=ALU.mult, op1=ALU.mult))(),
                     reads=["psB", "s_up", f"sil{hf}"], writes=["gu"], tag=f"DC{j}{hf}")
        last = gi == len(GROUPS) - 1
        jbase = jlist[0]
        nj = len(jlist)
        if not last:
            for i in range(NCk):
                P.op("sync", (lambda i=i, jbase=jbase, nj=nj: lambda e: e.dma_start(
                    out=wdT_col[:, 0:nj * 128].rearrange("p (m n) -> p m n", m=nj),
                    in_=wdT_d[jbase * 128:(jbase + nj) * 128, i * 128:(i + 1) * 128]
                        .rearrange("(m p) n -> p m n", p=128)))(),
                     reads=["wdT"], writes=["wdTcol"], dma=True, tag=f"Dc{gi}{i}")
                pst, pkey = (psA, "psA") if i % 2 == 0 else (psB, "psB")
                def f_dmm(t, pst=pst, nj=nj):
                    for jj in range(nj):
                        for qq in range(4):
                            ins = t.matmul(pst[:, qq * 512:(qq + 1) * 512],
                                           wdT_col[:, jj * 128:(jj + 1) * 128],
                                           gu_ar[:, jj * T + qq * 512:jj * T + (qq + 1) * 512],
                                           start=(jj == 0), stop=(jj == nj - 1))
                    return ins
                P.op("tensor", f_dmm, reads=["wdTcol", "gu"], writes=[pkey], tag=f"Dd{gi}{i}")
                if gi == 0:
                    P.op("scalar", (lambda pst=pst: lambda s: s.copy(out=outasm, in_=pst))(),
                         reads=[pkey], writes=["outasm", "sil0", "sil1"], tag=f"De{gi}{i}")
                else:
                    P.op("gpsimd", (lambda i=i: lambda g: g.dma_start(
                        out=s2t[0], in_=part_d[i * 128:(i + 1) * 128, :]))(),
                         reads=["part_d", "s2t0"], writes=["s2t0"], dma=True, tag=f"Dpl{gi}{i}")
                    def f_acc(v, pst=pst):
                        return v.tensor_tensor(out=outasm, in0=pst, in1=s2t[0], op=ALU.add)
                    P.op("vector", f_acc, reads=[pkey, "s2t0"], writes=["outasm", "sil0", "sil1"], tag=f"De{gi}{i}")
                P.op("gpsimd", (lambda i=i: lambda g: g.dma_start(
                    out=part_d[i * 128:(i + 1) * 128, :], in_=outasm))(),
                     reads=["outasm"], writes=["part_d"], dma=True, tag=f"Dw{gi}{i}")
        else:
            P.transfer("xn2_fm", "hiT")
            P.transfer("xn2_fm", "loT")
            for th in range(2):
                for i in range(NCk):
                    P.op("sync", (lambda i=i, jbase=jbase, nj=nj: lambda e: e.dma_start(
                        out=wdT_col[:, 0:nj * 128].rearrange("p (m n) -> p m n", m=nj),
                        in_=wdT_d[jbase * 128:(jbase + nj) * 128, i * 128:(i + 1) * 128]
                            .rearrange("(m p) n -> p m n", p=128)))(),
                         reads=["wdT"], writes=["wdTcol"], dma=True, tag=f"Dc2{th}{i}")
                    pst, pkey = (psA, "psA") if i % 2 == 0 else (psB, "psB")
                    def f_dmm(t, pst=pst, nj=nj, th=th):
                        for jj in range(nj):
                            for qq in range(2):
                                ins = t.matmul(pst[:, qq * 512:(qq + 1) * 512],
                                               wdT_col[:, jj * 128:(jj + 1) * 128],
                                               gu_ar[:, jj * T + th * 1024 + qq * 512:
                                                     jj * T + th * 1024 + (qq + 1) * 512],
                                               start=(jj == 0), stop=(jj == nj - 1))
                        return ins
                    P.op("tensor", f_dmm, reads=["wdTcol", "gu"], writes=[pkey], tag=f"DD{th}{i}")
                    P.op("gpsimd", (lambda i=i, th=th: lambda g: g.dma_start(
                        out=s2t[0][:, 0:1024],
                        in_=part_d[i * 128:(i + 1) * 128, th * 1024:(th + 1) * 1024]))(),
                         reads=["part_d", "s2t0"], writes=["s2t0"], dma=True, tag=f"DP{th}{i}")
                    def f_fin(v, pst=pst, i=i):
                        P.seq(v.tensor_tensor(out=outasm[:, 0:1024], in0=pst[:, 0:1024],
                                              in1=s2t[0][:, 0:1024], op=ALU.add), v)
                        P.seq(v.tensor_scalar(out=outasm[:, 0:1024], in0=outasm[:, 0:1024],
                                              scalar1=s_down[:, i:i + 1], scalar2=None, op0=ALU.mult), v)
                        P.seq(v.tensor_copy(out=hilo[:, 0:1024], in_=outasm[:, 0:1024]), v)
                        return v.tensor_tensor(out=hilo[:, 1024:2048], in0=outasm[:, 0:1024],
                                               in1=hilo[:, 0:1024], op=ALU.subtract)
                    P.op("vector", f_fin, reads=[pkey, "s2t0", "s_down"],
                         writes=["outasm", "hilo", "sil0", "sil1"], tag=f"DF{th}{i}")
                    P.op("sync", (lambda i=i: lambda e: e.dma_start(
                        out=hiT[:, 0:16384].rearrange("p (tt ii n) -> p tt ii n", tt=8, ii=NCk)[:, :, i, :],
                        in_=hilo[:, 0:1024].rearrange("p (tt n) -> p tt n", tt=8), transpose=True))(),
                         reads=["hilo"], writes=["hiT"], dma=True, tag=f"DH{th}{i}")
                    P.op("sync", (lambda i=i: lambda e: e.dma_start(
                        out=hiT[:, LOT:LOT + 16384].rearrange("p (tt ii n) -> p tt ii n", tt=8, ii=NCk)[:, :, i, :],
                        in_=hilo[:, 1024:2048].rearrange("p (tt n) -> p tt n", tt=8), transpose=True))(),
                         reads=["hilo"], writes=["loT"], dma=True, tag=f"DL{th}{i}")
                for tt in range(8):
                    ti = th * 8 + tt
                    P.op("gpsimd", (lambda ti=ti: lambda g: g.dma_start(
                        out=s2t[1], in_=s2_d[ti * 128:(ti + 1) * 128, :]))(),
                         reads=["s2_d", "s2t1"], writes=["s2t1"], dma=True, tag=f"DS{ti}")
                    def f_asm(v, tt=tt):
                        P.seq(v.tensor_tensor(out=outasm, in0=hiT[:, tt * 2048:(tt + 1) * 2048],
                                              in1=hiT[:, LOT + tt * 2048:LOT + (tt + 1) * 2048], op=ALU.add), v)
                        return v.tensor_tensor(out=outasm, in0=outasm, in1=s2t[1], op=ALU.add)
                    P.op("vector", f_asm, reads=["hiT", "loT", "s2t1"],
                         writes=["outasm", "sil0", "sil1"], tag=f"DO{ti}")
                    P.op("gpsimd", (lambda ti=ti: lambda g: g.dma_start(
                        out=out_d[ti * 128:(ti + 1) * 128, :], in_=outasm))(),
                         reads=["outasm"], writes=[f"out{ti}", "outasm"], dma=True, tag=f"DW{ti}")

    if upto >= 8:
        P.op("gpsimd", lambda g: g.engine_nop(),
             reads=[f"out{ti}" for ti in range(NT)], tag="fin")
    else:
        P.op("gpsimd", lambda g: g.dma_start(out=out_d[0:128, :], in_=x_d[0:128, :]),
             writes=["outdbg"], dma=True, tag="dbgout")
        P.op("gpsimd", lambda g: g.engine_nop(), reads=["outdbg"], tag="fin")

    n_ops, n_inc = P.emit()
    return nc, n_ops, n_inc


_NC_CACHE = {}


def kernel(L0_attn_x, L0_attn_Wq, L0_attn_Wk, L0_attn_Wv, L0_attn_Wp,
           L0_ffn_wq_gate, L0_ffn_s_gate, L0_ffn_wq_up, L0_ffn_s_up,
           L0_ffn_wq_down, L0_ffn_s_down):
    x = np.ascontiguousarray(np.asarray(L0_attn_x, dtype=np.float32))
    wq = np.ascontiguousarray(np.asarray(L0_attn_Wq, dtype=np.float32))
    wk = np.ascontiguousarray(np.asarray(L0_attn_Wk, dtype=np.float32))
    wv = np.ascontiguousarray(np.asarray(L0_attn_Wv, dtype=np.float32))
    wp = np.ascontiguousarray(np.asarray(L0_attn_Wp, dtype=np.float32))
    wg = np.ascontiguousarray(np.asarray(L0_ffn_wq_gate, dtype=np.int32))
    wu = np.ascontiguousarray(np.asarray(L0_ffn_wq_up, dtype=np.int32))
    wd = np.ascontiguousarray(np.asarray(L0_ffn_wq_down, dtype=np.int32))
    sg = np.ascontiguousarray(np.asarray(L0_ffn_s_gate, dtype=np.float32).reshape(NH, 128).T)
    su = np.ascontiguousarray(np.asarray(L0_ffn_s_up, dtype=np.float32).reshape(NH, 128).T)
    sd = np.ascontiguousarray(np.asarray(L0_ffn_s_down, dtype=np.float32).reshape(NCk, 128).T)

    if "nc" not in _NC_CACHE:
        _NC_CACHE["nc"] = build()[0]
    nc = _NC_CACHE["nc"]
    in_maps = [{"x": x[b], "wq": wq, "wk": wk, "wv": wv, "wp": wp,
                "wg": wg, "wu": wu, "wd": wd, "sg": sg, "su": su, "sd": sd}
               for b in range(8)]
    res = run_bass_kernel_spmd(nc, in_maps, core_ids=list(range(8)))
    return np.stack([r["out"] for r in res.results], axis=0)


# version 06
# speedup vs baseline: 1.0349x; 1.0303x over previous
import numpy as np
import concourse.bass as bass
import concourse.mybir as mybir
from concourse.bass_utils import run_bass_kernel_spmd
import ml_dtypes

F32 = mybir.dt.float32
BF16 = mybir.dt.bfloat16
I32 = mybir.dt.int32
AF = mybir.ActivationFunctionType
ALU = mybir.AluOpType
AX = mybir.AxisListType
BF = ml_dtypes.bfloat16

T = 2048
C = 2048
H = 16
D = 128
HID = 5632
NT = T // 128
NCk = C // 128
NH = HID // 128
EPS = 1e-5
LN127 = float(np.log(127.0))
INV127 = 1.0 / 127.0
NELW = float(C * C)
GROUPS = [list(range(0, 15)), list(range(15, 30)), list(range(30, 44))]


class Prog:

    ENGINES = ("sync", "gpsimd", "tensor", "vector", "scalar")

    def __init__(self, nc):
        self.nc = nc
        self.ops = []
        self.last_write = {}
        self.readers = {}

    def op(self, eng, fn, reads=(), writes=(), dma=False, tag=""):
        deps = set()
        for k in reads:
            w = self.last_write.get(k)
            if w is not None:
                deps.add(w)
        for k in writes:
            w = self.last_write.get(k)
            if w is not None:
                deps.add(w)
            for r in self.readers.get(k, ()):
                deps.add(r)
        i = len(self.ops)
        if dma:
            assert writes, f"dma op {tag} needs a write resource for its semaphore"
        self.ops.append(dict(eng=eng, fn=fn, deps=sorted(deps), dma=dma, tag=tag,
                             w0=(writes[0] if writes else None)))
        for k in reads:
            self.readers.setdefault(k, []).append(i)
        for k in writes:
            self.last_write[k] = i
            self.readers[k] = []
        return i

    def transfer(self, old, new):
        if old in self.last_write:
            lw = self.last_write[old]
            if new not in self.last_write or self.last_write[new] < lw:
                self.last_write[new] = lw
        self.readers.setdefault(new, [])
        self.readers[new] += list(self.readers.get(old, ()))

    def seq(self, instr, handle):
        e = self._cur
        instr.then_inc(self._aux[e], 1)
        self._auxcnt[e] += 1
        handle.wait_ge(self._aux[e], self._auxcnt[e])

    def emit(self):
        nc = self.nc
        self._aux = {e: nc.semaphore(f"aux_{e}").__enter__() for e in self.ENGINES}
        self._auxcnt = {e: 0 for e in self.ENGINES}
        self._cur = None
        need_inc = set()
        for i, o in enumerate(self.ops):
            if o["dma"]:
                need_inc.add(i)
            for d in o["deps"]:
                need_inc.add(d)
        semid = {}
        res2sem = {}
        for i, o in enumerate(self.ops):
            if o["dma"]:
                r = o["w0"]
                if r not in res2sem:
                    res2sem[r] = f"dma_{len(res2sem)}_{r}"
                semid[i] = res2sem[r]
            else:
                semid[i] = f"eng_{o['eng']}"
        counts = {}
        running = {}
        for i, o in enumerate(self.ops):
            if i in need_inc:
                s = semid[i]
                running[s] = running.get(s, 0) + (16 if o["dma"] else 1)
                counts[i] = running[s]
        sems = {s: nc.semaphore(s).__enter__() for s in
                set(semid.values())}
        ops = self.ops

        def engine_body(ename, handle):
            waited = {}
            self._cur = ename
            for i, o in enumerate(ops):
                if o["eng"] != ename:
                    continue
                self._cur = ename
                for d in o["deps"]:
                    s = semid[d]
                    if d in counts and counts[d] > waited.get(s, 0):
                        handle.wait_ge(sems[s], counts[d])
                        waited[s] = counts[d]
                instr = o["fn"](handle)
                if i in need_inc:
                    assert instr is not None, f"op {i} {o['tag']} returned None but needs inc"
                    instr.then_inc(sems[semid[i]], 16 if o["dma"] else 1)

        with nc.Block() as block:
            @block.sync
            def _(e):
                engine_body("sync", e)

            @block.gpsimd
            def _(e):
                engine_body("gpsimd", e)

            @block.tensor
            def _(e):
                engine_body("tensor", e)

            @block.vector
            def _(e):
                engine_body("vector", e)

            @block.scalar
            def _(e):
                engine_body("scalar", e)
        return len(self.ops), len(need_inc)


def build(upto=99, bheads=16, bcut=9, bq=9):
    nc = bass.Bass("TRN2", target_bir_lowering=False, debug=False)

    x_d = nc.dram_tensor("x", [T, C], F32, kind="ExternalInput").ap()
    w_attn = {k: nc.dram_tensor(f"w{k}", [C, C], F32, kind="ExternalInput").ap()
              for k in ("q", "k", "v", "p")}
    wg_d = nc.dram_tensor("wg", [HID, C], I32, kind="ExternalInput").ap()
    wu_d = nc.dram_tensor("wu", [HID, C], I32, kind="ExternalInput").ap()
    wd_d = nc.dram_tensor("wd", [C, HID], I32, kind="ExternalInput").ap()
    sg_d = nc.dram_tensor("sg", [128, NH], F32, kind="ExternalInput").ap()
    su_d = nc.dram_tensor("su", [128, NH], F32, kind="ExternalInput").ap()
    sd_d = nc.dram_tensor("sd", [128, NCk], F32, kind="ExternalInput").ap()
    out_d = nc.dram_tensor("out", [T, C], F32, kind="ExternalOutput").ap()

    tern_d = {k: nc.dram_tensor(f"tern{k}", [C, C], BF16, kind="Internal").ap()
              for k in ("q", "k", "v", "p")}
    wgb_d = nc.dram_tensor("wgb", [HID, C], BF16, kind="Internal").ap()
    wub_d = nc.dram_tensor("wub", [HID, C], BF16, kind="Internal").ap()
    wdb_d = nc.dram_tensor("wdb", [C, HID], BF16, kind="Internal").ap()
    wdT_d = nc.dram_tensor("wdT", [HID, C], BF16, kind="Internal").ap()
    s2_d = nc.dram_tensor("s2", [T, C], F32, kind="Internal").ap()
    part_d = nc.dram_tensor("part", [C, T], F32, kind="Internal").ap()

    ar1 = nc.sbuf_tensor("ar1", [128, 16384], F32).__enter__()[:]
    ar2 = nc.sbuf_tensor("ar2", [128, 16384], F32).__enter__()[:]
    ar3 = nc.sbuf_tensor("ar3", [128, 14848], F32).__enter__()[:]
    sm = nc.sbuf_tensor("smalls", [128, 512], F32).__enter__()[:]
    ones_r = nc.sbuf_tensor("ones_r", [1, 128], F32).__enter__()[:]
    ones_c = nc.sbuf_tensor("ones_c", [128, 1], F32).__enter__()[:]
    rowscr = nc.sbuf_tensor("rowscr", [1, 16], F32).__enter__()[:]

    def carve(ar, cur, n):
        return ar[:, cur:cur + n], cur + n

    xq_fm = ar1.bitcast(BF16)
    WpT = ar1.bitcast(BF16)
    gu_ar = ar1.bitcast(BF16)

    cur = 0
    xs0, cur = carve(ar2, cur, 2048)
    xs1, cur = carve(ar2, cur, 2048)
    sqv, cur = carve(ar2, cur, 2048)
    xnv, cur = carve(ar2, cur, 2048)
    rxq = ar2[:, 8192:9216].bitcast(BF16)
    yqT = ar2.bitcast(BF16)
    xn2_fm = ar2.bitcast(BF16)
    hiT = ar2.bitcast(BF16)
    LOT = 16384

    cur = 0
    a_ternT, cur = carve(ar3, cur, 3072)
    a_qkvraw, cur = carve(ar3, cur, 1536)
    a_i32, cur = carve(ar3, cur, 1024)
    a_p127a, cur = carve(ar3, cur, 1024)
    a_p127b, cur = carve(ar3, cur, 1024)
    a_qk_tm, cur = carve(ar3, cur, 1024)
    a_v_tm, cur = carve(ar3, cur, 1024)
    a_q_fm, cur = carve(ar3, cur, 1024)
    a_k_fm, cur = carve(ar3, cur, 1024)
    a_r, cur = carve(ar3, cur, 1024)
    a_rT0, cur = carve(ar3, cur, 1024)
    a_rT1, cur = carve(ar3, cur, 1024)
    assert cur <= 14848, cur

    off = {}
    o = 0
    for nm, n in (("ternT", 3072), ("qkvraw", 1536), ("i32", 1024), ("p127a", 1024),
                  ("p127b", 1024), ("qk_tm", 1024), ("v_tm", 1024), ("q_fm", 1024),
                  ("k_fm", 1024), ("r", 1024), ("rT0", 1024), ("rT1", 1024)):
        off[nm] = o
        o += n
    ternT = a_ternT.bitcast(BF16)
    qkvraw = a_qkvraw.bitcast(BF16)
    y_q = a_qkvraw.bitcast(BF16)
    i32b = a_i32.bitcast(I32)
    p127 = [a_p127a, a_p127b]
    q_tm = a_qk_tm.bitcast(BF16)
    k_tm = a_qk_tm.bitcast(BF16)
    y_raw = a_qk_tm.bitcast(BF16)
    v_tm = a_v_tm.bitcast(BF16)
    q_fm = a_q_fm.bitcast(BF16)
    k_fm = a_k_fm.bitcast(BF16)
    r_bf = a_r.bitcast(BF16)
    rT = [a_rT0.bitcast(BF16), a_rT1.bitcast(BF16)]
    wrow = [a_p127a, a_p127b, a_qk_tm, a_v_tm]
    w2i = [a_i32.bitcast(I32)[:, 0:1024], a_v_tm.bitcast(I32)[:, 0:1024]]
    w2o = [a_q_fm.bitcast(BF16)[:, 0:2048], a_k_fm.bitcast(BF16)[:, 0:2048]]
    xsC = [a_ternT[:, 0:2048], ar3[:, off["qk_tm"]:off["qk_tm"] + 2048]]
    wgT = [a_ternT.bitcast(BF16)[:, 0:2048], a_ternT.bitcast(BF16)[:, 2048:4096]]
    wuT = [a_qkvraw.bitcast(BF16)[:, 0:2048],
           a_ternT.bitcast(BF16)[:, 4096:6144]]
    wdT_col2 = [ar3[:, off["p127a"]:off["p127a"] + 2048].bitcast(BF16),
                ar3[:, off["p127a"]:off["p127a"] + 2048].bitcast(BF16)]
    wdT_col = wdT_col2[0]
    sqscr = ar3[:, off["p127a"]:off["p127a"] + 2048]
    sil = [ar3[:, off["qk_tm"]:off["qk_tm"] + 1024],
           ar3[:, off["v_tm"]:off["v_tm"] + 1024]]
    s2t = [ar3[:, off["q_fm"]:off["q_fm"] + 2048],
           ar3[:, off["r"]:off["r"] + 2048]]
    outasm = ar3[:, off["qk_tm"]:off["qk_tm"] + 2048]
    hilo = a_i32.bitcast(BF16)

    scur = 0

    def scarve(n):
        nonlocal scur
        ap = sm[:, scur:scur + n]
        scur += n
        return ap

    consts = scarve(5)
    c_eps, c_ln127, c_one, c_tiny, c_half = (consts[:, i:i + 1] for i in range(5))
    Ceff = scarve(1)
    colM = scarve(3 * NT)
    colU = scarve(3 * NT)
    colV = scarve(3 * NT)
    colR = scarve(NT)
    colG = scarve(NT)
    colB = scarve(NT)
    colS = scarve(NT)
    colSh = scarve(NT)
    colSi = scarve(NT)
    colMy = scarve(NT)
    colUy = scarve(NT)
    colVy = scarve(NT)
    invq_xq = scarve(NT)
    aW = scarve(4)
    sWi = scarve(4)
    asum = scarve(NCk)
    stat = scarve(8)
    tmp = scarve(16)
    s_gate = scarve(NH)
    s_up = scarve(NH)
    s_down = scarve(NCk)
    pM = scarve(1)
    pU = scarve(1)
    pV = scarve(1)
    assert scur <= 512, scur

    psA = nc.psum_tensor("psA", [128, 2048], F32).__enter__()[:]
    psB = nc.psum_tensor("psB", [128, 2048], F32).__enter__()[:]

    P = Prog(nc)

    def f_init(v):
        P.seq(v.memset(c_eps, EPS), v)
        P.seq(v.memset(c_ln127, LN127), v)
        P.seq(v.memset(c_one, 1.0), v)
        P.seq(v.memset(c_tiny, float(2.0 ** -30)), v)
        P.seq(v.memset(c_half, 0.75), v)
        P.seq(v.memset(ones_c, 1.0), v)
        P.seq(v.memset(ones_r, 1.0), v)
        P.seq(v.tensor_copy(out=i32b[:, 0:1], in_=c_half), v)
        return v.tensor_scalar(out=Ceff, in0=i32b[:, 0:1], scalar1=-0.5, scalar2=128.5,
                               op0=ALU.mult, op1=ALU.add)
    P.op("vector", f_init, writes=["consts", "Ceff", "i32b", "ones"], tag="init")
    P.op("gpsimd", lambda g: g.dma_start(out=s_gate, in_=sg_d), writes=["s_gate"], dma=True)
    P.op("gpsimd", lambda g: g.dma_start(out=s_up, in_=su_d), writes=["s_up"], dma=True)
    P.op("gpsimd", lambda g: g.dma_start(out=s_down, in_=sd_d), writes=["s_down"], dma=True)

    for wi, wkey in (enumerate(("q", "k", "v", "p")) if upto >= 1 else []):
        wd = w_attn[wkey]
        for ch in range(NCk):
            for hf in range(2):
                bi = (ch * 2 + hf) % 4
                buf, bkey = wrow[bi], f"wrow{bi}"
                src = wd[ch * 128:(ch + 1) * 128, hf * 1024:(hf + 1) * 1024]
                P.op("sync", (lambda b, s: lambda e: e.dma_start(out=b, in_=s))(buf, src),
                     writes=[bkey], dma=True, tag=f"w0l{wkey}{ch}{hf}")
                dst = asum[:, ch:ch + 1] if hf == 0 else tmp[:, 4:5]
                dkey = "asum" if hf == 0 else "tmp45"
                P.op("scalar", (lambda b, d: lambda e: e.activation(
                    out=sqv[:, 0:1024], in_=b, func=AF.Abs, accum_out=d))(buf, dst),
                     reads=[bkey], writes=["sqv", dkey], tag=f"w0a{wkey}{ch}{hf}")
                if hf == 1:
                    P.op("vector", (lambda ci: lambda v: v.tensor_tensor(
                        out=asum[:, ci:ci + 1], in0=asum[:, ci:ci + 1], in1=tmp[:, 4:5], op=ALU.add))(ch),
                         reads=["tmp45"], writes=["asum"], tag=f"w0+{wkey}{ch}")
        P.op("vector", lambda v: v.tensor_reduce(out=tmp[:, 5:6], in_=asum[:, 0:NCk],
                                                 axis=AX.X, op=ALU.add),
             reads=["asum"], writes=["tmp5"], tag=f"w0r{wkey}")
        P.op("tensor", lambda t: t.matmul(psA[0:1, 0:1], ones_c, tmp[:, 5:6], start=True, stop=True),
             reads=["tmp5", "ones"], writes=["psA"], tag=f"w0m{wkey}")
        def f_sml(v):
            P.seq(v.tensor_scalar(out=rowscr[0:1, 0:1], in0=psA[0:1, 0:1], scalar1=1.0 / NELW,
                                  scalar2=EPS, op0=ALU.mult, op1=ALU.max), v)
            return v.reciprocal(out=rowscr[0:1, 1:2], in_=rowscr[0:1, 0:1])
        P.op("vector", f_sml, reads=["psA"], writes=["rowscr"], tag=f"w0s{wkey}")
        P.op("tensor", lambda t: t.matmul(psB[:, 0:2], ones_r, rowscr[0:1, 0:2], start=True, stop=True),
             reads=["rowscr", "ones"], writes=["psB"], tag=f"w0b{wkey}")
        def f_cp(v, wi=wi):
            v.tensor_copy(out=aW[:, wi:wi + 1], in_=psB[:, 0:1])
            return v.tensor_copy(out=sWi[:, wi:wi + 1], in_=psB[:, 1:2])
        P.op("vector", f_cp, reads=["psB"], writes=["aW", "sWi"], tag=f"w0c{wkey}")

    ternbuf = rxq
    for wi, wkey in (enumerate(("q", "k", "v", "p")) if upto >= 2 else []):
        wd = w_attn[wkey]
        for ch in range(NCk):
            for hf in range(2):
                bi = (ch * 2 + hf) % 4
                buf, bkey = wrow[bi], f"wrow{bi}"
                src = wd[ch * 128:(ch + 1) * 128, hf * 1024:(hf + 1) * 1024]
                P.op("sync", (lambda b, s: lambda e: e.dma_start(out=b, in_=s))(buf, src),
                     writes=[bkey], dma=True, tag=f"w1l{wkey}{ch}{hf}")
                def f_tern(v, b=buf, wi=wi, hf=hf):
                    P.seq(v.tensor_scalar(out=i32b[:, 0:1024], in0=b, scalar1=sWi[:, wi:wi + 1],
                                          scalar2=Ceff, op0=ALU.mult, op1=ALU.add), v)
                    P.seq(v.tensor_scalar(out=i32b[:, 0:1024], in0=i32b[:, 0:1024], scalar1=127.0,
                                          scalar2=129.0, op0=ALU.max, op1=ALU.min), v)
                    return v.tensor_scalar(out=ternbuf[:, hf * 1024:(hf + 1) * 1024],
                                           in0=i32b[:, 0:1024], scalar1=-128.0, scalar2=None, op0=ALU.add)
                P.op("vector", f_tern, reads=[bkey, "sWi", "Ceff"], writes=["i32b", "ternbuf"],
                     tag=f"w1t{wkey}{ch}{hf}")
            P.op("gpsimd", (lambda wkey, ch: lambda g: g.dma_start(
                out=tern_d[wkey][ch * 128:(ch + 1) * 128, :], in_=ternbuf))(wkey, ch),
                 reads=["ternbuf"], writes=[f"tern_d{wkey}"], dma=True, tag=f"w1s{wkey}{ch}")

    P.transfer("i32b", "w2i0")
    P.transfer("wrow3", "w2i1")
    for nm, src_d, dst_d in ((("g", wg_d, wgb_d), ("u", wu_d, wub_d)) if upto >= 3 else []):
        for ch in range(NH):
            bi = ch % 2
            ibuf, ikey = w2i[bi], ("w2i0" if bi == 0 else "w2i1")
            obuf, okey = w2o[bi], ("w2o0" if bi == 0 else "w2o1")
            for hf in range(2):
                P.op("sync", (lambda ib, s=src_d, ch=ch, hf=hf: lambda e: e.dma_start(
                    out=ib, in_=s[ch * 128:(ch + 1) * 128, hf * 1024:(hf + 1) * 1024]))(ibuf),
                     writes=[ikey], dma=True, tag=f"w2l{nm}{ch}{hf}")
                P.op("vector", (lambda ib, ob, hf=hf: lambda v: v.tensor_scalar(
                    out=ob[:, hf * 1024:(hf + 1) * 1024], in0=ib, scalar1=-128.0,
                    scalar2=None, op0=ALU.add))(ibuf, obuf),
                     reads=[ikey], writes=[okey], tag=f"w2c{nm}{ch}{hf}")
            P.op("gpsimd", (lambda ob, d=dst_d, ch=ch: lambda g: g.dma_start(
                out=d[ch * 128:(ch + 1) * 128, :], in_=ob))(obuf),
                 reads=[okey], writes=[f"wb{nm}"], dma=True, tag=f"w2s{nm}{ch}")
    P.transfer("w2i0", "i32b")
    dn_o = xq_fm[:, 0:HID]
    dn_t = xq_fm[:, 8192:8192 + NH * 128]
    for ch in (range(NCk) if upto >= 3 else []):
        for sub in range(6):
            w = 1024 if sub < 5 else 512
            P.op("sync", (lambda s=wd_d, ch=ch, sub=sub, w=w: lambda e: e.dma_start(
                out=i32b[:, 0:w], in_=s[ch * 128:(ch + 1) * 128, sub * 1024:sub * 1024 + w]))(),
                 writes=["i32b"], dma=True, tag=f"w2ld{ch}{sub}")
            P.op("vector", (lambda sub=sub, w=w: lambda v: v.tensor_scalar(
                out=dn_o[:, sub * 1024:sub * 1024 + w], in0=i32b[:, 0:w],
                scalar1=-128.0, scalar2=None, op0=ALU.add))(),
                 reads=["i32b"], writes=["dn_o"], tag=f"w2cd{ch}{sub}")
        P.op("gpsimd", (lambda ch=ch: lambda g: g.dma_start(
            out=wdb_d[ch * 128:(ch + 1) * 128, :], in_=dn_o))(),
             reads=["dn_o"], writes=["wdb"], dma=True, tag=f"w2sd{ch}")
        P.op("sync", (lambda ch=ch: lambda e: e.dma_start(
            out=dn_t.rearrange("p (m n) -> p m n", m=NH),
            in_=dn_o.rearrange("p (m n) -> p m n", m=NH), transpose=True))(),
             reads=["dn_o"], writes=["dn_t"], dma=True, tag=f"w2tp{ch}")
        P.op("gpsimd", (lambda ch=ch: lambda g: g.dma_start(
            out=wdT_d[:, ch * 128:(ch + 1) * 128].rearrange("(m p) n -> p m n", p=128),
            in_=dn_t.rearrange("p (m n) -> p m n", m=NH)))(),
             reads=["dn_t"], writes=["wdT"], dma=True, tag=f"w2st{ch}")

    P.transfer("dn_o", "xq_fm")
    P.transfer("dn_t", "xq_fm")
    P.transfer("ternbuf", "rxq")
    for ti in (range(NT) if upto >= 4 else []):
        xb, xkey = (xs0, "xs0") if ti % 2 == 0 else (xs1, "xs1")
        P.op("gpsimd", (lambda xb=xb, ti=ti: lambda g: g.dma_start(
            out=xb, in_=x_d[ti * 128:(ti + 1) * 128, :]))(),
             writes=[xkey], dma=True, tag=f"Ax{ti}")
        P.op("vector", (lambda xb=xb: lambda v: v.tensor_reduce(
            out=stat[:, 0:1], in_=xb, axis=AX.X, op=ALU.add))(),
             reads=[xkey], writes=["stat0"], tag=f"As{ti}")
        P.op("scalar", (lambda xb=xb: lambda s: s.activation(
            out=sqv, in_=xb, func=AF.Square, accum_out=stat[:, 1:2]))(),
             reads=[xkey], writes=["sqv", "stat1"], tag=f"Aq{ti}")
        def f_stats(v):
            P.seq(v.tensor_scalar(out=stat[:, 2:3], in0=stat[:, 0:1], scalar1=1.0 / C, scalar2=None, op0=ALU.mult), v)
            P.seq(v.tensor_tensor(out=stat[:, 3:4], in0=stat[:, 2:3], in1=stat[:, 2:3], op=ALU.mult), v)
            P.seq(v.tensor_scalar(out=stat[:, 4:5], in0=stat[:, 1:2], scalar1=1.0 / C, scalar2=None, op0=ALU.mult), v)
            return v.tensor_tensor(out=stat[:, 4:5], in0=stat[:, 4:5], in1=stat[:, 3:4], op=ALU.subtract)
        P.op("vector", f_stats, reads=["stat0", "stat1"], writes=["statm"], tag=f"Av{ti}")
        def f_rstd(s):
            P.seq(s.activation(out=stat[:, 5:6], in_=stat[:, 4:5], func=AF.Ln, bias=c_eps), s)
            return s.activation(out=stat[:, 6:7], in_=stat[:, 5:6], func=AF.Exp, scale=-0.5)
        P.op("scalar", f_rstd, reads=["statm", "consts"], writes=["rstd"], tag=f"Ar{ti}")
        def f_nmr(v):
            P.seq(v.tensor_scalar(out=stat[:, 7:8], in0=stat[:, 2:3], scalar1=-1.0, scalar2=None, op0=ALU.mult), v)
            return v.tensor_tensor(out=stat[:, 7:8], in0=stat[:, 7:8], in1=stat[:, 6:7], op=ALU.mult)
        P.op("vector", f_nmr, reads=["statm", "rstd"], writes=["nmr"], tag=f"An{ti}")
        P.op("scalar", (lambda xb=xb: lambda s: s.activation(
            out=xnv, in_=xb, func=AF.Identity, scale=stat[:, 6:7], bias=stat[:, 7:8]))(),
             reads=[xkey, "rstd", "nmr"], writes=["xnv"], tag=f"Aa{ti}")
        P.op("vector", lambda v: v.tensor_reduce(out=tmp[:, 6:7], in_=xnv, axis=AX.X,
                                                 op=ALU.max, apply_absolute_value=True),
             reads=["xnv"], writes=["qm"], tag=f"Am{ti}")
        def f_qs(v, ti=ti):
            P.seq(v.tensor_scalar(out=tmp[:, 7:8], in0=tmp[:, 6:7], scalar1=EPS, scalar2=None, op0=ALU.max), v)
            P.seq(v.tensor_scalar(out=invq_xq[:, ti:ti + 1], in0=tmp[:, 7:8], scalar1=INV127, scalar2=None, op0=ALU.mult), v)
            P.seq(v.reciprocal(out=tmp[:, 8:9], in_=tmp[:, 7:8]), v)
            return v.tensor_scalar(out=tmp[:, 9:10], in0=tmp[:, 8:9], scalar1=127.0, scalar2=None, op0=ALU.mult)
        P.op("vector", f_qs, reads=["qm"], writes=["invq_xq", "qs"], tag=f"Aw{ti}")
        for hf in range(2):
            P.op("vector", (lambda hf=hf: lambda v: v.tensor_scalar(
                out=i32b[:, 0:1024], in0=xnv[:, hf * 1024:(hf + 1) * 1024],
                scalar1=tmp[:, 9:10], scalar2=Ceff, op0=ALU.mult, op1=ALU.add))(),
                 reads=["xnv", "qs", "Ceff"], writes=["i32b"], tag=f"A1{ti}{hf}")
            P.op("vector", (lambda hf=hf: lambda v: v.tensor_scalar(
                out=rxq[:, hf * 1024:(hf + 1) * 1024], in0=i32b[:, 0:1024],
                scalar1=-128.0, scalar2=None, op0=ALU.add))(),
                 reads=["i32b"], writes=["rxq"], tag=f"A2{ti}{hf}")
        P.op("sync", (lambda ti=ti: lambda e: e.dma_start(
            out=xq_fm.rearrange("p (j t) -> p j t", j=NCk)[:, :, ti * 128:(ti + 1) * 128],
            in_=rxq.rearrange("p (j n) -> p j n", j=NCk), transpose=True))(),
             reads=["rxq"], writes=["xq_fm"], dma=True, tag=f"At{ti}")

    for nm in ("sqv", "xnv", "xs0", "xs1", "rxq"):
        P.transfer(nm, "yqT")
    P.transfer("wrow0", "p1270")
    P.transfer("wrow1", "p1271")
    P.transfer("wrow2", "qk_tm")
    P.transfer("wrow3", "v_tm")
    P.transfer("w2i1", "v_tm")
    P.transfer("w2o0", "q_fm")
    P.transfer("w2o1", "k_fm")

    for h in (range(min(H, bheads)) if upto >= 5 else []):
        P.transfer("y_q", "qkvraw")
        P.transfer("y_raw", "qk_tm")
        P.transfer("qk_tm", "y_raw")
        for wi, wkey in enumerate(("q", "k", "v")):
            P.op("sync", (lambda wkey=wkey, wi=wi, h=h: lambda e: e.dma_start(
                out=ternT.rearrange("p (j w) -> p j w", j=NCk)[:, :, wi * 128:(wi + 1) * 128],
                in_=tern_d[wkey][h * 128:(h + 1) * 128, :].rearrange("p (j n) -> p j n", j=NCk),
                transpose=True))(),
                 reads=[f"tern_d{wkey}"], writes=["ternT"], dma=True, tag=f"B{h}T{wkey}")
        colM3 = colM.rearrange("p (w t) -> p w t", w=3)
        for half in range(2):
            for tq in (range(8) if bq >= 1 else []):
                ti = half * 8 + tq
                pst, pkey = (psA, "psA") if ti % 2 == 0 else (psB, "psB")
                def f_mm(t, pst=pst, ti=ti):
                    for j in range(NCk):
                        ins = t.matmul(pst[:, 0:384],
                                       xq_fm[:, j * T + ti * 128:j * T + (ti + 1) * 128],
                                       ternT[:, j * 384:(j + 1) * 384],
                                       start=(j == 0), stop=(j == NCk - 1))
                    return ins
                P.op("tensor", f_mm, reads=["xq_fm", "ternT"], writes=[pkey], tag=f"B{h}q{ti}")
                if bq < 2:
                    continue
                P.op("scalar", (lambda pst=pst, tq=tq: lambda s: s.copy(
                    out=qkvraw[:, tq * 384:(tq + 1) * 384], in_=pst[:, 0:384]))(),
                     reads=[pkey], writes=["qkvraw"], tag=f"B{h}e{ti}")
                if bq < 3:
                    continue
                def f_red3(v, tq=tq, ti=ti):
                    for wi in range(3):
                        ins = v.tensor_reduce(
                            out=colM[:, wi * NT + ti:wi * NT + ti + 1],
                            in_=qkvraw[:, tq * 384 + wi * 128:tq * 384 + (wi + 1) * 128],
                            axis=AX.X, op=ALU.max, apply_absolute_value=True)
                        if wi < 2:
                            P.seq(ins, v)
                    return ins
                P.op("vector", f_red3, reads=["qkvraw"], writes=["colM"], tag=f"B{h}m{ti}")
            if bcut < 2:
                continue
            def f_usml(v, half=half):
                iq = invq_xq[:, half * 8:half * 8 + 8]
                for wi in range(3):
                    sl = slice(wi * NT + half * 8, wi * NT + half * 8 + 8)
                    P.seq(v.tensor_tensor(out=colU[:, sl], in0=colM[:, sl], in1=iq, op=ALU.mult), v)
                    P.seq(v.tensor_scalar(out=colU[:, sl], in0=colU[:, sl], scalar1=aW[:, wi:wi + 1],
                                          scalar2=EPS, op0=ALU.mult, op1=ALU.max), v)
                    P.seq(v.tensor_scalar(out=colV[:, sl], in0=colU[:, sl], scalar1=INV127, scalar2=None, op0=ALU.mult), v)
                    P.seq(v.reciprocal(out=colU[:, sl], in_=colU[:, sl]), v)
                    P.seq(v.tensor_tensor(out=colU[:, sl], in0=colU[:, sl], in1=iq, op=ALU.mult), v)
                    P.seq(v.tensor_scalar(out=colU[:, sl], in0=colU[:, sl], scalar1=aW[:, wi:wi + 1],
                                          scalar2=None, op0=ALU.mult), v)
                    P.seq(v.tensor_scalar(out=colU[:, sl], in0=colU[:, sl], scalar1=127.0,
                                          scalar2=None, op0=ALU.mult), v)
                return v.tensor_scalar(out=colG[:, half * 8:half * 8 + 8],
                                       in0=colV[:, half * 8:half * 8 + 8],
                                       scalar1=float(1.0 / np.sqrt(128.0)), scalar2=None, op0=ALU.mult)
            P.op("vector", f_usml, reads=["colM", "invq_xq", "aW"],
                 writes=["colU", "colV", "colG"], tag=f"B{h}u{half}")
            for wi, (dst, dkey) in enumerate(((q_tm, "qk_tm"), (k_tm, "qk_tm"), (v_tm, "v_tm"))):
                for tq in range(8):
                    ti = half * 8 + tq
                    P.op("vector", (lambda wi=wi, ti=ti, tq=tq: lambda v: v.tensor_scalar(
                        out=i32b[:, 0:128], in0=qkvraw[:, tq * 384 + wi * 128:tq * 384 + (wi + 1) * 128],
                        scalar1=colU[:, wi * NT + ti:wi * NT + ti + 1], scalar2=Ceff,
                        op0=ALU.mult, op1=ALU.add))(),
                         reads=["qkvraw", "colU", "Ceff"], writes=["i32b"], tag=f"B{h}c{wi}{ti}")
                    if wi == 0:
                        fn = (lambda dst=dst, ti=ti: lambda v: v.tensor_scalar(
                            out=dst[:, ti * 128:(ti + 1) * 128], in0=i32b[:, 0:128],
                            scalar1=-128.0, scalar2=None, op0=ALU.add))()
                    else:
                        fn = (lambda dst=dst, ti=ti, wi=wi: lambda v: v.tensor_scalar(
                            out=dst[:, ti * 128:(ti + 1) * 128], in0=i32b[:, 0:128],
                            scalar1=-128.0, scalar2=colV[:, wi * NT + ti:wi * NT + ti + 1],
                            op0=ALU.add, op1=ALU.mult))()
                    P.op("vector", fn, reads=["i32b", "colV"], writes=[dkey], tag=f"B{h}d{wi}{ti}")
                if wi == 0:
                    P.op("sync", (lambda half=half: lambda e: e.dma_start(
                        out=q_fm.rearrange("p (j n) -> p j n", j=NT)[:, half * 8:(half + 1) * 8, :],
                        in_=q_tm[:, half * 1024:(half + 1) * 1024].rearrange("p (j n) -> p j n", j=8),
                        transpose=True))(),
                         reads=["qk_tm"], writes=["q_fm"], dma=True, tag=f"B{h}tq{half}")
                elif wi == 1:
                    P.op("sync", (lambda half=half: lambda e: e.dma_start(
                        out=k_fm.rearrange("p (j n) -> p j n", j=NT)[:, half * 8:(half + 1) * 8, :],
                        in_=k_tm[:, half * 1024:(half + 1) * 1024].rearrange("p (j n) -> p j n", j=8),
                        transpose=True))(),
                         reads=["qk_tm"], writes=["k_fm"], dma=True, tag=f"B{h}tk{half}")
        def emit_av(ti):
            pst, pkey = (psA, "psA") if ti % 2 == 0 else (psB, "psB")
            rb, rkey = rT[ti % 2], f"rT{ti % 2}"
            def f_av(t, pst=pst, rb=rb):
                for j in range(NT):
                    ins = t.matmul(pst[:, 0:128], rb[:, j * 128:(j + 1) * 128],
                                   v_tm[:, j * 128:(j + 1) * 128],
                                   start=(j == 0), stop=(j == NT - 1))
                return ins
            P.op("tensor", f_av, reads=[rkey, "v_tm"], writes=[pkey], tag=f"B{h}A{ti}")
            P.op("scalar", (lambda pst=pst, ti=ti: lambda s: s.activation(
                out=y_raw[:, ti * 128:(ti + 1) * 128], in_=pst[:, 0:128],
                func=AF.Identity, scale=colSi[:, ti:ti + 1]))(),
                 reads=[pkey, "colSi"], writes=["y_raw"], tag=f"B{h}Y{ti}")
            P.op("vector", (lambda ti=ti: lambda v: v.tensor_reduce(
                out=colMy[:, ti:ti + 1], in_=y_raw[:, ti * 128:(ti + 1) * 128],
                axis=AX.X, op=ALU.max, apply_absolute_value=True))(),
                 reads=["y_raw"], writes=["colMy"], tag=f"B{h}M{ti}")

        for ti in (range(NT) if bcut >= 3 else []):
            pst, pkey = (psA, "psA") if ti % 2 == 0 else (psB, "psB")
            def f_smm(t, pst=pst, ti=ti):
                for half in range(4):
                    ins = t.matmul(pst[:, half * 512:(half + 1) * 512],
                                   q_fm[:, ti * 128:(ti + 1) * 128],
                                   k_fm[:, half * 512:(half + 1) * 512],
                                   start=True, stop=True)
                return ins
            P.op("tensor", f_smm, reads=["q_fm", "k_fm"], writes=[pkey], tag=f"B{h}s{ti}")
            P.op("vector", (lambda pst=pst, ti=ti: lambda v: v.tensor_reduce(
                out=colR[:, ti:ti + 1], in_=pst, axis=AX.X, op=ALU.max))(),
                 reads=[pkey], writes=["colR"], tag=f"B{h}x{ti}")
            def f_bias(v, ti=ti):
                P.seq(v.tensor_tensor(out=colB[:, ti:ti + 1], in0=colR[:, ti:ti + 1],
                                      in1=colG[:, ti:ti + 1], op=ALU.mult), v)
                return v.tensor_scalar(out=colB[:, ti:ti + 1], in0=colB[:, ti:ti + 1],
                                       scalar1=-1.0, scalar2=LN127, op0=ALU.mult, op1=ALU.add)
            P.op("vector", f_bias, reads=["colR", "colG"], writes=["colB"], tag=f"B{h}b{ti}")
            for hf in range(2):
                P.op("scalar", (lambda pst=pst, ti=ti, hf=hf: lambda s: s.activation(
                    out=p127[hf], in_=pst[:, hf * 1024:(hf + 1) * 1024], func=AF.Exp,
                    scale=colG[:, ti:ti + 1], bias=colB[:, ti:ti + 1],
                    accum_out=(colSh if hf == 0 else colS)[:, ti:ti + 1]))(),
                     reads=[pkey, "colG", "colB"],
                     writes=[f"p127{hf}", "colSh" if hf == 0 else "colSx"], tag=f"B{h}E{ti}{hf}")
                P.op("vector", (lambda hf=hf: lambda v: v.tensor_scalar(
                    out=i32b[:, 0:1024], in0=p127[hf], scalar1=Ceff, scalar2=None, op0=ALU.add))(),
                     reads=[f"p127{hf}", "Ceff"], writes=["i32b"], tag=f"B{h}1{ti}{hf}")
                P.op("vector", (lambda hf=hf: lambda v: v.tensor_scalar(
                    out=r_bf[:, hf * 1024:(hf + 1) * 1024], in0=i32b[:, 0:1024],
                    scalar1=-128.0, scalar2=None, op0=ALU.add))(),
                     reads=["i32b"], writes=["r_bf"], tag=f"B{h}2{ti}{hf}")
            def f_sums(v, ti=ti):
                P.seq(v.tensor_tensor(out=colS[:, ti:ti + 1], in0=colS[:, ti:ti + 1],
                                      in1=colSh[:, ti:ti + 1], op=ALU.add), v)
                return v.reciprocal(out=colSi[:, ti:ti + 1], in_=colS[:, ti:ti + 1])
            P.op("vector", f_sums, reads=["colSx", "colSh"], writes=["colS", "colSi"], tag=f"B{h}S{ti}")
            rb, rkey = rT[ti % 2], f"rT{ti % 2}"
            P.op("sync", (lambda rb=rb: lambda e: e.dma_start(
                out=rb.rearrange("p (j n) -> p j n", j=NT),
                in_=r_bf.rearrange("p (j n) -> p j n", j=NT), transpose=True))(),
                 reads=["r_bf"], writes=[rkey], dma=True, tag=f"B{h}R{ti}")
            if ti > 0 and bcut >= 3:
                emit_av(ti - 1)
        if bcut >= 3:
            emit_av(NT - 1)
        if bcut < 4:
            continue
        def f_ysml(v):
            P.seq(v.tensor_scalar(out=colMy, in0=colMy, scalar1=EPS, scalar2=None, op0=ALU.max), v)
            P.seq(v.tensor_scalar(out=colVy, in0=colMy, scalar1=INV127, scalar2=None, op0=ALU.mult), v)
            P.seq(v.reciprocal(out=colUy, in_=colMy), v)
            return v.tensor_scalar(out=colUy, in0=colUy, scalar1=127.0, scalar2=None, op0=ALU.mult)
        P.op("vector", f_ysml, reads=["colMy"], writes=["colUy", "colVy"], tag=f"B{h}Z")
        for ti in range(NT):
            P.op("vector", (lambda ti=ti: lambda v: v.tensor_scalar(
                out=i32b[:, 0:128], in0=y_raw[:, ti * 128:(ti + 1) * 128],
                scalar1=colUy[:, ti:ti + 1], scalar2=Ceff, op0=ALU.mult, op1=ALU.add))(),
                 reads=["y_raw", "colUy", "Ceff"], writes=["i32b"], tag=f"B{h}g{ti}")
            P.op("vector", (lambda ti=ti: lambda v: v.tensor_scalar(
                out=y_q[:, ti * 128:(ti + 1) * 128], in0=i32b[:, 0:128],
                scalar1=-128.0, scalar2=colVy[:, ti:ti + 1], op0=ALU.add, op1=ALU.mult))(),
                 reads=["i32b", "colVy"], writes=["y_q"], tag=f"B{h}h{ti}")
        P.op("sync", (lambda h=h: lambda e: e.dma_start(
            out=yqT.rearrange("p (a n) -> p a n", a=H * NT)[:, h * NT:(h + 1) * NT, :],
            in_=y_q[:, 0:T].rearrange("p (j n) -> p j n", j=NT), transpose=True))(),
             reads=["y_q"], writes=["yqT"], dma=True, tag=f"B{h}yT")

    P.transfer("xq_fm", "WpT")
    P.transfer("ternT", "xsC0")
    P.transfer("y_raw", "xsC1")
    P.transfer("qk_tm", "xsC1")
    for j in (range(NCk) if upto >= 6 else []):
        P.op("sync", (lambda j=j: lambda e: e.dma_start(
            out=WpT[:, j * C:(j + 1) * C],
            in_=tern_d["p"][:, j * 128:(j + 1) * 128], transpose=True))(),
             reads=["tern_dp"], writes=["WpT"], dma=True, tag=f"Cw{j}")
    for ti in (range(NT) if upto >= 6 else []):
        pst, pkey = (psA, "psA") if ti % 2 == 0 else (psB, "psB")
        xb, xkey = (xsC[0], "xsC0") if ti % 2 == 0 else (xsC[1], "xsC1")
        P.op("gpsimd", (lambda xb=xb, ti=ti: lambda g: g.dma_start(
            out=xb, in_=x_d[ti * 128:(ti + 1) * 128, :]))(),
             writes=[xkey], dma=True, tag=f"Cx{ti}")
        def f_pmm(t, pst=pst, ti=ti):
            for j in range(NCk):
                for half in range(4):
                    ins = t.matmul(pst[:, half * 512:(half + 1) * 512],
                                   yqT[:, (j * NT + ti) * 128:(j * NT + ti + 1) * 128],
                                   WpT[:, j * C + half * 512:j * C + (half + 1) * 512],
                                   start=(j == 0), stop=(j == NCk - 1))
            return ins
        P.op("tensor", f_pmm, reads=["yqT", "WpT"], writes=[pkey], tag=f"Cp{ti}")
        P.op("vector", (lambda pst=pst: lambda v: v.tensor_reduce(
            out=pM, in_=pst, axis=AX.X, op=ALU.max, apply_absolute_value=True))(),
             reads=[pkey], writes=["pM"], tag=f"Cm{ti}")
        def f_psml(v):
            P.seq(v.tensor_scalar(out=pM, in0=pM, scalar1=aW[:, 3:4], scalar2=EPS, op0=ALU.mult, op1=ALU.max), v)
            P.seq(v.tensor_scalar(out=pV, in0=pM, scalar1=INV127, scalar2=None, op0=ALU.mult), v)
            P.seq(v.reciprocal(out=pU, in_=pM), v)
            P.seq(v.tensor_scalar(out=pU, in0=pU, scalar1=aW[:, 3:4], scalar2=None, op0=ALU.mult), v)
            return v.tensor_scalar(out=pU, in0=pU, scalar1=127.0, scalar2=None, op0=ALU.mult)
        P.op("vector", f_psml, reads=["pM", "aW"], writes=["pU", "pV"], tag=f"Cs{ti}")
        for hf in range(2):
            P.op("vector", (lambda pst=pst, hf=hf: lambda v: v.tensor_scalar(
                out=i32b[:, 0:1024], in0=pst[:, hf * 1024:(hf + 1) * 1024],
                scalar1=pU, scalar2=Ceff, op0=ALU.mult, op1=ALU.add))(),
                 reads=[pkey, "pU", "Ceff"], writes=["i32b"], tag=f"C1{ti}{hf}")
            P.op("vector", (lambda hf=hf: lambda v: v.tensor_scalar(
                out=p127[hf], in0=i32b[:, 0:1024], scalar1=-128.0, scalar2=pV,
                op0=ALU.add, op1=ALU.mult))(),
                 reads=["i32b", "pV"], writes=[f"p127{hf}"], tag=f"C2{ti}{hf}")
            P.op("vector", (lambda xb=xb, hf=hf: lambda v: v.tensor_tensor(
                out=xb[:, hf * 1024:(hf + 1) * 1024], in0=xb[:, hf * 1024:(hf + 1) * 1024],
                in1=p127[hf], op=ALU.add))(),
                 reads=[f"p127{hf}", xkey], writes=[xkey], tag=f"C3{ti}{hf}")
        P.op("gpsimd", (lambda xb=xb, ti=ti: lambda g: g.dma_start(
            out=s2_d[ti * 128:(ti + 1) * 128, :], in_=xb))(),
             reads=[xkey], writes=["s2_d"], dma=True, tag=f"Cq{ti}")

    P.transfer("yqT", "xn2_fm")
    P.transfer("WpT", "gu")
    P.transfer("xsC0", "wgT0")
    P.transfer("xsC0", "wgT1")
    P.transfer("xsC0", "wuT1")
    P.transfer("qkvraw", "wuT0")
    P.transfer("y_q", "wuT0")
    P.transfer("q_fm", "s2t0")
    P.transfer("k_fm", "s2t0")
    P.transfer("r_bf", "s2t1")
    P.transfer("rT0", "s2t1")
    P.transfer("p1270", "wdTcol0")
    P.transfer("p1271", "wdTcol0")
    P.transfer("wrow0", "wdTcol0")
    P.transfer("wrow1", "wdTcol0")
    P.transfer("xsC1", "wdTcol1")
    P.transfer("sil0", "wdTcol1")
    P.transfer("sil1", "wdTcol1")
    P.transfer("outasm", "wdTcol1")
    P.transfer("i32b", "hilo")
    P.transfer("xsC1", "sil0")
    P.transfer("v_tm", "sil1")
    P.transfer("xsC1", "outasm")
    P.transfer("v_tm", "outasm")
    xn2b = hilo
    for ti in (range(NT) if upto >= 7 else []):
        sb_, skey = (s2t[0], "s2t0") if ti % 2 == 0 else (s2t[1], "s2t1")
        P.op("gpsimd", (lambda sb_=sb_, ti=ti: lambda g: g.dma_start(
            out=sb_, in_=s2_d[ti * 128:(ti + 1) * 128, :]))(),
             reads=["s2_d"], writes=[skey], dma=True, tag=f"Dl{ti}")
        P.op("vector", (lambda sb_=sb_: lambda v: v.tensor_reduce(
            out=stat[:, 0:1], in_=sb_, axis=AX.X, op=ALU.add))(),
             reads=[skey], writes=["stat0"], tag=f"Ds{ti}")
        P.op("scalar", (lambda sb_=sb_: lambda s: s.activation(
            out=sqscr, in_=sb_, func=AF.Square, accum_out=stat[:, 1:2]))(),
             reads=[skey], writes=["wdTcol0", "stat1"], tag=f"Dq{ti}")
        def f_stats(v):
            P.seq(v.tensor_scalar(out=stat[:, 2:3], in0=stat[:, 0:1], scalar1=1.0 / C, scalar2=None, op0=ALU.mult), v)
            P.seq(v.tensor_tensor(out=stat[:, 3:4], in0=stat[:, 2:3], in1=stat[:, 2:3], op=ALU.mult), v)
            P.seq(v.tensor_scalar(out=stat[:, 4:5], in0=stat[:, 1:2], scalar1=1.0 / C, scalar2=None, op0=ALU.mult), v)
            return v.tensor_tensor(out=stat[:, 4:5], in0=stat[:, 4:5], in1=stat[:, 3:4], op=ALU.subtract)
        P.op("vector", f_stats, reads=["stat0", "stat1"], writes=["statm"], tag=f"Dv{ti}")
        def f_rstd(s):
            P.seq(s.activation(out=stat[:, 5:6], in_=stat[:, 4:5], func=AF.Ln, bias=c_eps), s)
            return s.activation(out=stat[:, 6:7], in_=stat[:, 5:6], func=AF.Exp, scale=-0.5)
        P.op("scalar", f_rstd, reads=["statm", "consts"], writes=["rstd"], tag=f"Dr{ti}")
        def f_nmr(v):
            P.seq(v.tensor_scalar(out=stat[:, 7:8], in0=stat[:, 2:3], scalar1=-1.0, scalar2=None, op0=ALU.mult), v)
            return v.tensor_tensor(out=stat[:, 7:8], in0=stat[:, 7:8], in1=stat[:, 6:7], op=ALU.mult)
        P.op("vector", f_nmr, reads=["statm", "rstd"], writes=["nmr"], tag=f"Dn{ti}")
        P.op("scalar", (lambda sb_=sb_: lambda s: s.activation(
            out=xn2b, in_=sb_, func=AF.Identity, scale=stat[:, 6:7], bias=stat[:, 7:8]))(),
             reads=[skey, "rstd", "nmr"], writes=["hilo"], tag=f"Da{ti}")
        P.op("sync", (lambda ti=ti: lambda e: e.dma_start(
            out=xn2_fm.rearrange("p (j t) -> p j t", j=NCk)[:, :, ti * 128:(ti + 1) * 128],
            in_=xn2b.rearrange("p (j n) -> p j n", j=NCk), transpose=True))(),
             reads=["hilo"], writes=["xn2_fm"], dma=True, tag=f"Dt{ti}")

    for gi, jlist in (enumerate(GROUPS) if upto >= 8 else []):
        for jj, j in enumerate(jlist):
            gslot = jj % 2
            gview, gk = wgT[gslot], f"wgT{gslot}"
            uview, uk = wuT[gslot], f"wuT{gslot}"
            P.op("sync", (lambda j=j, gview=gview: lambda e: e.dma_start(
                out=gview.rearrange("p (a n) -> p a n", a=NCk),
                in_=wgb_d[j * 128:(j + 1) * 128, :].rearrange("p (a n) -> p a n", a=NCk),
                transpose=True))(),
                 reads=["wbg"], writes=[gk], dma=True, tag=f"DG{j}")
            P.op("sync", (lambda j=j, uview=uview: lambda e: e.dma_start(
                out=uview.rearrange("p (a n) -> p a n", a=NCk),
                in_=wub_d[j * 128:(j + 1) * 128, :].rearrange("p (a n) -> p a n", a=NCk),
                transpose=True))(),
                 reads=["wbu"], writes=[uk], dma=True, tag=f"DU{j}")
            def f_gmm(t, gview=gview):
                for a in range(NCk):
                    for qq in range(4):
                        ins = t.matmul(psA[:, qq * 512:(qq + 1) * 512],
                                       gview[:, a * 128:(a + 1) * 128],
                                       xn2_fm[:, a * T + qq * 512:a * T + (qq + 1) * 512],
                                       start=(a == 0), stop=(a == NCk - 1))
                return ins
            P.op("tensor", f_gmm, reads=[gk, "xn2_fm"], writes=["psA"], tag=f"Dg{j}")
            def f_umm(t, uview=uview):
                for a in range(NCk):
                    for qq in range(4):
                        ins = t.matmul(psB[:, qq * 512:(qq + 1) * 512],
                                       uview[:, a * 128:(a + 1) * 128],
                                       xn2_fm[:, a * T + qq * 512:a * T + (qq + 1) * 512],
                                       start=(a == 0), stop=(a == NCk - 1))
                return ins
            P.op("tensor", f_umm, reads=[uk, "xn2_fm"], writes=["psB"], tag=f"Dh{j}")
            guv = gu_ar[:, jj * T:(jj + 1) * T]
            for hf in range(2):
                P.op("scalar", (lambda hf=hf, j=j: lambda s: s.activation(
                    out=sil[hf], in_=psA[:, hf * 1024:(hf + 1) * 1024],
                    func=AF.Silu, scale=s_gate[:, j:j + 1]))(),
                     reads=["psA", "s_gate"], writes=[f"sil{hf}", "outasm", "wdTcol1"], tag=f"DA{j}{hf}")
                P.op("vector", (lambda hf=hf, j=j, guv=guv: lambda v: v.scalar_tensor_tensor(
                    out=guv[:, hf * 1024:(hf + 1) * 1024], in0=psB[:, hf * 1024:(hf + 1) * 1024],
                    scalar=s_up[:, j:j + 1], in1=sil[hf], op0=ALU.mult, op1=ALU.mult))(),
                     reads=["psB", "s_up", f"sil{hf}"], writes=["gu"], tag=f"DC{j}{hf}")
        last = gi == len(GROUPS) - 1
        jbase = jlist[0]
        nj = len(jlist)
        if not last:
            for i in range(NCk):
                wc, wck = wdT_col2[0], "wdTcol0"
                P.op("sync", (lambda i=i, jbase=jbase, nj=nj, wc=wc: lambda e: e.dma_start(
                    out=wc[:, 0:nj * 128].rearrange("p (m n) -> p m n", m=nj),
                    in_=wdT_d[jbase * 128:(jbase + nj) * 128, i * 128:(i + 1) * 128]
                        .rearrange("(m p) n -> p m n", p=128)))(),
                     reads=["wdT"], writes=[wck], dma=True, tag=f"Dc{gi}{i}")
                pst, pkey = (psA, "psA") if i % 2 == 0 else (psB, "psB")
                def f_dmm(t, pst=pst, nj=nj, wc=wc):
                    for jj in range(nj):
                        for qq in range(4):
                            ins = t.matmul(pst[:, qq * 512:(qq + 1) * 512],
                                           wc[:, jj * 128:(jj + 1) * 128],
                                           gu_ar[:, jj * T + qq * 512:jj * T + (qq + 1) * 512],
                                           start=(jj == 0), stop=(jj == nj - 1))
                    return ins
                P.op("tensor", f_dmm, reads=[wck, "gu"], writes=[pkey], tag=f"Dd{gi}{i}")
                if gi == 0:
                    P.op("scalar", (lambda pst=pst: lambda s: s.copy(out=outasm, in_=pst))(),
                         reads=[pkey], writes=["outasm", "sil0", "sil1", "wdTcol1"], tag=f"De{gi}{i}")
                else:
                    P.op("gpsimd", (lambda i=i: lambda g: g.dma_start(
                        out=s2t[0], in_=part_d[i * 128:(i + 1) * 128, :]))(),
                         reads=["part_d", "s2t0"], writes=["s2t0"], dma=True, tag=f"Dpl{gi}{i}")
                    def f_acc(v, pst=pst):
                        return v.tensor_tensor(out=outasm, in0=pst, in1=s2t[0], op=ALU.add)
                    P.op("vector", f_acc, reads=[pkey, "s2t0"], writes=["outasm", "sil0", "sil1", "wdTcol1"], tag=f"De{gi}{i}")
                P.op("gpsimd", (lambda i=i: lambda g: g.dma_start(
                    out=part_d[i * 128:(i + 1) * 128, :], in_=outasm))(),
                     reads=["outasm"], writes=["part_d"], dma=True, tag=f"Dw{gi}{i}")
        else:
            P.transfer("xn2_fm", "hiT")
            P.transfer("xn2_fm", "loT")
            for th in range(2):
                for i in range(NCk):
                    wc, wck = wdT_col2[0], "wdTcol0"
                    P.op("sync", (lambda i=i, jbase=jbase, nj=nj, wc=wc: lambda e: e.dma_start(
                        out=wc[:, 0:nj * 128].rearrange("p (m n) -> p m n", m=nj),
                        in_=wdT_d[jbase * 128:(jbase + nj) * 128, i * 128:(i + 1) * 128]
                            .rearrange("(m p) n -> p m n", p=128)))(),
                         reads=["wdT"], writes=[wck], dma=True, tag=f"Dc2{th}{i}")
                    pst, pkey = (psA, "psA") if i % 2 == 0 else (psB, "psB")
                    def f_dmm(t, pst=pst, nj=nj, th=th, wc=wc):
                        for jj in range(nj):
                            for qq in range(2):
                                ins = t.matmul(pst[:, qq * 512:(qq + 1) * 512],
                                               wc[:, jj * 128:(jj + 1) * 128],
                                               gu_ar[:, jj * T + th * 1024 + qq * 512:
                                                     jj * T + th * 1024 + (qq + 1) * 512],
                                               start=(jj == 0), stop=(jj == nj - 1))
                        return ins
                    P.op("tensor", f_dmm, reads=[wck, "gu"], writes=[pkey], tag=f"DD{th}{i}")
                    P.op("gpsimd", (lambda i=i, th=th: lambda g: g.dma_start(
                        out=s2t[0][:, 0:1024],
                        in_=part_d[i * 128:(i + 1) * 128, th * 1024:(th + 1) * 1024]))(),
                         reads=["part_d", "s2t0"], writes=["s2t0"], dma=True, tag=f"DP{th}{i}")
                    def f_fin(v, pst=pst, i=i):
                        P.seq(v.tensor_tensor(out=outasm[:, 0:1024], in0=pst[:, 0:1024],
                                              in1=s2t[0][:, 0:1024], op=ALU.add), v)
                        P.seq(v.tensor_scalar(out=outasm[:, 0:1024], in0=outasm[:, 0:1024],
                                              scalar1=s_down[:, i:i + 1], scalar2=None, op0=ALU.mult), v)
                        P.seq(v.tensor_copy(out=hilo[:, 0:1024], in_=outasm[:, 0:1024]), v)
                        return v.tensor_tensor(out=hilo[:, 1024:2048], in0=outasm[:, 0:1024],
                                               in1=hilo[:, 0:1024], op=ALU.subtract)
                    P.op("vector", f_fin, reads=[pkey, "s2t0", "s_down"],
                         writes=["outasm", "hilo", "sil0", "sil1", "wdTcol1"], tag=f"DF{th}{i}")
                    P.op("sync", (lambda i=i: lambda e: e.dma_start(
                        out=hiT[:, 0:16384].rearrange("p (tt ii n) -> p tt ii n", tt=8, ii=NCk)[:, :, i, :],
                        in_=hilo[:, 0:1024].rearrange("p (tt n) -> p tt n", tt=8), transpose=True))(),
                         reads=["hilo"], writes=["hiT"], dma=True, tag=f"DH{th}{i}")
                    P.op("sync", (lambda i=i: lambda e: e.dma_start(
                        out=hiT[:, LOT:LOT + 16384].rearrange("p (tt ii n) -> p tt ii n", tt=8, ii=NCk)[:, :, i, :],
                        in_=hilo[:, 1024:2048].rearrange("p (tt n) -> p tt n", tt=8), transpose=True))(),
                         reads=["hilo"], writes=["loT"], dma=True, tag=f"DL{th}{i}")
                for tt in range(8):
                    ti = th * 8 + tt
                    P.op("gpsimd", (lambda ti=ti: lambda g: g.dma_start(
                        out=s2t[1], in_=s2_d[ti * 128:(ti + 1) * 128, :]))(),
                         reads=["s2_d", "s2t1"], writes=["s2t1"], dma=True, tag=f"DS{ti}")
                    def f_asm(v, tt=tt):
                        P.seq(v.tensor_tensor(out=outasm, in0=hiT[:, tt * 2048:(tt + 1) * 2048],
                                              in1=hiT[:, LOT + tt * 2048:LOT + (tt + 1) * 2048], op=ALU.add), v)
                        return v.tensor_tensor(out=outasm, in0=outasm, in1=s2t[1], op=ALU.add)
                    P.op("vector", f_asm, reads=["hiT", "loT", "s2t1"],
                         writes=["outasm", "sil0", "sil1", "wdTcol1"], tag=f"DO{ti}")
                    P.op("gpsimd", (lambda ti=ti: lambda g: g.dma_start(
                        out=out_d[ti * 128:(ti + 1) * 128, :], in_=outasm))(),
                         reads=["outasm"], writes=[f"out{ti}", "outasm"], dma=True, tag=f"DW{ti}")

    if upto >= 8:
        P.op("gpsimd", lambda g: g.engine_nop(),
             reads=[f"out{ti}" for ti in range(NT)], tag="fin")
    else:
        P.op("gpsimd", lambda g: g.dma_start(out=out_d[0:128, :], in_=x_d[0:128, :]),
             writes=["outdbg"], dma=True, tag="dbgout")
        P.op("gpsimd", lambda g: g.engine_nop(), reads=["outdbg"], tag="fin")

    n_ops, n_inc = P.emit()
    return nc, n_ops, n_inc


_NC_CACHE = {}


def kernel(L0_attn_x, L0_attn_Wq, L0_attn_Wk, L0_attn_Wv, L0_attn_Wp,
           L0_ffn_wq_gate, L0_ffn_s_gate, L0_ffn_wq_up, L0_ffn_s_up,
           L0_ffn_wq_down, L0_ffn_s_down):
    x = np.ascontiguousarray(np.asarray(L0_attn_x, dtype=np.float32))
    wq = np.ascontiguousarray(np.asarray(L0_attn_Wq, dtype=np.float32))
    wk = np.ascontiguousarray(np.asarray(L0_attn_Wk, dtype=np.float32))
    wv = np.ascontiguousarray(np.asarray(L0_attn_Wv, dtype=np.float32))
    wp = np.ascontiguousarray(np.asarray(L0_attn_Wp, dtype=np.float32))
    wg = np.ascontiguousarray(np.asarray(L0_ffn_wq_gate, dtype=np.int32))
    wu = np.ascontiguousarray(np.asarray(L0_ffn_wq_up, dtype=np.int32))
    wd = np.ascontiguousarray(np.asarray(L0_ffn_wq_down, dtype=np.int32))
    sg = np.ascontiguousarray(np.asarray(L0_ffn_s_gate, dtype=np.float32).reshape(NH, 128).T)
    su = np.ascontiguousarray(np.asarray(L0_ffn_s_up, dtype=np.float32).reshape(NH, 128).T)
    sd = np.ascontiguousarray(np.asarray(L0_ffn_s_down, dtype=np.float32).reshape(NCk, 128).T)

    if "nc" not in _NC_CACHE:
        _NC_CACHE["nc"] = build()[0]
    nc = _NC_CACHE["nc"]
    in_maps = [{"x": x[b], "wq": wq, "wk": wk, "wv": wv, "wp": wp,
                "wg": wg, "wu": wu, "wd": wd, "sg": sg, "su": su, "sd": sd}
               for b in range(8)]
    res = run_bass_kernel_spmd(nc, in_maps, core_ids=list(range(8)))
    return np.stack([r["out"] for r in res.results], axis=0)


# version 07 (final)
# speedup vs baseline: 1.0366x; 1.0016x over previous
import numpy as np
import concourse.bass as bass
import concourse.mybir as mybir
from concourse.bass_utils import run_bass_kernel_spmd
import ml_dtypes

F32 = mybir.dt.float32
BF16 = mybir.dt.bfloat16
I32 = mybir.dt.int32
AF = mybir.ActivationFunctionType
ALU = mybir.AluOpType
AX = mybir.AxisListType
BF = ml_dtypes.bfloat16

T = 2048
C = 2048
H = 16
D = 128
HID = 5632
NT = T // 128
NCk = C // 128
NH = HID // 128
EPS = 1e-5
LN127 = float(np.log(127.0))
INV127 = 1.0 / 127.0
NELW = float(C * C)
GROUPS = [list(range(0, 15)), list(range(15, 30)), list(range(30, 44))]


class Prog:

    ENGINES = ("sync", "gpsimd", "tensor", "vector", "scalar")

    def __init__(self, nc):
        self.nc = nc
        self.ops = []
        self.last_write = {}
        self.readers = {}

    def op(self, eng, fn, reads=(), writes=(), dma=False, tag=""):
        deps = set()
        for k in reads:
            w = self.last_write.get(k)
            if w is not None:
                deps.add(w)
        for k in writes:
            w = self.last_write.get(k)
            if w is not None:
                deps.add(w)
            for r in self.readers.get(k, ()):
                deps.add(r)
        i = len(self.ops)
        if dma:
            assert writes, f"dma op {tag} needs a write resource for its semaphore"
        self.ops.append(dict(eng=eng, fn=fn, deps=sorted(deps), dma=dma, tag=tag,
                             w0=(writes[0] if writes else None)))
        for k in reads:
            self.readers.setdefault(k, []).append(i)
        for k in writes:
            self.last_write[k] = i
            self.readers[k] = []
        return i

    def transfer(self, old, new):
        if old in self.last_write:
            lw = self.last_write[old]
            if new not in self.last_write or self.last_write[new] < lw:
                self.last_write[new] = lw
        self.readers.setdefault(new, [])
        self.readers[new] += list(self.readers.get(old, ()))

    def seq(self, instr, handle):
        e = self._cur
        instr.then_inc(self._aux[e], 1)
        self._auxcnt[e] += 1
        handle.wait_ge(self._aux[e], self._auxcnt[e])

    def emit(self):
        nc = self.nc
        self._aux = {e: nc.semaphore(f"aux_{e}").__enter__() for e in self.ENGINES}
        self._auxcnt = {e: 0 for e in self.ENGINES}
        self._cur = None
        need_inc = set()
        for i, o in enumerate(self.ops):
            if o["dma"]:
                need_inc.add(i)
            for d in o["deps"]:
                need_inc.add(d)
        semid = {}
        res2sem = {}
        for i, o in enumerate(self.ops):
            if o["dma"]:
                r = o["w0"]
                if r not in res2sem:
                    res2sem[r] = f"dma_{len(res2sem)}_{r}"
                semid[i] = res2sem[r]
            else:
                semid[i] = f"eng_{o['eng']}"
        counts = {}
        running = {}
        for i, o in enumerate(self.ops):
            if i in need_inc:
                s = semid[i]
                running[s] = running.get(s, 0) + (16 if o["dma"] else 1)
                counts[i] = running[s]
        sems = {s: nc.semaphore(s).__enter__() for s in
                set(semid.values())}
        ops = self.ops

        def engine_body(ename, handle):
            waited = {}
            self._cur = ename
            for i, o in enumerate(ops):
                if o["eng"] != ename:
                    continue
                self._cur = ename
                for d in o["deps"]:
                    s = semid[d]
                    if d in counts and counts[d] > waited.get(s, 0):
                        handle.wait_ge(sems[s], counts[d])
                        waited[s] = counts[d]
                instr = o["fn"](handle)
                if i in need_inc:
                    assert instr is not None, f"op {i} {o['tag']} returned None but needs inc"
                    instr.then_inc(sems[semid[i]], 16 if o["dma"] else 1)

        with nc.Block() as block:
            @block.sync
            def _(e):
                engine_body("sync", e)

            @block.gpsimd
            def _(e):
                engine_body("gpsimd", e)

            @block.tensor
            def _(e):
                engine_body("tensor", e)

            @block.vector
            def _(e):
                engine_body("vector", e)

            @block.scalar
            def _(e):
                engine_body("scalar", e)
        return len(self.ops), len(need_inc)


def build(upto=99, bheads=16, bcut=9, bq=9):
    nc = bass.Bass("TRN2", target_bir_lowering=False, debug=False)

    x_d = nc.dram_tensor("x", [T, C], F32, kind="ExternalInput").ap()
    w_attn = {k: nc.dram_tensor(f"w{k}", [C, C], F32, kind="ExternalInput").ap()
              for k in ("q", "k", "v", "p")}
    wg_d = nc.dram_tensor("wg", [HID, C], I32, kind="ExternalInput").ap()
    wu_d = nc.dram_tensor("wu", [HID, C], I32, kind="ExternalInput").ap()
    wd_d = nc.dram_tensor("wd", [C, HID], I32, kind="ExternalInput").ap()
    sg_d = nc.dram_tensor("sg", [128, NH], F32, kind="ExternalInput").ap()
    su_d = nc.dram_tensor("su", [128, NH], F32, kind="ExternalInput").ap()
    sd_d = nc.dram_tensor("sd", [128, NCk], F32, kind="ExternalInput").ap()
    out_d = nc.dram_tensor("out", [T, C], F32, kind="ExternalOutput").ap()

    tern_d = {k: nc.dram_tensor(f"tern{k}", [C, C], BF16, kind="Internal").ap()
              for k in ("q", "k", "v", "p")}
    wgb_d = nc.dram_tensor("wgb", [HID, C], BF16, kind="Internal").ap()
    wub_d = nc.dram_tensor("wub", [HID, C], BF16, kind="Internal").ap()
    wdb_d = nc.dram_tensor("wdb", [C, HID], BF16, kind="Internal").ap()
    wdT_d = nc.dram_tensor("wdT", [HID, C], BF16, kind="Internal").ap()
    s2_d = nc.dram_tensor("s2", [T, C], F32, kind="Internal").ap()
    part_d = nc.dram_tensor("part", [C, T], F32, kind="Internal").ap()

    ar1 = nc.sbuf_tensor("ar1", [128, 16384], F32).__enter__()[:]
    ar2 = nc.sbuf_tensor("ar2", [128, 16384], F32).__enter__()[:]
    ar3 = nc.sbuf_tensor("ar3", [128, 14848], F32).__enter__()[:]
    sm = nc.sbuf_tensor("smalls", [128, 512], F32).__enter__()[:]
    ones_r = nc.sbuf_tensor("ones_r", [1, 128], F32).__enter__()[:]
    ones_c = nc.sbuf_tensor("ones_c", [128, 1], F32).__enter__()[:]
    rowscr = nc.sbuf_tensor("rowscr", [1, 16], F32).__enter__()[:]

    def carve(ar, cur, n):
        return ar[:, cur:cur + n], cur + n

    xq_fm = ar1.bitcast(BF16)
    WpT = ar1.bitcast(BF16)
    gu_ar = ar1.bitcast(BF16)

    cur = 0
    xs0, cur = carve(ar2, cur, 2048)
    xs1, cur = carve(ar2, cur, 2048)
    sqv, cur = carve(ar2, cur, 2048)
    xnv, cur = carve(ar2, cur, 2048)
    rxq = ar2[:, 8192:9216].bitcast(BF16)
    yqT = ar2.bitcast(BF16)
    xn2_fm = ar2.bitcast(BF16)
    hiT = ar2.bitcast(BF16)
    LOT = 16384

    cur = 0
    a_ternT, cur = carve(ar3, cur, 3072)
    a_qkvraw, cur = carve(ar3, cur, 1536)
    a_i32, cur = carve(ar3, cur, 1024)
    a_p127a, cur = carve(ar3, cur, 1024)
    a_p127b, cur = carve(ar3, cur, 1024)
    a_qk_tm, cur = carve(ar3, cur, 1024)
    a_v_tm, cur = carve(ar3, cur, 1024)
    a_q_fm, cur = carve(ar3, cur, 1024)
    a_k_fm, cur = carve(ar3, cur, 1024)
    a_r, cur = carve(ar3, cur, 1024)
    a_rT0, cur = carve(ar3, cur, 1024)
    a_rT1, cur = carve(ar3, cur, 1024)
    assert cur <= 14848, cur

    off = {}
    o = 0
    for nm, n in (("ternT", 3072), ("qkvraw", 1536), ("i32", 1024), ("p127a", 1024),
                  ("p127b", 1024), ("qk_tm", 1024), ("v_tm", 1024), ("q_fm", 1024),
                  ("k_fm", 1024), ("r", 1024), ("rT0", 1024), ("rT1", 1024)):
        off[nm] = o
        o += n
    ternT = a_ternT.bitcast(BF16)
    qkvraw = a_qkvraw.bitcast(BF16)
    y_q = a_qkvraw.bitcast(BF16)
    i32b = a_i32.bitcast(I32)
    p127 = [a_p127a, a_p127b]
    q_tm = a_qk_tm.bitcast(BF16)
    k_tm = a_qk_tm.bitcast(BF16)
    y_raw = a_qk_tm.bitcast(BF16)
    v_tm = a_v_tm.bitcast(BF16)
    q_fm = a_q_fm.bitcast(BF16)
    k_fm = a_k_fm.bitcast(BF16)
    r_bf = a_r.bitcast(BF16)
    rT = [a_rT0.bitcast(BF16), a_rT1.bitcast(BF16)]
    wrow = [a_p127a, a_p127b, a_qk_tm, a_v_tm]
    w2i = [a_i32.bitcast(I32)[:, 0:1024], a_v_tm.bitcast(I32)[:, 0:1024]]
    w2o = [a_q_fm.bitcast(BF16)[:, 0:2048], a_k_fm.bitcast(BF16)[:, 0:2048]]
    xsC = [a_ternT[:, 0:2048], ar3[:, off["qk_tm"]:off["qk_tm"] + 2048]]
    wgT = [a_ternT.bitcast(BF16)[:, 0:2048], a_ternT.bitcast(BF16)[:, 2048:4096]]
    wuT = [a_qkvraw.bitcast(BF16)[:, 0:2048],
           a_ternT.bitcast(BF16)[:, 4096:6144]]
    wdT_col2 = [ar3[:, off["p127a"]:off["p127a"] + 2048].bitcast(BF16),
                ar3[:, off["p127a"]:off["p127a"] + 2048].bitcast(BF16)]
    wdT_col = wdT_col2[0]
    sqscr = ar3[:, off["p127a"]:off["p127a"] + 2048]
    sil = [ar3[:, off["qk_tm"]:off["qk_tm"] + 1024],
           ar3[:, off["v_tm"]:off["v_tm"] + 1024]]
    s2t = [ar3[:, off["q_fm"]:off["q_fm"] + 2048],
           ar3[:, off["r"]:off["r"] + 2048]]
    outasm = ar3[:, off["qk_tm"]:off["qk_tm"] + 2048]
    hilo = a_i32.bitcast(BF16)

    scur = 0

    def scarve(n):
        nonlocal scur
        ap = sm[:, scur:scur + n]
        scur += n
        return ap

    consts = scarve(5)
    c_eps, c_ln127, c_one, c_tiny, c_half = (consts[:, i:i + 1] for i in range(5))
    Ceff = scarve(1)
    colM = scarve(3 * NT)
    colU = scarve(3 * NT)
    colV = scarve(3 * NT)
    colR = scarve(NT)
    colG = scarve(NT)
    colB = scarve(NT)
    colS = scarve(NT)
    colSh = scarve(NT)
    colSi = scarve(NT)
    colMy = scarve(NT)
    colUy = scarve(NT)
    colVy = scarve(NT)
    invq_xq = scarve(NT)
    aW = scarve(4)
    sWi = scarve(4)
    asum = scarve(NCk)
    stat = scarve(8)
    tmp = scarve(16)
    s_gate = scarve(NH)
    s_up = scarve(NH)
    s_down = scarve(NCk)
    pM = scarve(1)
    pU = scarve(1)
    pV = scarve(1)
    assert scur <= 512, scur

    psA = nc.psum_tensor("psA", [128, 2048], F32).__enter__()[:]
    psB = nc.psum_tensor("psB", [128, 2048], F32).__enter__()[:]

    P = Prog(nc)

    def f_init(v):
        P.seq(v.memset(c_eps, EPS), v)
        P.seq(v.memset(c_ln127, LN127), v)
        P.seq(v.memset(c_one, 1.0), v)
        P.seq(v.memset(c_tiny, float(2.0 ** -30)), v)
        P.seq(v.memset(c_half, 0.75), v)
        P.seq(v.memset(ones_c, 1.0), v)
        P.seq(v.memset(ones_r, 1.0), v)
        P.seq(v.tensor_copy(out=i32b[:, 0:1], in_=c_half), v)
        return v.tensor_scalar(out=Ceff, in0=i32b[:, 0:1], scalar1=-0.5, scalar2=128.5,
                               op0=ALU.mult, op1=ALU.add)
    P.op("vector", f_init, writes=["consts", "Ceff", "i32b", "ones"], tag="init")
    P.op("gpsimd", lambda g: g.dma_start(out=s_gate, in_=sg_d), writes=["s_gate"], dma=True)
    P.op("gpsimd", lambda g: g.dma_start(out=s_up, in_=su_d), writes=["s_up"], dma=True)
    P.op("gpsimd", lambda g: g.dma_start(out=s_down, in_=sd_d), writes=["s_down"], dma=True)

    for wi, wkey in (enumerate(("q", "k", "v", "p")) if upto >= 1 else []):
        wd = w_attn[wkey]
        for ch in range(NCk):
            for hf in range(2):
                bi = (ch * 2 + hf) % 4
                buf, bkey = wrow[bi], f"wrow{bi}"
                src = wd[ch * 128:(ch + 1) * 128, hf * 1024:(hf + 1) * 1024]
                P.op("sync", (lambda b, s: lambda e: e.dma_start(out=b, in_=s))(buf, src),
                     writes=[bkey], dma=True, tag=f"w0l{wkey}{ch}{hf}")
                dst = asum[:, ch:ch + 1] if hf == 0 else tmp[:, 4:5]
                dkey = "asum" if hf == 0 else "tmp45"
                P.op("scalar", (lambda b, d: lambda e: e.activation(
                    out=sqv[:, 0:1024], in_=b, func=AF.Abs, accum_out=d))(buf, dst),
                     reads=[bkey], writes=["sqv", dkey], tag=f"w0a{wkey}{ch}{hf}")
                if hf == 1:
                    P.op("vector", (lambda ci: lambda v: v.tensor_tensor(
                        out=asum[:, ci:ci + 1], in0=asum[:, ci:ci + 1], in1=tmp[:, 4:5], op=ALU.add))(ch),
                         reads=["tmp45"], writes=["asum"], tag=f"w0+{wkey}{ch}")
        P.op("vector", lambda v: v.tensor_reduce(out=tmp[:, 5:6], in_=asum[:, 0:NCk],
                                                 axis=AX.X, op=ALU.add),
             reads=["asum"], writes=["tmp5"], tag=f"w0r{wkey}")
        P.op("tensor", lambda t: t.matmul(psA[0:1, 0:1], ones_c, tmp[:, 5:6], start=True, stop=True),
             reads=["tmp5", "ones"], writes=["psA"], tag=f"w0m{wkey}")
        def f_sml(v):
            P.seq(v.tensor_scalar(out=rowscr[0:1, 0:1], in0=psA[0:1, 0:1], scalar1=1.0 / NELW,
                                  scalar2=EPS, op0=ALU.mult, op1=ALU.max), v)
            return v.reciprocal(out=rowscr[0:1, 1:2], in_=rowscr[0:1, 0:1])
        P.op("vector", f_sml, reads=["psA"], writes=["rowscr"], tag=f"w0s{wkey}")
        P.op("tensor", lambda t: t.matmul(psB[:, 0:2], ones_r, rowscr[0:1, 0:2], start=True, stop=True),
             reads=["rowscr", "ones"], writes=["psB"], tag=f"w0b{wkey}")
        def f_cp(v, wi=wi):
            v.tensor_copy(out=aW[:, wi:wi + 1], in_=psB[:, 0:1])
            return v.tensor_copy(out=sWi[:, wi:wi + 1], in_=psB[:, 1:2])
        P.op("vector", f_cp, reads=["psB"], writes=["aW", "sWi"], tag=f"w0c{wkey}")

    ternbuf = rxq
    for wi, wkey in (enumerate(("q", "k", "v", "p")) if upto >= 2 else []):
        wd = w_attn[wkey]
        for ch in range(NCk):
            for hf in range(2):
                bi = (ch * 2 + hf) % 4
                buf, bkey = wrow[bi], f"wrow{bi}"
                src = wd[ch * 128:(ch + 1) * 128, hf * 1024:(hf + 1) * 1024]
                P.op("sync", (lambda b, s: lambda e: e.dma_start(out=b, in_=s))(buf, src),
                     writes=[bkey], dma=True, tag=f"w1l{wkey}{ch}{hf}")
                def f_tern(v, b=buf, wi=wi, hf=hf):
                    P.seq(v.tensor_scalar(out=i32b[:, 0:1024], in0=b, scalar1=sWi[:, wi:wi + 1],
                                          scalar2=Ceff, op0=ALU.mult, op1=ALU.add), v)
                    P.seq(v.tensor_scalar(out=i32b[:, 0:1024], in0=i32b[:, 0:1024], scalar1=127.0,
                                          scalar2=129.0, op0=ALU.max, op1=ALU.min), v)
                    return v.tensor_scalar(out=ternbuf[:, hf * 1024:(hf + 1) * 1024],
                                           in0=i32b[:, 0:1024], scalar1=-128.0, scalar2=None, op0=ALU.add)
                P.op("vector", f_tern, reads=[bkey, "sWi", "Ceff"], writes=["i32b", "ternbuf"],
                     tag=f"w1t{wkey}{ch}{hf}")
            P.op("gpsimd", (lambda wkey, ch: lambda g: g.dma_start(
                out=tern_d[wkey][ch * 128:(ch + 1) * 128, :], in_=ternbuf))(wkey, ch),
                 reads=["ternbuf"], writes=[f"tern_d{wkey}"], dma=True, tag=f"w1s{wkey}{ch}")

    P.transfer("i32b", "w2i0")
    P.transfer("wrow3", "w2i1")
    for nm, src_d, dst_d in ((("g", wg_d, wgb_d), ("u", wu_d, wub_d)) if upto >= 3 else []):
        for ch in range(NH):
            bi = ch % 2
            ibuf, ikey = w2i[bi], ("w2i0" if bi == 0 else "w2i1")
            obuf, okey = w2o[bi], ("w2o0" if bi == 0 else "w2o1")
            for hf in range(2):
                P.op("sync", (lambda ib, s=src_d, ch=ch, hf=hf: lambda e: e.dma_start(
                    out=ib, in_=s[ch * 128:(ch + 1) * 128, hf * 1024:(hf + 1) * 1024]))(ibuf),
                     writes=[ikey], dma=True, tag=f"w2l{nm}{ch}{hf}")
                P.op("vector", (lambda ib, ob, hf=hf: lambda v: v.tensor_scalar(
                    out=ob[:, hf * 1024:(hf + 1) * 1024], in0=ib, scalar1=-128.0,
                    scalar2=None, op0=ALU.add))(ibuf, obuf),
                     reads=[ikey], writes=[okey], tag=f"w2c{nm}{ch}{hf}")
            P.op("gpsimd", (lambda ob, d=dst_d, ch=ch: lambda g: g.dma_start(
                out=d[ch * 128:(ch + 1) * 128, :], in_=ob))(obuf),
                 reads=[okey], writes=[f"wb{nm}"], dma=True, tag=f"w2s{nm}{ch}")
    P.transfer("w2i0", "i32b")
    dn_o = xq_fm[:, 0:HID]
    dn_t = xq_fm[:, 8192:8192 + NH * 128]
    for ch in (range(NCk) if upto >= 3 else []):
        for sub in range(6):
            w = 1024 if sub < 5 else 512
            P.op("sync", (lambda s=wd_d, ch=ch, sub=sub, w=w: lambda e: e.dma_start(
                out=i32b[:, 0:w], in_=s[ch * 128:(ch + 1) * 128, sub * 1024:sub * 1024 + w]))(),
                 writes=["i32b"], dma=True, tag=f"w2ld{ch}{sub}")
            P.op("vector", (lambda sub=sub, w=w: lambda v: v.tensor_scalar(
                out=dn_o[:, sub * 1024:sub * 1024 + w], in0=i32b[:, 0:w],
                scalar1=-128.0, scalar2=None, op0=ALU.add))(),
                 reads=["i32b"], writes=["dn_o"], tag=f"w2cd{ch}{sub}")
        P.op("gpsimd", (lambda ch=ch: lambda g: g.dma_start(
            out=wdb_d[ch * 128:(ch + 1) * 128, :], in_=dn_o))(),
             reads=["dn_o"], writes=["wdb"], dma=True, tag=f"w2sd{ch}")
        P.op("sync", (lambda ch=ch: lambda e: e.dma_start(
            out=dn_t.rearrange("p (m n) -> p m n", m=NH),
            in_=dn_o.rearrange("p (m n) -> p m n", m=NH), transpose=True))(),
             reads=["dn_o"], writes=["dn_t"], dma=True, tag=f"w2tp{ch}")
        P.op("gpsimd", (lambda ch=ch: lambda g: g.dma_start(
            out=wdT_d[:, ch * 128:(ch + 1) * 128].rearrange("(m p) n -> p m n", p=128),
            in_=dn_t.rearrange("p (m n) -> p m n", m=NH)))(),
             reads=["dn_t"], writes=["wdT"], dma=True, tag=f"w2st{ch}")

    P.transfer("dn_o", "xq_fm")
    P.transfer("dn_t", "xq_fm")
    P.transfer("ternbuf", "rxq")
    for ti in (range(NT) if upto >= 4 else []):
        xb, xkey = (xs0, "xs0") if ti % 2 == 0 else (xs1, "xs1")
        P.op("gpsimd", (lambda xb=xb, ti=ti: lambda g: g.dma_start(
            out=xb, in_=x_d[ti * 128:(ti + 1) * 128, :]))(),
             writes=[xkey], dma=True, tag=f"Ax{ti}")
        P.op("vector", (lambda xb=xb: lambda v: v.tensor_reduce(
            out=stat[:, 0:1], in_=xb, axis=AX.X, op=ALU.add))(),
             reads=[xkey], writes=["stat0"], tag=f"As{ti}")
        P.op("scalar", (lambda xb=xb: lambda s: s.activation(
            out=sqv, in_=xb, func=AF.Square, accum_out=stat[:, 1:2]))(),
             reads=[xkey], writes=["sqv", "stat1"], tag=f"Aq{ti}")
        def f_stats(v):
            P.seq(v.tensor_scalar(out=stat[:, 2:3], in0=stat[:, 0:1], scalar1=1.0 / C, scalar2=None, op0=ALU.mult), v)
            P.seq(v.tensor_tensor(out=stat[:, 3:4], in0=stat[:, 2:3], in1=stat[:, 2:3], op=ALU.mult), v)
            P.seq(v.tensor_scalar(out=stat[:, 4:5], in0=stat[:, 1:2], scalar1=1.0 / C, scalar2=None, op0=ALU.mult), v)
            return v.tensor_tensor(out=stat[:, 4:5], in0=stat[:, 4:5], in1=stat[:, 3:4], op=ALU.subtract)
        P.op("vector", f_stats, reads=["stat0", "stat1"], writes=["statm"], tag=f"Av{ti}")
        def f_rstd(s):
            P.seq(s.activation(out=stat[:, 5:6], in_=stat[:, 4:5], func=AF.Ln, bias=c_eps), s)
            return s.activation(out=stat[:, 6:7], in_=stat[:, 5:6], func=AF.Exp, scale=-0.5)
        P.op("scalar", f_rstd, reads=["statm", "consts"], writes=["rstd"], tag=f"Ar{ti}")
        def f_nmr(v):
            P.seq(v.tensor_scalar(out=stat[:, 7:8], in0=stat[:, 2:3], scalar1=-1.0, scalar2=None, op0=ALU.mult), v)
            return v.tensor_tensor(out=stat[:, 7:8], in0=stat[:, 7:8], in1=stat[:, 6:7], op=ALU.mult)
        P.op("vector", f_nmr, reads=["statm", "rstd"], writes=["nmr"], tag=f"An{ti}")
        P.op("scalar", (lambda xb=xb: lambda s: s.activation(
            out=xnv, in_=xb, func=AF.Identity, scale=stat[:, 6:7], bias=stat[:, 7:8]))(),
             reads=[xkey, "rstd", "nmr"], writes=["xnv"], tag=f"Aa{ti}")
        P.op("vector", lambda v: v.tensor_reduce(out=tmp[:, 6:7], in_=xnv, axis=AX.X,
                                                 op=ALU.max, apply_absolute_value=True),
             reads=["xnv"], writes=["qm"], tag=f"Am{ti}")
        def f_qs(v, ti=ti):
            P.seq(v.tensor_scalar(out=tmp[:, 7:8], in0=tmp[:, 6:7], scalar1=EPS, scalar2=None, op0=ALU.max), v)
            P.seq(v.tensor_scalar(out=invq_xq[:, ti:ti + 1], in0=tmp[:, 7:8], scalar1=INV127, scalar2=None, op0=ALU.mult), v)
            P.seq(v.reciprocal(out=tmp[:, 8:9], in_=tmp[:, 7:8]), v)
            return v.tensor_scalar(out=tmp[:, 9:10], in0=tmp[:, 8:9], scalar1=127.0, scalar2=None, op0=ALU.mult)
        P.op("vector", f_qs, reads=["qm"], writes=["invq_xq", "qs"], tag=f"Aw{ti}")
        for hf in range(2):
            P.op("vector", (lambda hf=hf: lambda v: v.tensor_scalar(
                out=i32b[:, 0:1024], in0=xnv[:, hf * 1024:(hf + 1) * 1024],
                scalar1=tmp[:, 9:10], scalar2=Ceff, op0=ALU.mult, op1=ALU.add))(),
                 reads=["xnv", "qs", "Ceff"], writes=["i32b"], tag=f"A1{ti}{hf}")
            P.op("vector", (lambda hf=hf: lambda v: v.tensor_scalar(
                out=rxq[:, hf * 1024:(hf + 1) * 1024], in0=i32b[:, 0:1024],
                scalar1=-128.0, scalar2=None, op0=ALU.add))(),
                 reads=["i32b"], writes=["rxq"], tag=f"A2{ti}{hf}")
        P.op("sync", (lambda ti=ti: lambda e: e.dma_start(
            out=xq_fm.rearrange("p (j t) -> p j t", j=NCk)[:, :, ti * 128:(ti + 1) * 128],
            in_=rxq.rearrange("p (j n) -> p j n", j=NCk), transpose=True))(),
             reads=["rxq"], writes=["xq_fm"], dma=True, tag=f"At{ti}")

    for nm in ("sqv", "xnv", "xs0", "xs1", "rxq"):
        P.transfer(nm, "yqT")
    P.transfer("wrow0", "p1270")
    P.transfer("wrow1", "p1271")
    P.transfer("wrow2", "qk_tm")
    P.transfer("wrow3", "v_tm")
    P.transfer("w2i1", "v_tm")
    P.transfer("w2o0", "q_fm")
    P.transfer("w2o1", "k_fm")

    for h in (range(min(H, bheads)) if upto >= 5 else []):
        P.transfer("y_q", "qkvraw")
        P.transfer("y_raw", "qk_tm")
        P.transfer("qk_tm", "y_raw")
        for wi, wkey in enumerate(("q", "k", "v")):
            P.op("sync", (lambda wkey=wkey, wi=wi, h=h: lambda e: e.dma_start(
                out=ternT.rearrange("p (j w) -> p j w", j=NCk)[:, :, wi * 128:(wi + 1) * 128],
                in_=tern_d[wkey][h * 128:(h + 1) * 128, :].rearrange("p (j n) -> p j n", j=NCk),
                transpose=True))(),
                 reads=[f"tern_d{wkey}"], writes=["ternT"], dma=True, tag=f"B{h}T{wkey}")
        colM3 = colM.rearrange("p (w t) -> p w t", w=3)
        for half in range(2):
            for tq in (range(8) if bq >= 1 else []):
                ti = half * 8 + tq
                pst, pkey = (psA, "psA") if ti % 2 == 0 else (psB, "psB")
                def f_mm(t, pst=pst, ti=ti):
                    for j in range(NCk):
                        ins = t.matmul(pst[:, 0:384],
                                       xq_fm[:, j * T + ti * 128:j * T + (ti + 1) * 128],
                                       ternT[:, j * 384:(j + 1) * 384],
                                       start=(j == 0), stop=(j == NCk - 1))
                    return ins
                P.op("tensor", f_mm, reads=["xq_fm", "ternT"], writes=[pkey + "a", pkey + "b"], tag=f"B{h}q{ti}")
                if bq < 2:
                    continue
                P.op("scalar", (lambda pst=pst, tq=tq: lambda s: s.copy(
                    out=qkvraw[:, tq * 384:(tq + 1) * 384], in_=pst[:, 0:384]))(),
                     reads=[pkey + "a"], writes=["qkvraw"], tag=f"B{h}e{ti}")
                if bq < 3:
                    continue
                def f_red3(v, tq=tq, ti=ti):
                    for wi in range(3):
                        ins = v.tensor_reduce(
                            out=colM[:, wi * NT + ti:wi * NT + ti + 1],
                            in_=qkvraw[:, tq * 384 + wi * 128:tq * 384 + (wi + 1) * 128],
                            axis=AX.X, op=ALU.max, apply_absolute_value=True)
                        if wi < 2:
                            P.seq(ins, v)
                    return ins
                P.op("vector", f_red3, reads=["qkvraw"], writes=["colM"], tag=f"B{h}m{ti}")
            if bcut < 2:
                continue
            def f_usml(v, half=half):
                iq = invq_xq[:, half * 8:half * 8 + 8]
                for wi in range(3):
                    sl = slice(wi * NT + half * 8, wi * NT + half * 8 + 8)
                    P.seq(v.tensor_tensor(out=colU[:, sl], in0=colM[:, sl], in1=iq, op=ALU.mult), v)
                    P.seq(v.tensor_scalar(out=colU[:, sl], in0=colU[:, sl], scalar1=aW[:, wi:wi + 1],
                                          scalar2=EPS, op0=ALU.mult, op1=ALU.max), v)
                    P.seq(v.tensor_scalar(out=colV[:, sl], in0=colU[:, sl], scalar1=INV127, scalar2=None, op0=ALU.mult), v)
                    P.seq(v.reciprocal(out=colU[:, sl], in_=colU[:, sl]), v)
                    P.seq(v.tensor_tensor(out=colU[:, sl], in0=colU[:, sl], in1=iq, op=ALU.mult), v)
                    P.seq(v.tensor_scalar(out=colU[:, sl], in0=colU[:, sl], scalar1=aW[:, wi:wi + 1],
                                          scalar2=None, op0=ALU.mult), v)
                    P.seq(v.tensor_scalar(out=colU[:, sl], in0=colU[:, sl], scalar1=127.0,
                                          scalar2=None, op0=ALU.mult), v)
                return v.tensor_scalar(out=colG[:, half * 8:half * 8 + 8],
                                       in0=colV[:, half * 8:half * 8 + 8],
                                       scalar1=float(1.0 / np.sqrt(128.0)), scalar2=None, op0=ALU.mult)
            P.op("vector", f_usml, reads=["colM", "invq_xq", "aW"],
                 writes=["colU", "colV", "colG"], tag=f"B{h}u{half}")
            for wi, (dst, dkey) in enumerate(((q_tm, "qk_tm"), (k_tm, "qk_tm"), (v_tm, "v_tm"))):
                for tq in range(8):
                    ti = half * 8 + tq
                    P.op("vector", (lambda wi=wi, ti=ti, tq=tq: lambda v: v.tensor_scalar(
                        out=i32b[:, 0:128], in0=qkvraw[:, tq * 384 + wi * 128:tq * 384 + (wi + 1) * 128],
                        scalar1=colU[:, wi * NT + ti:wi * NT + ti + 1], scalar2=Ceff,
                        op0=ALU.mult, op1=ALU.add))(),
                         reads=["qkvraw", "colU", "Ceff"], writes=["i32b"], tag=f"B{h}c{wi}{ti}")
                    if wi == 0:
                        fn = (lambda dst=dst, ti=ti: lambda v: v.tensor_scalar(
                            out=dst[:, ti * 128:(ti + 1) * 128], in0=i32b[:, 0:128],
                            scalar1=-128.0, scalar2=None, op0=ALU.add))()
                    else:
                        fn = (lambda dst=dst, ti=ti, wi=wi: lambda v: v.tensor_scalar(
                            out=dst[:, ti * 128:(ti + 1) * 128], in0=i32b[:, 0:128],
                            scalar1=-128.0, scalar2=colV[:, wi * NT + ti:wi * NT + ti + 1],
                            op0=ALU.add, op1=ALU.mult))()
                    P.op("vector", fn, reads=["i32b", "colV"], writes=[dkey], tag=f"B{h}d{wi}{ti}")
                if wi == 0:
                    P.op("sync", (lambda half=half: lambda e: e.dma_start(
                        out=q_fm.rearrange("p (j n) -> p j n", j=NT)[:, half * 8:(half + 1) * 8, :],
                        in_=q_tm[:, half * 1024:(half + 1) * 1024].rearrange("p (j n) -> p j n", j=8),
                        transpose=True))(),
                         reads=["qk_tm"], writes=["q_fm"], dma=True, tag=f"B{h}tq{half}")
                elif wi == 1:
                    P.op("sync", (lambda half=half: lambda e: e.dma_start(
                        out=k_fm.rearrange("p (j n) -> p j n", j=NT)[:, half * 8:(half + 1) * 8, :],
                        in_=k_tm[:, half * 1024:(half + 1) * 1024].rearrange("p (j n) -> p j n", j=8),
                        transpose=True))(),
                         reads=["qk_tm"], writes=["k_fm"], dma=True, tag=f"B{h}tk{half}")
        def emit_av(ti):
            pst, pkey = (psA, "psA") if ti % 2 == 0 else (psB, "psB")
            rb, rkey = rT[ti % 2], f"rT{ti % 2}"
            def f_av(t, pst=pst, rb=rb):
                for j in range(NT):
                    ins = t.matmul(pst[:, 0:128], rb[:, j * 128:(j + 1) * 128],
                                   v_tm[:, j * 128:(j + 1) * 128],
                                   start=(j == 0), stop=(j == NT - 1))
                return ins
            P.op("tensor", f_av, reads=[rkey, "v_tm"], writes=[pkey + "a", pkey + "b"], tag=f"B{h}A{ti}")
            P.op("scalar", (lambda pst=pst, ti=ti: lambda s: s.activation(
                out=y_raw[:, ti * 128:(ti + 1) * 128], in_=pst[:, 0:128],
                func=AF.Identity, scale=colSi[:, ti:ti + 1]))(),
                 reads=[pkey + "a", "colSi"], writes=["y_raw"], tag=f"B{h}Y{ti}")
            P.op("vector", (lambda ti=ti: lambda v: v.tensor_reduce(
                out=colMy[:, ti:ti + 1], in_=y_raw[:, ti * 128:(ti + 1) * 128],
                axis=AX.X, op=ALU.max, apply_absolute_value=True))(),
                 reads=["y_raw"], writes=["colMy"], tag=f"B{h}M{ti}")

        for ti in (range(NT) if bcut >= 3 else []):
            pst, pkey = (psA, "psA") if ti % 2 == 0 else (psB, "psB")
            def f_smmA(t, pst=pst, ti=ti):
                for half in range(2):
                    ins = t.matmul(pst[:, half * 512:(half + 1) * 512],
                                   q_fm[:, ti * 128:(ti + 1) * 128],
                                   k_fm[:, half * 512:(half + 1) * 512],
                                   start=True, stop=True)
                return ins
            P.op("tensor", f_smmA, reads=["q_fm", "k_fm"], writes=[pkey + "a"], tag=f"B{h}sa{ti}")
            def f_smmB(t, pst=pst, ti=ti):
                for half in range(2, 4):
                    ins = t.matmul(pst[:, half * 512:(half + 1) * 512],
                                   q_fm[:, ti * 128:(ti + 1) * 128],
                                   k_fm[:, half * 512:(half + 1) * 512],
                                   start=True, stop=True)
                return ins
            P.op("tensor", f_smmB, reads=["q_fm", "k_fm"], writes=[pkey + "b"], tag=f"B{h}sb{ti}")
            P.op("vector", (lambda pst=pst: lambda v: v.tensor_reduce(
                out=tmp[:, 10:11], in_=pst[:, 0:1024], axis=AX.X, op=ALU.max))(),
                 reads=[pkey + "a"], writes=["rmA"], tag=f"B{h}xa{ti}")
            P.op("vector", (lambda pst=pst: lambda v: v.tensor_reduce(
                out=tmp[:, 11:12], in_=pst[:, 1024:2048], axis=AX.X, op=ALU.max))(),
                 reads=[pkey + "b"], writes=["rmB"], tag=f"B{h}xb{ti}")
            def f_bias(v, ti=ti):
                P.seq(v.tensor_tensor(out=colR[:, ti:ti + 1], in0=tmp[:, 10:11],
                                      in1=tmp[:, 11:12], op=ALU.max), v)
                P.seq(v.tensor_tensor(out=colB[:, ti:ti + 1], in0=colR[:, ti:ti + 1],
                                      in1=colG[:, ti:ti + 1], op=ALU.mult), v)
                return v.tensor_scalar(out=colB[:, ti:ti + 1], in0=colB[:, ti:ti + 1],
                                       scalar1=-1.0, scalar2=LN127, op0=ALU.mult, op1=ALU.add)
            P.op("vector", f_bias, reads=["rmA", "rmB", "colG"], writes=["colR", "colB"], tag=f"B{h}b{ti}")
            for hf in range(2):
                P.op("scalar", (lambda pst=pst, ti=ti, hf=hf: lambda s: s.activation(
                    out=p127[hf], in_=pst[:, hf * 1024:(hf + 1) * 1024], func=AF.Exp,
                    scale=colG[:, ti:ti + 1], bias=colB[:, ti:ti + 1],
                    accum_out=(colSh if hf == 0 else colS)[:, ti:ti + 1]))(),
                     reads=[pkey + ("a" if hf == 0 else "b"), "colG", "colB"],
                     writes=[f"p127{hf}", "colSh" if hf == 0 else "colSx"], tag=f"B{h}E{ti}{hf}")
                P.op("vector", (lambda hf=hf: lambda v: v.tensor_scalar(
                    out=i32b[:, 0:1024], in0=p127[hf], scalar1=Ceff, scalar2=None, op0=ALU.add))(),
                     reads=[f"p127{hf}", "Ceff"], writes=["i32b"], tag=f"B{h}1{ti}{hf}")
                P.op("vector", (lambda hf=hf: lambda v: v.tensor_scalar(
                    out=r_bf[:, hf * 1024:(hf + 1) * 1024], in0=i32b[:, 0:1024],
                    scalar1=-128.0, scalar2=None, op0=ALU.add))(),
                     reads=["i32b"], writes=["r_bf"], tag=f"B{h}2{ti}{hf}")
            def f_sums(v, ti=ti):
                P.seq(v.tensor_tensor(out=colS[:, ti:ti + 1], in0=colS[:, ti:ti + 1],
                                      in1=colSh[:, ti:ti + 1], op=ALU.add), v)
                return v.reciprocal(out=colSi[:, ti:ti + 1], in_=colS[:, ti:ti + 1])
            P.op("vector", f_sums, reads=["colSx", "colSh"], writes=["colS", "colSi"], tag=f"B{h}S{ti}")
            rb, rkey = rT[ti % 2], f"rT{ti % 2}"
            P.op("sync", (lambda rb=rb: lambda e: e.dma_start(
                out=rb.rearrange("p (j n) -> p j n", j=NT),
                in_=r_bf.rearrange("p (j n) -> p j n", j=NT), transpose=True))(),
                 reads=["r_bf"], writes=[rkey], dma=True, tag=f"B{h}R{ti}")
            if ti > 0 and bcut >= 3:
                emit_av(ti - 1)
        if bcut >= 3:
            emit_av(NT - 1)
        if bcut < 4:
            continue
        def f_ysml(v):
            P.seq(v.tensor_scalar(out=colMy, in0=colMy, scalar1=EPS, scalar2=None, op0=ALU.max), v)
            P.seq(v.tensor_scalar(out=colVy, in0=colMy, scalar1=INV127, scalar2=None, op0=ALU.mult), v)
            P.seq(v.reciprocal(out=colUy, in_=colMy), v)
            return v.tensor_scalar(out=colUy, in0=colUy, scalar1=127.0, scalar2=None, op0=ALU.mult)
        P.op("vector", f_ysml, reads=["colMy"], writes=["colUy", "colVy"], tag=f"B{h}Z")
        for ti in range(NT):
            P.op("vector", (lambda ti=ti: lambda v: v.tensor_scalar(
                out=i32b[:, 0:128], in0=y_raw[:, ti * 128:(ti + 1) * 128],
                scalar1=colUy[:, ti:ti + 1], scalar2=Ceff, op0=ALU.mult, op1=ALU.add))(),
                 reads=["y_raw", "colUy", "Ceff"], writes=["i32b"], tag=f"B{h}g{ti}")
            P.op("vector", (lambda ti=ti: lambda v: v.tensor_scalar(
                out=y_q[:, ti * 128:(ti + 1) * 128], in0=i32b[:, 0:128],
                scalar1=-128.0, scalar2=colVy[:, ti:ti + 1], op0=ALU.add, op1=ALU.mult))(),
                 reads=["i32b", "colVy"], writes=["y_q"], tag=f"B{h}h{ti}")
        P.op("sync", (lambda h=h: lambda e: e.dma_start(
            out=yqT.rearrange("p (a n) -> p a n", a=H * NT)[:, h * NT:(h + 1) * NT, :],
            in_=y_q[:, 0:T].rearrange("p (j n) -> p j n", j=NT), transpose=True))(),
             reads=["y_q"], writes=["yqT"], dma=True, tag=f"B{h}yT")

    for pk in ("psA", "psB"):
        P.transfer(pk + "a", pk)
        P.transfer(pk + "b", pk)
    P.transfer("xq_fm", "WpT")
    P.transfer("ternT", "xsC0")
    P.transfer("y_raw", "xsC1")
    P.transfer("qk_tm", "xsC1")
    for j in (range(NCk) if upto >= 6 else []):
        P.op("sync", (lambda j=j: lambda e: e.dma_start(
            out=WpT[:, j * C:(j + 1) * C],
            in_=tern_d["p"][:, j * 128:(j + 1) * 128], transpose=True))(),
             reads=["tern_dp"], writes=["WpT"], dma=True, tag=f"Cw{j}")
    for ti in (range(NT) if upto >= 6 else []):
        pst, pkey = (psA, "psA") if ti % 2 == 0 else (psB, "psB")
        xb, xkey = (xsC[0], "xsC0") if ti % 2 == 0 else (xsC[1], "xsC1")
        P.op("gpsimd", (lambda xb=xb, ti=ti: lambda g: g.dma_start(
            out=xb, in_=x_d[ti * 128:(ti + 1) * 128, :]))(),
             writes=[xkey], dma=True, tag=f"Cx{ti}")
        def f_pmm(t, pst=pst, ti=ti):
            for j in range(NCk):
                for half in range(4):
                    ins = t.matmul(pst[:, half * 512:(half + 1) * 512],
                                   yqT[:, (j * NT + ti) * 128:(j * NT + ti + 1) * 128],
                                   WpT[:, j * C + half * 512:j * C + (half + 1) * 512],
                                   start=(j == 0), stop=(j == NCk - 1))
            return ins
        P.op("tensor", f_pmm, reads=["yqT", "WpT"], writes=[pkey], tag=f"Cp{ti}")
        P.op("vector", (lambda pst=pst: lambda v: v.tensor_reduce(
            out=pM, in_=pst, axis=AX.X, op=ALU.max, apply_absolute_value=True))(),
             reads=[pkey], writes=["pM"], tag=f"Cm{ti}")
        def f_psml(v):
            P.seq(v.tensor_scalar(out=pM, in0=pM, scalar1=aW[:, 3:4], scalar2=EPS, op0=ALU.mult, op1=ALU.max), v)
            P.seq(v.tensor_scalar(out=pV, in0=pM, scalar1=INV127, scalar2=None, op0=ALU.mult), v)
            P.seq(v.reciprocal(out=pU, in_=pM), v)
            P.seq(v.tensor_scalar(out=pU, in0=pU, scalar1=aW[:, 3:4], scalar2=None, op0=ALU.mult), v)
            return v.tensor_scalar(out=pU, in0=pU, scalar1=127.0, scalar2=None, op0=ALU.mult)
        P.op("vector", f_psml, reads=["pM", "aW"], writes=["pU", "pV"], tag=f"Cs{ti}")
        for hf in range(2):
            P.op("vector", (lambda pst=pst, hf=hf: lambda v: v.tensor_scalar(
                out=i32b[:, 0:1024], in0=pst[:, hf * 1024:(hf + 1) * 1024],
                scalar1=pU, scalar2=Ceff, op0=ALU.mult, op1=ALU.add))(),
                 reads=[pkey, "pU", "Ceff"], writes=["i32b"], tag=f"C1{ti}{hf}")
            P.op("vector", (lambda hf=hf: lambda v: v.tensor_scalar(
                out=p127[hf], in0=i32b[:, 0:1024], scalar1=-128.0, scalar2=pV,
                op0=ALU.add, op1=ALU.mult))(),
                 reads=["i32b", "pV"], writes=[f"p127{hf}"], tag=f"C2{ti}{hf}")
            P.op("vector", (lambda xb=xb, hf=hf: lambda v: v.tensor_tensor(
                out=xb[:, hf * 1024:(hf + 1) * 1024], in0=xb[:, hf * 1024:(hf + 1) * 1024],
                in1=p127[hf], op=ALU.add))(),
                 reads=[f"p127{hf}", xkey], writes=[xkey], tag=f"C3{ti}{hf}")
        P.op("gpsimd", (lambda xb=xb, ti=ti: lambda g: g.dma_start(
            out=s2_d[ti * 128:(ti + 1) * 128, :], in_=xb))(),
             reads=[xkey], writes=["s2_d"], dma=True, tag=f"Cq{ti}")

    P.transfer("yqT", "xn2_fm")
    P.transfer("WpT", "gu")
    P.transfer("xsC0", "wgT0")
    P.transfer("xsC0", "wgT1")
    P.transfer("xsC0", "wuT1")
    P.transfer("qkvraw", "wuT0")
    P.transfer("y_q", "wuT0")
    P.transfer("q_fm", "s2t0")
    P.transfer("k_fm", "s2t0")
    P.transfer("r_bf", "s2t1")
    P.transfer("rT0", "s2t1")
    P.transfer("p1270", "wdTcol0")
    P.transfer("p1271", "wdTcol0")
    P.transfer("wrow0", "wdTcol0")
    P.transfer("wrow1", "wdTcol0")
    P.transfer("xsC1", "wdTcol1")
    P.transfer("sil0", "wdTcol1")
    P.transfer("sil1", "wdTcol1")
    P.transfer("outasm", "wdTcol1")
    P.transfer("i32b", "hilo")
    P.transfer("xsC1", "sil0")
    P.transfer("v_tm", "sil1")
    P.transfer("xsC1", "outasm")
    P.transfer("v_tm", "outasm")
    xn2b = hilo
    for ti in (range(NT) if upto >= 7 else []):
        sb_, skey = (s2t[0], "s2t0") if ti % 2 == 0 else (s2t[1], "s2t1")
        P.op("gpsimd", (lambda sb_=sb_, ti=ti: lambda g: g.dma_start(
            out=sb_, in_=s2_d[ti * 128:(ti + 1) * 128, :]))(),
             reads=["s2_d"], writes=[skey], dma=True, tag=f"Dl{ti}")
        P.op("vector", (lambda sb_=sb_: lambda v: v.tensor_reduce(
            out=stat[:, 0:1], in_=sb_, axis=AX.X, op=ALU.add))(),
             reads=[skey], writes=["stat0"], tag=f"Ds{ti}")
        P.op("scalar", (lambda sb_=sb_: lambda s: s.activation(
            out=sqscr, in_=sb_, func=AF.Square, accum_out=stat[:, 1:2]))(),
             reads=[skey], writes=["wdTcol0", "stat1"], tag=f"Dq{ti}")
        def f_stats(v):
            P.seq(v.tensor_scalar(out=stat[:, 2:3], in0=stat[:, 0:1], scalar1=1.0 / C, scalar2=None, op0=ALU.mult), v)
            P.seq(v.tensor_tensor(out=stat[:, 3:4], in0=stat[:, 2:3], in1=stat[:, 2:3], op=ALU.mult), v)
            P.seq(v.tensor_scalar(out=stat[:, 4:5], in0=stat[:, 1:2], scalar1=1.0 / C, scalar2=None, op0=ALU.mult), v)
            return v.tensor_tensor(out=stat[:, 4:5], in0=stat[:, 4:5], in1=stat[:, 3:4], op=ALU.subtract)
        P.op("vector", f_stats, reads=["stat0", "stat1"], writes=["statm"], tag=f"Dv{ti}")
        def f_rstd(s):
            P.seq(s.activation(out=stat[:, 5:6], in_=stat[:, 4:5], func=AF.Ln, bias=c_eps), s)
            return s.activation(out=stat[:, 6:7], in_=stat[:, 5:6], func=AF.Exp, scale=-0.5)
        P.op("scalar", f_rstd, reads=["statm", "consts"], writes=["rstd"], tag=f"Dr{ti}")
        def f_nmr(v):
            P.seq(v.tensor_scalar(out=stat[:, 7:8], in0=stat[:, 2:3], scalar1=-1.0, scalar2=None, op0=ALU.mult), v)
            return v.tensor_tensor(out=stat[:, 7:8], in0=stat[:, 7:8], in1=stat[:, 6:7], op=ALU.mult)
        P.op("vector", f_nmr, reads=["statm", "rstd"], writes=["nmr"], tag=f"Dn{ti}")
        P.op("scalar", (lambda sb_=sb_: lambda s: s.activation(
            out=xn2b, in_=sb_, func=AF.Identity, scale=stat[:, 6:7], bias=stat[:, 7:8]))(),
             reads=[skey, "rstd", "nmr"], writes=["hilo"], tag=f"Da{ti}")
        P.op("sync", (lambda ti=ti: lambda e: e.dma_start(
            out=xn2_fm.rearrange("p (j t) -> p j t", j=NCk)[:, :, ti * 128:(ti + 1) * 128],
            in_=xn2b.rearrange("p (j n) -> p j n", j=NCk), transpose=True))(),
             reads=["hilo"], writes=["xn2_fm"], dma=True, tag=f"Dt{ti}")

    for gi, jlist in (enumerate(GROUPS) if upto >= 8 else []):
        for jj, j in enumerate(jlist):
            gslot = jj % 2
            gview, gk = wgT[gslot], f"wgT{gslot}"
            uview, uk = wuT[gslot], f"wuT{gslot}"
            P.op("sync", (lambda j=j, gview=gview: lambda e: e.dma_start(
                out=gview.rearrange("p (a n) -> p a n", a=NCk),
                in_=wgb_d[j * 128:(j + 1) * 128, :].rearrange("p (a n) -> p a n", a=NCk),
                transpose=True))(),
                 reads=["wbg"], writes=[gk], dma=True, tag=f"DG{j}")
            P.op("sync", (lambda j=j, uview=uview: lambda e: e.dma_start(
                out=uview.rearrange("p (a n) -> p a n", a=NCk),
                in_=wub_d[j * 128:(j + 1) * 128, :].rearrange("p (a n) -> p a n", a=NCk),
                transpose=True))(),
                 reads=["wbu"], writes=[uk], dma=True, tag=f"DU{j}")
            def f_gmm(t, gview=gview):
                for a in range(NCk):
                    for qq in range(4):
                        ins = t.matmul(psA[:, qq * 512:(qq + 1) * 512],
                                       gview[:, a * 128:(a + 1) * 128],
                                       xn2_fm[:, a * T + qq * 512:a * T + (qq + 1) * 512],
                                       start=(a == 0), stop=(a == NCk - 1))
                return ins
            P.op("tensor", f_gmm, reads=[gk, "xn2_fm"], writes=["psA"], tag=f"Dg{j}")
            def f_umm(t, uview=uview):
                for a in range(NCk):
                    for qq in range(4):
                        ins = t.matmul(psB[:, qq * 512:(qq + 1) * 512],
                                       uview[:, a * 128:(a + 1) * 128],
                                       xn2_fm[:, a * T + qq * 512:a * T + (qq + 1) * 512],
                                       start=(a == 0), stop=(a == NCk - 1))
                return ins
            P.op("tensor", f_umm, reads=[uk, "xn2_fm"], writes=["psB"], tag=f"Dh{j}")
            guv = gu_ar[:, jj * T:(jj + 1) * T]
            for hf in range(2):
                P.op("scalar", (lambda hf=hf, j=j: lambda s: s.activation(
                    out=sil[hf], in_=psA[:, hf * 1024:(hf + 1) * 1024],
                    func=AF.Silu, scale=s_gate[:, j:j + 1]))(),
                     reads=["psA", "s_gate"], writes=[f"sil{hf}", "outasm", "wdTcol1"], tag=f"DA{j}{hf}")
                P.op("vector", (lambda hf=hf, j=j, guv=guv: lambda v: v.scalar_tensor_tensor(
                    out=guv[:, hf * 1024:(hf + 1) * 1024], in0=psB[:, hf * 1024:(hf + 1) * 1024],
                    scalar=s_up[:, j:j + 1], in1=sil[hf], op0=ALU.mult, op1=ALU.mult))(),
                     reads=["psB", "s_up", f"sil{hf}"], writes=["gu"], tag=f"DC{j}{hf}")
        last = gi == len(GROUPS) - 1
        jbase = jlist[0]
        nj = len(jlist)
        if not last:
            for i in range(NCk):
                wc, wck = wdT_col2[0], "wdTcol0"
                P.op("sync", (lambda i=i, jbase=jbase, nj=nj, wc=wc: lambda e: e.dma_start(
                    out=wc[:, 0:nj * 128].rearrange("p (m n) -> p m n", m=nj),
                    in_=wdT_d[jbase * 128:(jbase + nj) * 128, i * 128:(i + 1) * 128]
                        .rearrange("(m p) n -> p m n", p=128)))(),
                     reads=["wdT"], writes=[wck], dma=True, tag=f"Dc{gi}{i}")
                pst, pkey = (psA, "psA") if i % 2 == 0 else (psB, "psB")
                def f_dmm(t, pst=pst, nj=nj, wc=wc):
                    for jj in range(nj):
                        for qq in range(4):
                            ins = t.matmul(pst[:, qq * 512:(qq + 1) * 512],
                                           wc[:, jj * 128:(jj + 1) * 128],
                                           gu_ar[:, jj * T + qq * 512:jj * T + (qq + 1) * 512],
                                           start=(jj == 0), stop=(jj == nj - 1))
                    return ins
                P.op("tensor", f_dmm, reads=[wck, "gu"], writes=[pkey], tag=f"Dd{gi}{i}")
                if gi == 0:
                    P.op("scalar", (lambda pst=pst: lambda s: s.copy(out=outasm, in_=pst))(),
                         reads=[pkey], writes=["outasm", "sil0", "sil1", "wdTcol1"], tag=f"De{gi}{i}")
                else:
                    P.op("gpsimd", (lambda i=i: lambda g: g.dma_start(
                        out=s2t[0], in_=part_d[i * 128:(i + 1) * 128, :]))(),
                         reads=["part_d", "s2t0"], writes=["s2t0"], dma=True, tag=f"Dpl{gi}{i}")
                    def f_acc(v, pst=pst):
                        return v.tensor_tensor(out=outasm, in0=pst, in1=s2t[0], op=ALU.add)
                    P.op("vector", f_acc, reads=[pkey, "s2t0"], writes=["outasm", "sil0", "sil1", "wdTcol1"], tag=f"De{gi}{i}")
                P.op("gpsimd", (lambda i=i: lambda g: g.dma_start(
                    out=part_d[i * 128:(i + 1) * 128, :], in_=outasm))(),
                     reads=["outasm"], writes=["part_d"], dma=True, tag=f"Dw{gi}{i}")
        else:
            P.transfer("xn2_fm", "hiT")
            P.transfer("xn2_fm", "loT")
            for th in range(2):
                for i in range(NCk):
                    wc, wck = wdT_col2[0], "wdTcol0"
                    P.op("sync", (lambda i=i, jbase=jbase, nj=nj, wc=wc: lambda e: e.dma_start(
                        out=wc[:, 0:nj * 128].rearrange("p (m n) -> p m n", m=nj),
                        in_=wdT_d[jbase * 128:(jbase + nj) * 128, i * 128:(i + 1) * 128]
                            .rearrange("(m p) n -> p m n", p=128)))(),
                         reads=["wdT"], writes=[wck], dma=True, tag=f"Dc2{th}{i}")
                    pst, pkey = (psA, "psA") if i % 2 == 0 else (psB, "psB")
                    def f_dmm(t, pst=pst, nj=nj, th=th, wc=wc):
                        for jj in range(nj):
                            for qq in range(2):
                                ins = t.matmul(pst[:, qq * 512:(qq + 1) * 512],
                                               wc[:, jj * 128:(jj + 1) * 128],
                                               gu_ar[:, jj * T + th * 1024 + qq * 512:
                                                     jj * T + th * 1024 + (qq + 1) * 512],
                                               start=(jj == 0), stop=(jj == nj - 1))
                        return ins
                    P.op("tensor", f_dmm, reads=[wck, "gu"], writes=[pkey], tag=f"DD{th}{i}")
                    P.op("gpsimd", (lambda i=i, th=th: lambda g: g.dma_start(
                        out=s2t[0][:, 0:1024],
                        in_=part_d[i * 128:(i + 1) * 128, th * 1024:(th + 1) * 1024]))(),
                         reads=["part_d", "s2t0"], writes=["s2t0"], dma=True, tag=f"DP{th}{i}")
                    def f_fin(v, pst=pst, i=i):
                        P.seq(v.tensor_tensor(out=outasm[:, 0:1024], in0=pst[:, 0:1024],
                                              in1=s2t[0][:, 0:1024], op=ALU.add), v)
                        P.seq(v.tensor_scalar(out=outasm[:, 0:1024], in0=outasm[:, 0:1024],
                                              scalar1=s_down[:, i:i + 1], scalar2=None, op0=ALU.mult), v)
                        P.seq(v.tensor_copy(out=hilo[:, 0:1024], in_=outasm[:, 0:1024]), v)
                        return v.tensor_tensor(out=hilo[:, 1024:2048], in0=outasm[:, 0:1024],
                                               in1=hilo[:, 0:1024], op=ALU.subtract)
                    P.op("vector", f_fin, reads=[pkey, "s2t0", "s_down"],
                         writes=["outasm", "hilo", "sil0", "sil1", "wdTcol1"], tag=f"DF{th}{i}")
                    P.op("sync", (lambda i=i: lambda e: e.dma_start(
                        out=hiT[:, 0:16384].rearrange("p (tt ii n) -> p tt ii n", tt=8, ii=NCk)[:, :, i, :],
                        in_=hilo[:, 0:1024].rearrange("p (tt n) -> p tt n", tt=8), transpose=True))(),
                         reads=["hilo"], writes=["hiT"], dma=True, tag=f"DH{th}{i}")
                    P.op("sync", (lambda i=i: lambda e: e.dma_start(
                        out=hiT[:, LOT:LOT + 16384].rearrange("p (tt ii n) -> p tt ii n", tt=8, ii=NCk)[:, :, i, :],
                        in_=hilo[:, 1024:2048].rearrange("p (tt n) -> p tt n", tt=8), transpose=True))(),
                         reads=["hilo"], writes=["loT"], dma=True, tag=f"DL{th}{i}")
                for tt in range(8):
                    ti = th * 8 + tt
                    P.op("gpsimd", (lambda ti=ti: lambda g: g.dma_start(
                        out=s2t[1], in_=s2_d[ti * 128:(ti + 1) * 128, :]))(),
                         reads=["s2_d", "s2t1"], writes=["s2t1"], dma=True, tag=f"DS{ti}")
                    def f_asm(v, tt=tt):
                        P.seq(v.tensor_tensor(out=outasm, in0=hiT[:, tt * 2048:(tt + 1) * 2048],
                                              in1=hiT[:, LOT + tt * 2048:LOT + (tt + 1) * 2048], op=ALU.add), v)
                        return v.tensor_tensor(out=outasm, in0=outasm, in1=s2t[1], op=ALU.add)
                    P.op("vector", f_asm, reads=["hiT", "loT", "s2t1"],
                         writes=["outasm", "sil0", "sil1", "wdTcol1"], tag=f"DO{ti}")
                    P.op("gpsimd", (lambda ti=ti: lambda g: g.dma_start(
                        out=out_d[ti * 128:(ti + 1) * 128, :], in_=outasm))(),
                         reads=["outasm"], writes=[f"out{ti}", "outasm"], dma=True, tag=f"DW{ti}")

    if upto >= 8:
        P.op("gpsimd", lambda g: g.engine_nop(),
             reads=[f"out{ti}" for ti in range(NT)], tag="fin")
    else:
        P.op("gpsimd", lambda g: g.dma_start(out=out_d[0:128, :], in_=x_d[0:128, :]),
             writes=["outdbg"], dma=True, tag="dbgout")
        P.op("gpsimd", lambda g: g.engine_nop(), reads=["outdbg"], tag="fin")

    n_ops, n_inc = P.emit()
    return nc, n_ops, n_inc


_NC_CACHE = {}


def kernel(L0_attn_x, L0_attn_Wq, L0_attn_Wk, L0_attn_Wv, L0_attn_Wp,
           L0_ffn_wq_gate, L0_ffn_s_gate, L0_ffn_wq_up, L0_ffn_s_up,
           L0_ffn_wq_down, L0_ffn_s_down):
    x = np.ascontiguousarray(np.asarray(L0_attn_x, dtype=np.float32))
    wq = np.ascontiguousarray(np.asarray(L0_attn_Wq, dtype=np.float32))
    wk = np.ascontiguousarray(np.asarray(L0_attn_Wk, dtype=np.float32))
    wv = np.ascontiguousarray(np.asarray(L0_attn_Wv, dtype=np.float32))
    wp = np.ascontiguousarray(np.asarray(L0_attn_Wp, dtype=np.float32))
    wg = np.ascontiguousarray(np.asarray(L0_ffn_wq_gate, dtype=np.int32))
    wu = np.ascontiguousarray(np.asarray(L0_ffn_wq_up, dtype=np.int32))
    wd = np.ascontiguousarray(np.asarray(L0_ffn_wq_down, dtype=np.int32))
    sg = np.ascontiguousarray(np.asarray(L0_ffn_s_gate, dtype=np.float32).reshape(NH, 128).T)
    su = np.ascontiguousarray(np.asarray(L0_ffn_s_up, dtype=np.float32).reshape(NH, 128).T)
    sd = np.ascontiguousarray(np.asarray(L0_ffn_s_down, dtype=np.float32).reshape(NCk, 128).T)

    if "nc" not in _NC_CACHE:
        _NC_CACHE["nc"] = build()[0]
    nc = _NC_CACHE["nc"]
    in_maps = [{"x": x[b], "wq": wq, "wk": wk, "wv": wv, "wp": wp,
                "wg": wg, "wu": wu, "wd": wd, "sg": sg, "su": su, "sd": sd}
               for b in range(8)]
    res = run_bass_kernel_spmd(nc, in_maps, core_ids=list(range(8)))
    return np.stack([r["out"] for r in res.results], axis=0)
